# version 04
# baseline (speedup 1.0000x reference)
import contextlib
import numpy as np
import concourse.bass as bass
import concourse.mybir as mybir
from concourse.bass_utils import run_bass_kernel_spmd

AF = mybir.ActivationFunctionType
ALU = mybir.AluOpType
AX = mybir.AxisListType
F32 = mybir.dt.float32
F32R = mybir.dt.float32r
BF16 = mybir.dt.bfloat16

N_CORES = 8
D = 1024
NC = 8
NEG_BIG = -1e30


LANES = {
    "mm": ("tensor", 1),
    "act": ("scalar", 1),
    "dve": ("vector", 1),
    "pool": ("gpsimd", 1),
    "ld": ("sync", 16),
    "sd": ("scalar", 16),
    "pd": ("gpsimd", 16),
}


class Prog:

    def __init__(self, nc):
        self.nc = nc
        self.ops = []
        self.count = {}
        self.lane_count = {k: 0 for k in LANES}
        self.regions = {}

    def op(self, lane, fn, reads=(), writes=(), tag="", sem=None, no_waw=False):
        assert lane in LANES
        is_dma = LANES[lane][1] == 16
        if sem is None:
            if is_dma:
                key0 = writes[0] if writes else (reads[0] if reads else lane)
                sem = "dma_" + key0
            else:
                sem = lane
        waits = {}

        def add(s, cnt):
            if waits.get(s, 0) < cnt:
                waits[s] = cnt

        for key in reads:
            reg = self.regions.get(key)
            if reg:
                if reg["w"]:
                    add(*reg["w"])
                for s, c in reg.get("ws", {}).items():
                    add(s, c)
        for key in writes:
            reg = self.regions.get(key)
            if reg:
                if not (no_waw and reg.get("no_waw")):
                    if reg["w"]:
                        add(*reg["w"])
                    for s, c in reg.get("ws", {}).items():
                        add(s, c)
                for r in reg["r"]:
                    add(*r)
        self.count[sem] = self.count.get(sem, 0) + LANES[lane][1]
        me = (sem, self.count[sem])
        for key in reads:
            reg = self.regions.setdefault(key, {"w": None, "r": [], "no_waw": False})
            reg["r"].append(me)
        for key in writes:
            if no_waw:
                reg = self.regions.setdefault(
                    key, {"w": None, "r": [], "no_waw": True, "ws": {}})
                reg["no_waw"] = True
                reg.setdefault("ws", {})[sem] = self.count[sem]
                reg["w"] = None
                reg["r"] = []
            else:
                self.regions[key] = {"w": me, "r": [], "no_waw": False}
        self.lane_count[lane] += 1
        self.ops.append({"lane": lane, "fn": fn, "waits": waits, "sem": sem,
                         "amt": LANES[lane][1], "tag": tag})
        return me

    def emit(self):
        nc = self.nc
        with contextlib.ExitStack() as es:
            sems = {s: es.enter_context(nc.semaphore(f"s_{s}"))
                    for s in self.count}
            block = es.enter_context(nc.Block())
            per_engine = {}
            for o in self.ops:
                per_engine.setdefault(LANES[o["lane"]][0], []).append(o)

            def make_section(ops):
                hwm = {}

                def section(q):
                    for o in ops:
                        for s, cnt in sorted(o["waits"].items()):
                            if hwm.get(s, 0) >= cnt:
                                continue
                            q.wait_ge(sems[s], cnt)
                            hwm[s] = cnt
                        ins = o["fn"](q)
                        assert ins is not None, f"op returned None: {o['tag']}"
                        ins.then_inc(sems[o["sem"]], o["amt"])
                return section

            for eng, ops in per_engine.items():
                getattr(block, eng)(make_section(ops))

            final = dict(self.count)

            def tail(q):
                for s, cnt in sorted(final.items()):
                    q.wait_ge(sems[s], cnt)

            block.sync(tail)


class Blk:
    def __init__(self, nc, cfg):
        self.nc = nc
        self.p = Prog(nc)
        self.cfg = cfg
        self.es = None
        self.uid = 0
        self.pb = None

    def name(self, base):
        self.uid += 1
        return f"{base}_{self.uid}"

    def sbuf(self, base, shape, dtype):
        return self.es.enter_context(
            self.nc.sbuf_tensor(self.name(base), list(shape), dtype))

    def psum(self, base, shape, dtype=F32):
        return self.es.enter_context(
            self.nc.psum_tensor(self.name(base), list(shape), dtype))



def common_prep(b):
    p = b.p
    b.pb = [b.psum("pb", [128, 512], F32) for _ in range(8)]
    b.ident = b.sbuf("ident", [128, 128], F32)
    b.identb = b.sbuf("identb", [128, 128], BF16)
    b.ones_r = b.sbuf("ones_r", [1, 128], F32R)
    b.ones_b = b.sbuf("ones_b", [1, 128], BF16)
    onef = b.onef = b.sbuf("onef", [128, 128], F32)
    oneb = b.oneb = b.sbuf("oneb", [128, 128], BF16)

    p.op("pool", lambda q: q.memset(onef[:], 1.0), writes=("onef",), tag="memset_onef")
    p.op("pool", lambda q: q.memset(oneb[:], 1.0), writes=("oneb",), tag="memset_oneb")
    p.op("pool", lambda q: q.affine_select(
        b.ident[:], onef[:], pattern=[[-1, 128]], compare_op=ALU.is_equal,
        fill=0.0, base=0, channel_multiplier=1),
        reads=("onef",), writes=("ident",), tag="ident")
    p.op("pool", lambda q: q.affine_select(
        b.identb[:], oneb[:], pattern=[[-1, 128]], compare_op=ALU.is_equal,
        fill=0.0, base=0, channel_multiplier=1),
        reads=("oneb",), writes=("identb",), tag="identb")
    p.op("dve", lambda q: q.tensor_copy(b.ones_r[:], onef[0:1, :]),
         reads=("onef",), writes=("ones_r",), tag="ones_r")
    p.op("dve", lambda q: q.tensor_copy(b.ones_b[:], oneb[0:1, :]),
         reads=("oneb",), writes=("ones_b",), tag="ones_b")



def load_cols(b, dst, dst_key, src_vec_ap, n):
    p = b.p
    for c in range(n):
        def ld(q, c=c):
            return q.dma_start(out=dst[:, c], in_=src_vec_ap[c * 128:(c + 1) * 128])
        p.op("ld", ld, writes=(dst_key,), tag=f"ldcol_{dst_key}_{c}")


def row_bcast(b, dst, dst_key, row_ap, row_key, bank_i, width, dtype):
    p = b.p
    bank = b.pb[bank_i]
    bk = f"pb{bank_i}"
    ones = b.ones_r

    def mm(q):
        return q.matmul(bank[:, 0:width], ones[:], row_ap, start=True, stop=True)

    p.op("mm", mm, reads=(row_key, "ones_r"), writes=(bk,), tag=f"bcast_{dst_key}")

    def ev(q):
        return q.tensor_copy(dst[:, 0:width], bank[:, 0:width])

    p.op("dve", ev, reads=(bk,), writes=(dst_key,), tag=f"bcast_ev_{dst_key}")


def tin(b, x_rows_ap, xres, xres_key, dsts, t0, banks=(6, 7)):
    p = b.p

    def ld(q):
        return q.dma_start(out=xres[:], in_=x_rows_ap)

    p.op("pd", ld, writes=(xres_key,), tag=f"tin_ld_{xres_key}_{t0}")
    for half in range(2):
        bank = b.pb[banks[half]]
        bk = f"pb{banks[half]}"

        def tr(q, half=half, bank=bank):
            ins = None
            for j in range(4):
                c = half * 4 + j
                ins = q.transpose(bank[:, j * 128:(j + 1) * 128],
                                  xres[:, c * 128:(c + 1) * 128], b.ident[:])
            return ins

        p.op("mm", tr, reads=(xres_key, "ident"), writes=(bk,),
             tag=f"tin_tr_{t0}_{half}")

        for (tens, key, cstride, colbase) in dsts:
            def ev(q, half=half, bank=bank, tens=tens, cstride=cstride,
                   colbase=colbase):
                ins = None
                for j in range(4):
                    c = half * 4 + j
                    ins = q.tensor_copy(
                        tens.ap()[:, c * cstride + colbase: c * cstride + colbase + 128],
                        bank[:, j * 128:(j + 1) * 128])
                return ins

            p.op("dve", ev, reads=(bk,), writes=(key,),
                 tag=f"tin_ev_{key}_{t0}_{half}")


class LNScratch:
    def __init__(self, b, cen=None, sq=None, cen_key="ln_cen", sq_key="ln_sq"):
        self.cen = cen if cen is not None else b.sbuf("ln_cen", [128, 1024], F32).ap()
        self.sq = sq if sq is not None else b.sbuf("ln_sq", [128, 1024], F32).ap()
        self.st = b.sbuf("ln_st", [128, 8], F32).ap()
        self.cen_key = cen_key
        self.sq_key = sq_key


def ln_tail(b, ln, s_t, s_key, out_ap, tag):
    p = b.p
    cen, sq, st = ln.cen, ln.sq, ln.st
    k_cen, k_sq, k_st = ln.cen_key, ln.sq_key, "ln_st"

    p.op("dve", lambda q: q.reduce_sum(st[:, 0:1], s_t[:], axis=AX.X),
         reads=(s_key,), writes=(k_st,), tag=f"ln_sum_{tag}")
    p.op("dve", lambda q: q.tensor_scalar_mul(st[:, 1:2], st[:, 0:1], 1.0 / 1024),
         reads=(k_st,), writes=(k_st,), tag=f"ln_mean_{tag}")
    p.op("dve", lambda q: q.tensor_scalar(cen, s_t[:], st[:, 1:2], None,
                                          op0=ALU.subtract),
         reads=(s_key, k_st), writes=(k_cen,), tag=f"ln_cen_{tag}")
    p.op("act", lambda q: q.activation(sq, cen, AF.Square, bias=0.0,
                                       scale=1.0, accum_out=st[:, 2:3]),
         reads=(k_cen,), writes=(k_sq, k_st), tag=f"ln_sq_{tag}")
    p.op("dve", lambda q: q.tensor_scalar(st[:, 3:4], st[:, 2:3], 1.0 / 1024, 1e-5,
                                          op0=ALU.mult, op1=ALU.add),
         reads=(k_st,), writes=(k_st,), tag=f"ln_var_{tag}")
    p.op("act", lambda q: q.activation(st[:, 4:5], st[:, 3:4], AF.Sqrt,
                                       bias=0.0, scale=1.0),
         reads=(k_st,), writes=(k_st,), tag=f"ln_sd_{tag}")
    p.op("dve", lambda q: q.reciprocal(st[:, 5:6], st[:, 4:5]),
         reads=(k_st,), writes=(k_st,), tag=f"ln_rstd_{tag}")
    p.op("dve", lambda q: q.tensor_scalar_mul(cen, cen, st[:, 5:6]),
         reads=(k_cen, k_st), writes=(k_cen,), tag=f"ln_y_{tag}")
    p.op("pd", lambda q: q.dma_start(out=out_ap, in_=cen),
         reads=(k_cen,), tag=f"ln_out_{tag}")



def build_l0_ffn(cfg):
    T, B = cfg["T"], cfg["B"]
    HF, E = 4096, 8
    TOK = T * B
    TCW = min(512, TOK)
    NTC = TOK // TCW
    NTT = TCW // 128
    NHG = HF // 512
    NH2 = HF // 128

    nc = bass.Bass("TRN2", target_bir_lowering=False)
    x_d = nc.dram_tensor("x", [TOK, D], F32, kind="ExternalInput").ap()
    w1_d = nc.dram_tensor("w1", [E, D, HF], F32, kind="ExternalInput").ap()
    b1_d = nc.dram_tensor("b1", [E, HF], F32, kind="ExternalInput").ap()
    w2_d = nc.dram_tensor("w2", [E, HF, D], F32, kind="ExternalInput").ap()
    b2_d = nc.dram_tensor("b2", [E, D], F32, kind="ExternalInput").ap()
    wr_d = nc.dram_tensor("wr", [D, E], F32, kind="ExternalInput").ap()
    br_d = nc.dram_tensor("br", [E], F32, kind="ExternalInput").ap()
    y_d = nc.dram_tensor("y", [TOK, D], F32, kind="ExternalOutput").ap()
    w1b_d = nc.dram_tensor("w1b", [E, D, HF], BF16, kind="Internal").ap()
    w2b_d = nc.dram_tensor("w2b", [E, HF, D], BF16, kind="Internal").ap()

    b = Blk(nc, cfg)
    p = b.p
    with contextlib.ExitStack() as es:
        b.es = es
        common_prep(b)

        cast_in = [b.sbuf("cast_in", [128, 4096], F32) for _ in range(2)]
        cast_out = [b.sbuf("cast_out", [128, 4096], BF16) for _ in range(2)]
        n_slab1 = E * D // 128
        n_slab2 = E * HF // 128
        w1_r = w1_d.rearrange("e (s p) h -> (e s) p h", p=128)
        w1b_r = w1b_d.rearrange("e (s p) h -> (e s) p h", p=128)
        w2_r = w2_d.rearrange("e (s p) d -> (e s) p d", p=128)
        w2b_r = w2b_d.rearrange("e (s p) d -> (e s) p d", p=128)
        for i in range(n_slab1):
            par = i % 2

            def ld(q, i=i, par=par):
                return q.dma_start(out=cast_in[par][:], in_=w1_r[i])

            def cv(q, par=par):
                return q.tensor_copy(cast_out[par][:], cast_in[par][:])

            def sto(q, i=i, par=par):
                return q.dma_start(out=w1b_r[i], in_=cast_out[par][:])

            p.op("ld", ld, writes=(f"cin{par}",), tag=f"w1cast_ld{i}")
            p.op("dve", cv, reads=(f"cin{par}",), writes=(f"cout{par}",),
                 tag=f"w1cast_cv{i}")
            p.op("pd", sto, reads=(f"cout{par}",), writes=("w1b_dram",),
                 sem=f"st_w1b{par}", no_waw=True, tag=f"w1cast_st{i}")
        w2_r4 = w2_d.rearrange("e (g s p) d -> (e g) p s d", p=128, s=4)
        w2b_r4 = w2b_d.rearrange("e (g s p) d -> (e g) p s d", p=128, s=4)
        for i in range(n_slab2 // 4):
            par = i % 2

            def ld(q, i=i, par=par):
                return q.dma_start(
                    out=cast_in[par].ap().rearrange("p (s d) -> p s d", s=4),
                    in_=w2_r4[i])

            def cv(q, par=par):
                return q.tensor_copy(cast_out[par][:], cast_in[par][:])

            def sto(q, i=i, par=par):
                return q.dma_start(
                    out=w2b_r4[i],
                    in_=cast_out[par].ap().rearrange("p (s d) -> p s d", s=4))

            p.op("ld", ld, writes=(f"cin{par}",), tag=f"w2cast_ld{i}")
            p.op("dve", cv, reads=(f"cin{par}",), writes=(f"cout{par}",),
                 tag=f"w2cast_cv{i}")
            p.op("pd", sto, reads=(f"cout{par}",), writes=("w2b_dram",),
                 sem=f"st_w2b{par}", no_waw=True, tag=f"w2cast_st{i}")

        wr_sb = b.sbuf("wr_sb", [128, NC * E], F32R)
        br_row = b.sbuf("br_row", [1, E], F32R)
        b1c = b.sbuf("b1c", [128, E * (HF // 128)], F32)
        b2_row = b.sbuf("b2_row", [8, D], BF16)
        b2f = b.sbuf("b2f", [8, D], F32)

        p.op("ld", lambda q: q.dma_start(
            out=wr_sb.ap().rearrange("p (c e) -> p c e", c=NC),
            in_=wr_d.rearrange("(c p) e -> p c e", p=128).bitcast(F32R)),
            writes=("wr_sb",), tag="ld_wr")
        p.op("ld", lambda q: q.dma_start(out=br_row[:], in_=br_d[None, :].bitcast(F32R)),
             writes=("br_row",), tag="ld_br")
        for e in range(E):
            for hc in range(HF // 128):
                def ld(q, e=e, hc=hc):
                    return q.dma_start(out=b1c[:, e * 32 + hc],
                                       in_=b1_d[e, hc * 128:(hc + 1) * 128])
                p.op("ld", ld, writes=("b1c",), tag=f"ld_b1_{e}_{hc}")
        p.op("ld", lambda q: q.dma_start(out=b2f[:], in_=b2_d), writes=("b2f",),
             tag="ld_b2")
        p.op("dve", lambda q: q.tensor_copy(b2_row[:], b2f[:]),
             reads=("b2f",), writes=("b2_row",), tag="cv_b2")
        oneh = b.sbuf("oneh", [8, E * 128], BF16)
        for e in range(E):
            def mkoh(q, e=e):
                return q.affine_select(oneh[:, e * 128:(e + 1) * 128],
                                       b.oneb[0:8, 0:128], pattern=[[0, 128]],
                                       compare_op=ALU.is_equal, fill=0.0,
                                       base=-e, channel_multiplier=1)
            p.op("pool", mkoh, reads=("oneb",), writes=("oneh",), tag=f"oneh{e}")

        xres = [b.sbuf("xres", [128, 1024], F32) for _ in range(NTT)]
        xT = b.sbuf("xT", [128, NC * TCW], BF16)
        xTr = b.sbuf("xTr", [128, NC * TCW], F32R)
        lg = b.sbuf("lg", [128, E], F32)
        r_m1 = b.sbuf("r_m1", [128, 8], F32)
        r_eq = b.sbuf("r_eq", [128, E], F32)
        r_w = b.sbuf("r_w", [128, E], F32)
        wT_r = b.sbuf("wT_r", [8, TCW], F32R)
        wT_b = b.sbuf("wT_b", [8, TCW], BF16)
        wbc = b.sbuf("wbc", [128, TCW], BF16)
        hg_t = b.sbuf("hg_t", [128, 512], BF16)
        hs = b.sbuf("hs", [128, NH2 * TCW], BF16)
        w1s = [b.sbuf("w1s", [128, NC * 512], BF16) for _ in range(2)]
        w2s = [b.sbuf("w2s", [128, 4 * D], BF16) for _ in range(2)]
        acc = b.sbuf("acc", [128, NC * TCW], F32)
        s_t = b.sbuf("s_t", [128, 1024], F32)
        ln = LNScratch(b)

        for tc in range(NTC):
            for j in range(NTT):
                t0g = tc * TCW + j * 128
                tin(b, x_d[t0g:t0g + 128, :], xres[j], f"xres{j}",
                    [(xT, "xT", TCW, j * 128), (xTr, "xTr", TCW, j * 128)],
                    t0g, banks=(4, 5))
            for j in range(NTT):
                bank = b.pb[5]

                def mmr(q, j=j):
                    ins = None
                    for c in range(NC):
                        ins = q.matmul(bank[:, 0:E],
                                       xTr.ap()[:, c * TCW + j * 128: c * TCW + (j) * 128 + 128],
                                       wr_sb.ap()[:, c * E:(c + 1) * E],
                                       start=(c == 0), stop=False)
                    ins = q.matmul(bank[:, 0:E], b.ones_r[:], br_row[:],
                                   start=False, stop=True)
                    return ins

                p.op("mm", mmr, reads=("xTr", "wr_sb", "br_row", "ones_r"),
                     writes=("pb5",), tag=f"route_mm_{tc}_{j}")

                K = ("lg", "r_m1", "r_eq", "r_w")
                p.op("dve", lambda q: q.tensor_copy(lg[:], b.pb[5][:, 0:E]),
                     reads=("pb5",), writes=("lg",), tag=f"rt_cp_{tc}_{j}")
                p.op("dve", lambda q: q.reduce_max(r_m1[:, 0:1], lg[:], axis=AX.X),
                     reads=("lg",), writes=("r_m1",), tag=f"rt_m1_{tc}_{j}")
                p.op("dve", lambda q: q.tensor_scalar(r_eq[:], lg[:], r_m1[:, 0:1],
                                                      None, op0=ALU.is_equal),
                     reads=("lg", "r_m1"), writes=("r_eq",), tag=f"rt_eq_{tc}_{j}")
                p.op("dve", lambda q: q.reduce_sum(r_m1[:, 1:2], r_eq[:], axis=AX.X),
                     reads=("r_eq",), writes=("r_m1",), tag=f"rt_cnt_{tc}_{j}")
                p.op("dve", lambda q: q.scalar_tensor_tensor(
                    r_eq[:], r_eq[:], NEG_BIG, lg[:], op0=ALU.mult, op1=ALU.add),
                    reads=("r_eq", "lg"), writes=("r_eq",), tag=f"rt_l2_{tc}_{j}")
                p.op("dve", lambda q: q.reduce_max(r_m1[:, 2:3], r_eq[:], axis=AX.X),
                     reads=("r_eq",), writes=("r_m1",), tag=f"rt_m2_{tc}_{j}")
                p.op("dve", lambda q: q.tensor_sub(r_m1[:, 3:4], r_m1[:, 0:1],
                                                   r_m1[:, 2:3]),
                     reads=("r_m1",), writes=("r_m1",), tag=f"rt_dif_{tc}_{j}")
                p.op("dve", lambda q: q.tensor_scalar(r_m1[:, 4:5], r_m1[:, 1:2],
                                                      2.0, None, op0=ALU.is_ge),
                     reads=("r_m1",), writes=("r_m1",), tag=f"rt_ge2_{tc}_{j}")
                p.op("dve", lambda q: q.tensor_mul(r_m1[:, 4:5], r_m1[:, 4:5],
                                                   r_m1[:, 3:4]),
                     reads=("r_m1",), writes=("r_m1",), tag=f"rt_g2d_{tc}_{j}")
                p.op("dve", lambda q: q.tensor_add(r_m1[:, 5:6], r_m1[:, 2:3],
                                                   r_m1[:, 4:5]),
                     reads=("r_m1",), writes=("r_m1",), tag=f"rt_thr_{tc}_{j}")
                p.op("dve", lambda q: q.tensor_scalar(r_w[:], lg[:], r_m1[:, 5:6],
                                                      None, op0=ALU.is_ge),
                     reads=("lg", "r_m1"), writes=("r_w",), tag=f"rt_msk_{tc}_{j}")
                p.op("dve", lambda q: q.tensor_scalar(r_w[:], r_w[:], 1.0, None,
                                                      op0=ALU.subtract),
                     reads=("r_w",), writes=("r_w",), tag=f"rt_mm1_{tc}_{j}")
                p.op("dve", lambda q: q.scalar_tensor_tensor(
                    lg[:], r_w[:], -NEG_BIG, lg[:], op0=ALU.mult, op1=ALU.add),
                    reads=("r_w", "lg"), writes=("lg",), tag=f"rt_l3_{tc}_{j}")
                p.op("dve", lambda q: q.tensor_scalar_mul(r_m1[:, 6:7],
                                                          r_m1[:, 0:1], -1.0),
                     reads=("r_m1",), writes=("r_m1",), tag=f"rt_nm1_{tc}_{j}")
                p.op("act", lambda q: q.activation(r_w[:], lg[:], AF.Exp,
                                                   bias=r_m1[:, 6:7], scale=1.0,
                                                   accum_out=r_m1[:, 7:8]),
                     reads=("lg", "r_m1"), writes=("r_w", "r_m1"),
                     tag=f"rt_exp_{tc}_{j}")
                p.op("dve", lambda q: q.reciprocal(r_m1[:, 6:7], r_m1[:, 7:8]),
                     reads=("r_m1",), writes=("r_m1",), tag=f"rt_rz_{tc}_{j}")
                p.op("dve", lambda q: q.tensor_scalar_mul(r_w[:], r_w[:],
                                                          r_m1[:, 6:7]),
                     reads=("r_w", "r_m1"), writes=("r_w",), tag=f"rt_w_{tc}_{j}")

                def trw(q, j=j):
                    return q.transpose(b.pb[6][0:E, j * 128:(j + 1) * 128],
                                       r_w[:, 0:E], b.ident[:])

                p.op("mm", trw, reads=("r_w", "ident"), writes=("pb6",),
                     tag=f"rt_tr_{tc}_{j}")
            p.op("dve", lambda q: q.tensor_copy(wT_r[:], b.pb[6][0:E, 0:TCW]),
                 reads=("pb6",), writes=("wT_r",), tag=f"rt_evr_{tc}")
            p.op("dve", lambda q: q.tensor_copy(wT_b[:], b.pb[6][0:E, 0:TCW]),
                 reads=("pb6",), writes=("wT_b",), tag=f"rt_evb_{tc}")

            for e in range(E):
                def mmb(q, e=e):
                    return q.matmul(b.pb[7][:, 0:TCW],
                                    oneh[:, e * 128:(e + 1) * 128],
                                    wT_b[:], start=True, stop=True)

                p.op("mm", mmb, reads=("wT_b", "oneh"), writes=("pb7",),
                     tag=f"wbc_mm_{tc}_{e}")
                p.op("dve", lambda q: q.tensor_copy(wbc[:], b.pb[7][:, 0:TCW]),
                     reads=("pb7",), writes=("wbc",), tag=f"wbc_ev_{tc}_{e}")

                for hg in range(NHG):
                    par = hg % 2
                    w1bs = w1b_d[e].rearrange("(c p) h -> p c h", p=128)

                    def ldw1(q, e=e, hg=hg, par=par, w1bs=w1bs):
                        return q.dma_start(
                            out=w1s[par].ap().rearrange("p (c h) -> p c h", c=NC),
                            in_=w1bs[:, :, hg * 512:(hg + 1) * 512])

                    p.op("ld", ldw1, reads=("w1b_dram",), writes=(f"w1s{par}",), tag=f"w1ld_{tc}_{e}_{hg}")
                    for hh in range(4):
                        hc = hg * 4 + hh
                        bank = b.pb[2 + (hc % 2)]
                        bkk = f"pb{2 + (hc % 2)}"

                        def mmw1(q, par=par, hh=hh, bank=bank):
                            ins = None
                            for c in range(NC):
                                ins = q.matmul(
                                    bank[:, 0:TCW],
                                    w1s[par].ap()[:, c * 512 + hh * 128: c * 512 + hh * 128 + 128],
                                    xT.ap()[:, c * TCW:(c + 1) * TCW],
                                    start=(c == 0), stop=(c == NC - 1))
                            return ins

                        p.op("mm", mmw1, reads=(f"w1s{par}", "xT"), writes=(bkk,),
                             tag=f"w1mm_{tc}_{e}_{hc}")

                        def evg(q, e=e, hc=hc, bank=bank):
                            return q.activation(hg_t[:, 0:TCW], bank[:, 0:TCW],
                                                AF.Gelu,
                                                bias=b1c[:, e * 32 + hc: e * 32 + hc + 1],
                                                scale=1.0)

                        p.op("act", evg, reads=(bkk, "b1c"), writes=("hg_t",),
                             tag=f"w1gelu_{tc}_{e}_{hc}")

                        def scl(q, hc=hc):
                            return q.tensor_mul(
                                hs.ap()[:, hc * TCW:(hc + 1) * TCW],
                                hg_t[:, 0:TCW], wbc[:, 0:TCW])

                        p.op("dve", scl, reads=("hg_t", "wbc"), writes=("hs",),
                             tag=f"w1scl_{tc}_{e}_{hc}")

                w2bs = w2b_d[e].rearrange("(s p) d -> p s d", p=128)
                for hq in range(NH2 // 4):
                    par = hq % 2

                    def ldw2(q, hq=hq, par=par, w2bs=w2bs):
                        return q.dma_start(
                            out=w2s[par].ap().rearrange("p (s d) -> p s d", s=4),
                            in_=w2bs[:, 4 * hq:4 * hq + 4, :])

                    p.op("ld", ldw2, reads=("w2b_dram",), writes=(f"w2s{par}",), tag=f"w2ld_{tc}_{e}_{hq}")

                    def mmw2(q, e=e, hq=hq, par=par):
                        ins = None
                        for s in range(4):
                            hc = 4 * hq + s
                            first = (hc == 0)
                            for dc in range(NC):
                                if e == 0 and first:
                                    q.matmul(b.pb[dc][:, 0:TCW],
                                             b2_row.ap()[:, dc * 128:(dc + 1) * 128],
                                             wT_b[:], start=True, stop=False)
                                ins = q.matmul(
                                    b.pb[dc][:, 0:TCW],
                                    w2s[par].ap()[:, s * D + dc * 128: s * D + (dc + 1) * 128],
                                    hs.ap()[:, hc * TCW:(hc + 1) * TCW],
                                    start=(first and e != 0), stop=(hc == NH2 - 1))
                        return ins

                    p.op("mm", mmw2,
                         reads=(f"w2s{par}", "hs", "b2_row", "wT_b"),
                         writes=tuple(f"pb{dc}" for dc in range(NC)),
                         tag=f"w2mm_{tc}_{e}_{hq}")

                for dc in range(NC):
                    if e == 0:
                        def ev0(q, dc=dc):
                            return q.tensor_copy(
                                acc.ap()[:, dc * TCW:(dc + 1) * TCW],
                                b.pb[dc][:, 0:TCW])

                        p.op("dve", ev0, reads=(f"pb{dc}",), writes=("acc",),
                             tag=f"w2ev_{tc}_{e}_{dc}")
                    else:
                        def eva(q, dc=dc):
                            return q.scalar_tensor_tensor(
                                acc.ap()[:, dc * TCW:(dc + 1) * TCW],
                                b.pb[dc][:, 0:TCW], 0.0,
                                acc.ap()[:, dc * TCW:(dc + 1) * TCW],
                                op0=ALU.bypass, op1=ALU.add)

                        p.op("dve", eva, reads=(f"pb{dc}", "acc"), writes=("acc",),
                             tag=f"w2ev_{tc}_{e}_{dc}")

            for j in range(NTT):
                for half in range(2):
                    bank = b.pb[4 + half]
                    bkk = f"pb{4 + half}"

                    def trb(q, j=j, half=half, bank=bank):
                        ins = None
                        for cc in range(4):
                            c = half * 4 + cc
                            ins = q.transpose(
                                bank[:, cc * 128:(cc + 1) * 128],
                                acc.ap()[:, c * TCW + j * 128: c * TCW + j * 128 + 128],
                                b.ident[:])
                        return ins

                    p.op("mm", trb, reads=("acc", "ident"), writes=(bkk,),
                         tag=f"tb_tr_{tc}_{j}_{half}")

                    def evs(q, j=j, half=half, bank=bank):
                        return q.scalar_tensor_tensor(
                            s_t[:, half * 512:(half + 1) * 512],
                            bank[:, 0:512], 0.0,
                            xres[j].ap()[:, half * 512:(half + 1) * 512],
                            op0=ALU.bypass, op1=ALU.add)

                    p.op("dve", evs, reads=(bkk, f"xres{j}"), writes=("s_t",),
                         tag=f"tb_res_{tc}_{j}_{half}")
                t0g = tc * TCW + j * 128
                ln_tail(b, ln, s_t, "s_t", y_d[t0g:t0g + 128, :], f"{tc}_{j}")

        p.emit()
    return nc


def prep_wts_dram(b, w_list, wts_d, wtl, wtc):
    p = b.p
    for wi, (nm, wd) in enumerate(w_list):
        for c in range(NC):
            for half in range(2):
                def ldt(q, wd=wd, c=c, half=half):
                    seg = wd[half * 512:(half + 1) * 512, c * 128:(c + 1) * 128]
                    return q.dma_start(
                        out=wtl.ap().rearrange("p (r d) -> p r d", r=4),
                        in_=seg.rearrange("(r p) d -> p r d", p=128))

                p.op("ld", ldt, writes=("wtl",), tag=f"wt_ld_{nm}_{c}_{half}")
                bank = b.pb[6]

                def trt(q, bank=bank):
                    ins = None
                    for j in range(4):
                        ins = q.transpose(bank[:, j * 128:(j + 1) * 128],
                                          wtl[:, j * 128:(j + 1) * 128],
                                          b.ident[:])
                    return ins

                p.op("mm", trt, reads=("wtl", "ident"), writes=("pb6",),
                     tag=f"wt_tr_{nm}_{c}_{half}")

                def evt(q, half=half, bank=bank):
                    return q.tensor_copy(
                        wtc[:, half * 512:(half + 1) * 512], bank[:, 0:512])

                p.op("dve", evt, reads=("pb6",), writes=("wtc",),
                     tag=f"wt_ev_{nm}_{c}_{half}")

            def sw(q, wi=wi, c=c):
                return q.dma_start(out=wts_d[wi, c], in_=wtc[:])

            p.op("pd", sw, reads=("wtc",), writes=("wts_dram",),
                 sem="st_wts", tag=f"wt_st_{nm}_{c}")


def load_wft(b, wft, wts_d, wi, tag):
    p = b.p

    def ldwf(q, wi=wi):
        return q.dma_start(
            out=wft.ap().rearrange("p (c d) -> p c d", c=NC),
            in_=wts_d[wi].rearrange("c p d -> p c d"))

    p.op("ld", ldwf, reads=("wts_dram",), writes=("wft",), tag=tag)



def build_wt(b, w_dram, name, banks=(6, 7)):
    p = b.p
    wt = b.sbuf(name, [128, NC * D], BF16)
    wsl = b.sbuf(name + "_sl", [128, D], F32)
    for rc in range(NC):
        def ldw(q, rc=rc):
            return q.dma_start(out=wsl[:], in_=w_dram[rc * 128:(rc + 1) * 128, :])

        p.op("ld", ldw, writes=(name + "_sl",), tag=f"{name}_ld{rc}")
        for half in range(2):
            bank = b.pb[banks[half]]
            bkk = f"pb{banks[half]}"

            def trs(q, half=half, bank=bank):
                ins = None
                for j in range(4):
                    c = half * 4 + j
                    ins = q.transpose(bank[:, j * 128:(j + 1) * 128],
                                      wsl[:, c * 128:(c + 1) * 128], b.ident[:])
                return ins

            p.op("mm", trs, reads=(name + "_sl", "ident"), writes=(bkk,),
                 tag=f"{name}_tr{rc}_{half}")

            def evw(q, rc=rc, half=half, bank=bank):
                ins = None
                for j in range(4):
                    c = half * 4 + j
                    ins = q.tensor_copy(
                        wt.ap()[:, c * D + rc * 128: c * D + rc * 128 + 128],
                        bank[:, j * 128:(j + 1) * 128])
                return ins

            p.op("dve", evw, reads=(bkk,), writes=(name,),
                 tag=f"{name}_ev{rc}_{half}")
    return wt



def build_l0_attn(cfg):
    T, B = cfg["T"], cfg["B"]
    H, HD = 16, 64
    NT = T // 128
    KCN = T // 512 if T >= 512 else 1
    KCW = min(512, T)
    SW = 2 * T - 128

    nc = bass.Bass("TRN2", target_bir_lowering=False)
    x_d = nc.dram_tensor("x", [B, T, D], F32, kind="ExternalInput").ap()
    wq_d = nc.dram_tensor("wq", [D, D], F32, kind="ExternalInput").ap()
    wk_d = nc.dram_tensor("wk", [D, D], F32, kind="ExternalInput").ap()
    wv_d = nc.dram_tensor("wv", [D, D], F32, kind="ExternalInput").ap()
    wo_d = nc.dram_tensor("wo", [D, D], F32, kind="ExternalInput").ap()
    bq_d = nc.dram_tensor("bq", [D], F32, kind="ExternalInput").ap()
    bk_d = nc.dram_tensor("bk", [D], F32, kind="ExternalInput").ap()
    bv_d = nc.dram_tensor("bv", [D], F32, kind="ExternalInput").ap()
    bo_d = nc.dram_tensor("bo", [D], F32, kind="ExternalInput").ap()
    al_d = nc.dram_tensor("alpha", [1], F32, kind="ExternalInput").ap()
    y_d = nc.dram_tensor("y", [B, T, D], F32, kind="ExternalOutput").ap()
    wts_d = nc.dram_tensor("wts", [4, NC, 128, D], BF16, kind="Internal").ap()

    b = Blk(nc, cfg)
    p = b.p
    with contextlib.ExitStack() as es:
        b.es = es
        common_prep(b)

        wtl = b.sbuf("wtl", [128, 512], F32)
        wtc = b.sbuf("wtc", [128, D], BF16)
        prep_wts_dram(b, [("wq", wq_d), ("wk", wk_d), ("wv", wv_d),
                          ("wo", wo_d)], wts_d, wtl, wtc)
        wft = b.sbuf("wft", [128, NC * D], BF16)

        bqc = b.sbuf("bqc", [128, NC], F32)
        bkc = b.sbuf("bkc", [128, NC], F32)
        load_cols(b, bqc, "bqc", bq_d, NC)
        load_cols(b, bkc, "bkc", bk_d, NC)
        bvf = b.sbuf("bvf", [1, D], F32)
        bof = b.sbuf("bof", [1, D], F32)
        bvr = b.sbuf("bvr", [1, D], BF16)
        bor = b.sbuf("bor", [1, D], BF16)
        p.op("ld", lambda q: q.dma_start(out=bvf[:], in_=bv_d[None, :]),
             writes=("bvf",), tag="ld_bv")
        p.op("ld", lambda q: q.dma_start(out=bof[:], in_=bo_d[None, :]),
             writes=("bof",), tag="ld_bo")
        p.op("dve", lambda q: q.tensor_copy(bvr[:], bvf[:]), reads=("bvf",),
             writes=("bvr",), tag="cv_bv")
        p.op("dve", lambda q: q.tensor_copy(bor[:], bof[:]), reads=("bof",),
             writes=("bor",), tag="cv_bo")

        a1 = b.sbuf("a1", [1, 2], F32R)
        nal = b.sbuf("nal", [128, 1], F32)
        for cc in range(2):
            p.op("ld", lambda q, cc=cc: q.dma_start(
                out=a1[:, cc:cc + 1], in_=al_d[None, :].bitcast(F32R)),
                writes=("a1",), sem="dma_a1", no_waw=True, tag=f"ld_alpha{cc}")
        p.op("mm", lambda q: q.matmul(b.pb[5][:, 0:2], b.ones_r[:], a1[:],
                                      start=True, stop=True),
             reads=("a1", "ones_r"), writes=("pb5",), tag="al_bc")
        p.op("dve", lambda q: q.tensor_scalar_mul(nal[:], b.pb[5][:, 0:1], -1.0),
             reads=("pb5",), writes=("nal",), tag="al_neg")
        strip = b.sbuf("strip", [128, SW], BF16)
        lnc = b.sbuf("lnc", [128, 1], F32)
        p.op("pool", lambda q: q.memset(lnc[:], float(np.log(1.0 / 8.0))),
             writes=("lnc",), tag="lnc")
        p.op("pool", lambda q: q.iota(strip[:], pattern=[[-1, SW]], base=T - 128,
                                      channel_multiplier=1,
                                      allow_small_or_imprecise_dtypes=True),
             writes=("strip",), tag="strip_iota")
        p.op("act", lambda q: q.activation(strip[:], strip[:], AF.Abs, bias=0.0,
                                           scale=1.0),
             reads=("strip",), writes=("strip",), tag="strip_abs")
        p.op("act", lambda q: q.activation(strip[:], strip[:], AF.Exp,
                                           bias=lnc[:, 0:1],
                                           scale=nal[:, 0:1]),
             reads=("strip", "nal", "lnc"), writes=("strip",), tag="strip_exp")

        xT = b.sbuf("xT", [128, NC * T], BF16)
        qT = b.sbuf("qT", [128, NC * T], BF16)
        kT = b.sbuf("kT", [128, NC * T], BF16)
        vtok = b.sbuf("vtok", [128, NT * D], BF16)
        xres = [b.sbuf("xres", [128, D], F32) for _ in range(2)]
        S_one = b.sbuf("S_sb", [128, max(T, 2048)], F32)
        S_sb = [S_one, S_one]
        mx4 = [b.sbuf("mx4", [128, 4], F32) for _ in range(2)]
        ast = [b.sbuf("ast", [128, 4], F32) for _ in range(2)]
        P_sb = [b.sbuf("P_sb", [128, T], BF16) for _ in range(2)]
        PT_sb = [b.sbuf("PT_sb", [128, T], BF16) for _ in range(2)]
        s_t = b.sbuf("s_t", [128, D], F32)
        ln = LNScratch(b, cen=S_one.ap()[:, 0:1024], sq=S_one.ap()[:, 1024:2048],
                       cen_key="S0", sq_key="S0")

        for s in range(B):
            xs = x_d[s]
            ys = y_d[s]
            for tt in range(NT):
                tin(b, xs[tt * 128:(tt + 1) * 128, :], xres[tt % 2],
                    f"xres{tt % 2}", [(xT, "xT", T, tt * 128)], tt, banks=(6, 7))
            for (wi, bc, bck, dst, dk) in ((0, bqc, "bqc", qT, "qT"),
                                           (1, bkc, "bkc", kT, "kT")):
                load_wft(b, wft, wts_d, wi, f"ldw_{dk}_{s}")
                for ic in range(NC):
                    for tq in range(KCN):
                        bank = b.pb[tq % 2]
                        bkk = f"pb{tq % 2}"

                        def mmq(q, ic=ic, tq=tq, bank=bank):
                            ins = None
                            for c in range(NC):
                                ins = q.matmul(
                                    bank[:, 0:KCW],
                                    wft.ap()[:, c * D + ic * 128: c * D + ic * 128 + 128],
                                    xT.ap()[:, c * T + tq * KCW: c * T + tq * KCW + KCW],
                                    start=(c == 0), stop=(c == NC - 1))
                            return ins

                        p.op("mm", mmq, reads=("wft", "xT"), writes=(bkk,),
                             tag=f"pj_{dk}_{s}_{ic}_{tq}")

                        def evq(q, bc=bc, ic=ic, tq=tq, dst=dst, bank=bank):
                            return q.activation(
                                dst.ap()[:, ic * T + tq * KCW: ic * T + tq * KCW + KCW],
                                bank[:, 0:KCW], AF.Identity,
                                bias=bc[:, ic:ic + 1], scale=1.0)

                        p.op("act", evq, reads=(bkk, bck), writes=(dk,),
                             tag=f"pjv_{dk}_{s}_{ic}_{tq}")
            load_wft(b, wft, wts_d, 2, f"ldw_v_{s}")
            for tt in range(NT):
                for og in range(2):
                    bank = b.pb[2 + og]
                    bkk = f"pb{2 + og}"

                    def mmv(q, tt=tt, og=og, bank=bank):
                        q.matmul(bank[:, 0:512], b.ones_b[:],
                                 bvr[:, og * 512:(og + 1) * 512],
                                 start=True, stop=False)
                        ins = None
                        for c in range(NC):
                            ins = q.matmul(
                                bank[:, 0:512],
                                xT.ap()[:, c * T + tt * 128: c * T + tt * 128 + 128],
                                wft.ap()[:, c * D + og * 512: c * D + og * 512 + 512],
                                start=False, stop=(c == NC - 1))
                        return ins

                    p.op("mm", mmv, reads=("xT", "wft", "bvr", "ones_b"),
                         writes=(bkk,), tag=f"pv_{s}_{tt}_{og}")

                    def evv(q, tt=tt, og=og, bank=bank):
                        return q.tensor_copy(
                            vtok.ap()[:, tt * D + og * 512: tt * D + og * 512 + 512],
                            bank[:, 0:512])

                    p.op("dve", evv, reads=(bkk,), writes=("vtok",),
                         tag=f"pvv_{s}_{tt}_{og}")

            for h in range(H):
                po = (h % 2) * 64
                hc = h // 2
                for qt in range(NT):
                    pr = (h * NT + qt) % 2
                    Sk, Pk, Tk = "S0", f"P{pr}", f"PT{pr}"
                    Mk, Ak = f"mx{pr}", f"ast{pr}"
                    Ssb, Psb, Tsb = S_sb[pr], P_sb[pr], PT_sb[pr]
                    Msb, Asb = mx4[pr], ast[pr]
                    base_v = (T - 128) - qt * 128

                    def mms(q, po=po, hc=hc, qt=qt):
                        ins = None
                        for kc in range(KCN):
                            ins = q.matmul(
                                b.pb[kc][:, 0:KCW],
                                qT.ap()[po:po + 64, hc * T + qt * 128: hc * T + qt * 128 + 128],
                                kT.ap()[po:po + 64, hc * T + kc * KCW: hc * T + kc * KCW + KCW],
                                start=True, stop=True)
                        return ins

                    p.op("mm", mms, reads=("qT", "kT"),
                         writes=tuple(f"pb{kc}" for kc in range(KCN)),
                         tag=f"at_s_{s}_{h}_{qt}")
                    for kc in range(KCN):
                        def ttr(q, kc=kc, Ssb=Ssb, base_v=base_v):
                            return q.tensor_tensor(
                                Ssb[:, kc * KCW:(kc + 1) * KCW],
                                b.pb[kc][:, 0:KCW],
                                strip[:, base_v + kc * KCW: base_v + (kc + 1) * KCW],
                                ALU.mult)

                        p.op("dve", ttr, reads=(f"pb{kc}", "strip"),
                             writes=(Sk,), tag=f"at_ttr_{s}_{h}_{qt}_{kc}")

                    def mxr(q, Ssb=Ssb, Asb=Asb):
                        return q.reduce_max(Asb[:, 0:1], Ssb[:, 0:T], axis=AX.X)

                    p.op("dve", mxr, reads=(Sk,), writes=(Ak,),
                         tag=f"at_m_{s}_{h}_{qt}")

                    def ngm(q, Asb=Asb):
                        return q.tensor_scalar_mul(Asb[:, 1:2], Asb[:, 0:1], -1.0)

                    p.op("dve", ngm, reads=(Ak,), writes=(Ak,),
                         tag=f"at_nm_{s}_{h}_{qt}")

                    def exp(q, Ssb=Ssb, Psb=Psb, Asb=Asb):
                        return q.activation(Psb[:], Ssb[:, 0:T], AF.Exp,
                                            bias=Asb[:, 1:2], scale=1.0,
                                            accum_out=Asb[:, 2:3])

                    p.op("act", exp, reads=(Sk, Ak), writes=(Pk, Ak),
                         tag=f"at_e_{s}_{h}_{qt}")

                    def rcz(q, Asb=Asb):
                        return q.reciprocal(Asb[:, 3:4], Asb[:, 2:3])

                    p.op("dve", rcz, reads=(Ak,), writes=(Ak,),
                         tag=f"at_rz_{s}_{h}_{qt}")

                    def pnm(q, Psb=Psb, Asb=Asb):
                        return q.tensor_scalar_mul(Psb[:], Psb[:], Asb[:, 3:4])

                    p.op("dve", pnm, reads=(Pk, Ak), writes=(Pk,),
                         tag=f"at_pn_{s}_{h}_{qt}")
                    nhalf = (NT + 7) // 8
                    for half in range(nhalf):
                        bank = b.pb[4 + half % 2]
                        bkk = f"pb{4 + half % 2}"
                        bview = bank.ap().bitcast(BF16)
                        ntl = min(8, NT - half * 8)

                        def trp(q, half=half, bview=bview, ntl=ntl, Psb=Psb):
                            ins = None
                            for j in range(ntl):
                                kt = half * 8 + j
                                ins = q.transpose(
                                    bview[:, j * 128:(j + 1) * 128],
                                    Psb[:, kt * 128:(kt + 1) * 128], b.identb[:])
                            return ins

                        p.op("mm", trp, reads=(Pk, "identb"), writes=(bkk,),
                             tag=f"at_tp_{s}_{h}_{qt}_{half}")

                        def evp(q, half=half, bview=bview, ntl=ntl, Tsb=Tsb):
                            return q.tensor_copy(
                                Tsb[:, half * 1024: half * 1024 + ntl * 128],
                                bview[:, 0:ntl * 128])

                        p.op("dve", evp, reads=(bkk,), writes=(Tk,),
                             tag=f"at_ep_{s}_{h}_{qt}_{half}")

                    def pv(q, h=h, Tsb=Tsb):
                        ins = None
                        for kt in range(NT):
                            ins = q.matmul(
                                b.pb[0][0:64, 0:128],
                                vtok.ap()[:, kt * D + h * 64: kt * D + h * 64 + 64],
                                Tsb[:, kt * 128:(kt + 1) * 128],
                                start=(kt == 0), stop=(kt == NT - 1))
                        return ins

                    p.op("mm", pv, reads=(Tk, "vtok"), writes=("pb0",),
                         tag=f"at_pv_{s}_{h}_{qt}")

                    def evo(q, po=po, hc=hc, qt=qt):
                        return q.copy(
                            xT.ap()[po:po + 64, hc * T + qt * 128: hc * T + qt * 128 + 128],
                            b.pb[0][0:64, 0:128])

                    p.op("act", evo, reads=("pb0",), writes=("xT",),
                         tag=f"at_o_{s}_{h}_{qt}")

            load_wft(b, wft, wts_d, 3, f"ldw_o_{s}")
            for tt in range(NT):
                def ldr(q, tt=tt, xs=xs):
                    return q.dma_start(out=xres[tt % 2][:],
                                       in_=xs[tt * 128:(tt + 1) * 128, :])

                p.op("pd", ldr, writes=(f"xres{tt % 2}",), tag=f"or_ld_{s}_{tt}")
                for og in range(2):
                    bank = b.pb[2 + og]
                    bkk = f"pb{2 + og}"

                    def mmo(q, tt=tt, og=og, bank=bank):
                        q.matmul(bank[:, 0:512], b.ones_b[:],
                                 bor[:, og * 512:(og + 1) * 512],
                                 start=True, stop=False)
                        ins = None
                        for c in range(NC):
                            ins = q.matmul(
                                bank[:, 0:512],
                                xT.ap()[:, c * T + tt * 128: c * T + tt * 128 + 128],
                                wft.ap()[:, c * D + og * 512: c * D + og * 512 + 512],
                                start=False, stop=(c == NC - 1))
                        return ins

                    p.op("mm", mmo, reads=("xT", "wft", "bor", "ones_b"),
                         writes=(bkk,), tag=f"or_mm_{s}_{tt}_{og}")

                    def evr(q, tt=tt, og=og, bank=bank):
                        return q.scalar_tensor_tensor(
                            s_t[:, og * 512:(og + 1) * 512], bank[:, 0:512],
                            0.0, xres[tt % 2].ap()[:, og * 512:(og + 1) * 512],
                            op0=ALU.bypass, op1=ALU.add)

                    p.op("dve", evr, reads=(bkk, f"xres{tt % 2}"), writes=("s_t",),
                         tag=f"or_rs_{s}_{tt}_{og}")
                ln_tail(b, ln, s_t, "s_t", ys[tt * 128:(tt + 1) * 128, :],
                        f"l0a_{s}_{tt}")

        p.emit()
    return nc



def build_l1_attn(cfg):
    T, B = cfg["T"], cfg["B"]
    H, HD = 16, 64
    NT = T // 128
    NTC = T // 512 if T >= 512 else 1
    TCW = min(512, T)
    NQ = NT // 4

    nc = bass.Bass("TRN2", target_bir_lowering=False)
    x_d = nc.dram_tensor("x", [B, T, D], F32, kind="ExternalInput").ap()
    dn = {}
    for nm in ("tmx", "tmw", "tmk", "tmv", "tmr", "tmg", "tdec", "lnw", "lnb"):
        dn[nm] = nc.dram_tensor(nm, [D], F32, kind="ExternalInput").ap()
    tw1_d = nc.dram_tensor("tw1", [D, 160], F32, kind="ExternalInput").ap()
    tw2_d = nc.dram_tensor("tw2", [5, 32, D], F32, kind="ExternalInput").ap()
    dw1_d = nc.dram_tensor("dw1", [D, 64], F32, kind="ExternalInput").ap()
    dw2_d = nc.dram_tensor("dw2", [64, D], F32, kind="ExternalInput").ap()
    wX_d = {nm: nc.dram_tensor(nm, [D, D], F32, kind="ExternalInput").ap()
            for nm in ("wr", "wk", "wv", "wg", "wo")}
    y_d = nc.dram_tensor("y", [B, T, D], F32, kind="ExternalOutput").ap()
    wts_d = nc.dram_tensor("wts", [5, NC, 128, D], BF16, kind="Internal").ap()
    ksp_d = nc.dram_tensor("ksp", [128, NC * T], BF16, kind="Internal").ap()
    rsp_d = nc.dram_tensor("rsp", [128, NC * T], BF16, kind="Internal").ap()
    gsp_d = nc.dram_tensor("gsp", [T, D], BF16, kind="Internal").ap()

    WI = {"wr": 0, "wk": 1, "wv": 2, "wg": 3, "wo": 4}
    b = Blk(nc, cfg)
    p = b.p
    with contextlib.ExitStack() as es:
        b.es = es
        common_prep(b)

        wtl = b.sbuf("wtl", [128, 512], F32)
        wtc = b.sbuf("wtc", [128, D], BF16)
        prep_wts_dram(b, [(nm, wX_d[nm]) for nm in
                          ("wr", "wk", "wv", "wg", "wo")], wts_d, wtl, wtc)

        mcols = {}
        for nm in ("tmx", "tmw", "tmk", "tmv", "tmr", "tmg", "tdec"):
            t = b.sbuf(nm + "_c", [128, NC], F32)
            load_cols(b, t, nm + "_c", dn[nm], NC)
            mcols[nm] = t
        fcol = [mcols[n] for n in ("tmw", "tmk", "tmv", "tmr", "tmg")]
        fkey = [n + "_c" for n in ("tmw", "tmk", "tmv", "tmr", "tmg")]

        tw1_sb = b.sbuf("tw1_sb", [128, NC * 160], BF16)
        p.op("pd", lambda q: q.dma_start(
            out=tw1_sb.ap().rearrange("p (c m) -> p c m", c=NC),
            in_=tw1_d.rearrange("(c p) m -> p c m", p=128)),
            writes=("tw1_sb",), tag="ld_tw1")
        tw2_sb = b.sbuf("tw2_sb", [64, D], BF16)
        dw1_sb = b.sbuf("dw1_sb", [128, NC * 64], BF16)
        p.op("pd", lambda q: q.dma_start(
            out=dw1_sb.ap().rearrange("p (c m) -> p c m", c=NC),
            in_=dw1_d.rearrange("(c p) m -> p c m", p=128)),
            writes=("dw1_sb",), tag="ld_dw1")
        dw2_sb = b.sbuf("dw2_sb", [64, D], BF16)
        p.op("pd", lambda q: q.dma_start(out=dw2_sb[:], in_=dw2_d),
             writes=("dw2_sb",), tag="ld_dw2")

        lnw_r = b.sbuf("lnw_r", [1, D], F32R)
        lnb_r = b.sbuf("lnb_r", [1, D], F32R)
        p.op("ld", lambda q: q.dma_start(out=lnw_r[:],
                                         in_=dn["lnw"][None, :].bitcast(F32R)),
             writes=("lnw_r",), tag="ld_lnw")
        p.op("ld", lambda q: q.dma_start(out=lnb_r[:],
                                         in_=dn["lnb"][None, :].bitcast(F32R)),
             writes=("lnb_r",), tag="ld_lnb")
        lnw_bc = b.sbuf("lnw_bc", [128, D], F32)
        lnb_bc = b.sbuf("lnb_bc", [128, D], F32)
        for (row, rk, dst, dk) in ((lnw_r, "lnw_r", lnw_bc, "lnw_bc"),
                                   (lnb_r, "lnb_r", lnb_bc, "lnb_bc")):
            for half in range(2):
                def mmbc(q, row=row, half=half):
                    return q.matmul(b.pb[5][:, 0:512], b.ones_r[:],
                                    row[:, half * 512:(half + 1) * 512],
                                    start=True, stop=True)

                p.op("mm", mmbc, reads=(rk, "ones_r"), writes=("pb5",),
                     tag=f"bc_{dk}_{half}")

                def evbc(q, dst=dst, half=half):
                    return q.tensor_copy(dst[:, half * 512:(half + 1) * 512],
                                         b.pb[5][:, 0:512])

                p.op("dve", evbc, reads=("pb5",), writes=(dk,),
                     tag=f"bcv_{dk}_{half}")

        xT = b.sbuf("xT", [128, NC * T], BF16)
        ar_a = b.sbuf("ar_a", [128, 2048], F32)
        ar_b = b.sbuf("ar_b", [128, 4096], F32)
        ar_c = b.sbuf("ar_c", [128, 4096], BF16)
        wft = b.sbuf("wft", [128, NC * D], BF16)
        ktile = b.sbuf("ktile", [128, NC * TCW], BF16)
        vtok = b.sbuf("vtok", [128, NT * D], BF16)
        ytok = b.sbuf("ytok", [128, NT * D], BF16)
        tw_sb = b.sbuf("tw_sb", [64, T], BF16)
        xx01 = b.sbuf("xx01", [64, T], BF16)
        xx23 = b.sbuf("xx23", [64, T], BF16)
        xx4 = b.sbuf("xx4", [32, T], BF16)
        xres_1 = b.sbuf("xres", [128, D], F32)
        xres = [xres_1, xres_1]
        gq = b.sbuf("gq", [128, D], BF16)
        gns = b.sbuf("gns", [128, 80], F32)
        s_t = ar_a.ap()[:, 0:1024]
        ln = LNScratch(b, cen=ar_b.ap()[:, 2048:3072], sq=ar_b.ap()[:, 3072:4096],
                       cen_key="ar_b", sq_key="ar_b")

        xmxb = ar_a.ap().bitcast(BF16)
        xbtb = ar_b.ap().bitcast(BF16)
        xTa = xT.ap()
        v_ef = xTa[:, 0 * T:1 * T]
        v_emf = xTa[:, 1 * T:2 * T]
        v_eb = xTa[:, 2 * T:3 * T]
        v_emb = xTa[:, 3 * T:4 * T]
        v_kp = xTa[:, 4 * T:5 * T]
        v_rp = xTa[:, 5 * T:6 * T]
        v_rxf = xTa[:, 6 * T:7 * T]
        v_kxf = xTa[:, 7 * T:8 * T]
        ktb = ktile.ap()
        v_rxb = ktb[:, 0:T]
        v_kxb = ktb[:, (NC * TCW - T):NC * TCW]
        S1 = ar_a.ap()[:, 0:T]
        S2 = ar_b.ap()[:, 0:T]
        S3 = ar_b.ap()[:, 2048:2048 + T]

        for s in range(B):
            xs = x_d[s]
            ys = y_d[s]
            for tt in range(NT):
                tin(b, xs[tt * 128:(tt + 1) * 128, :], xres[tt % 2],
                    "xres0", [(xT, "xT", T, tt * 128)], tt, banks=(6, 7))

            for tc in range(NTC):
                t0 = tc * TCW
                xTv = xT.ap().rearrange("p (c w) -> p c w", c=NC)
                dxv = ar_c.ap().rearrange("p (c w) -> p c w", c=NC)
                xmv = xmxb[:, 0:NC * TCW].rearrange("p (c w) -> p c w", c=NC)
                xbv = xbtb[:, 0:NC * TCW].rearrange("p (c w) -> p c w", c=NC)

                lo = 1 if tc == 0 else 0
                hi = TCW - 1 if tc == NTC - 1 else TCW

                def dxu(q, t0=t0, lo=lo, hi=hi):
                    return q.tensor_add(dxv[:, :, lo:hi],
                                        xTv[:, :, t0 + lo - 1: t0 + hi - 1],
                                        xTv[:, :, t0 + lo + 1: t0 + hi + 1])

                p.op("dve", dxu, reads=("xT",), writes=("ar_c",),
                     tag=f"dxu_{s}_{tc}")

                def dxm(q, t0=t0, lo=lo, hi=hi):
                    return q.scalar_tensor_tensor(
                        dxv[:, :, lo:hi], dxv[:, :, lo:hi], 0.5,
                        xTv[:, :, t0 + lo: t0 + hi],
                        op0=ALU.mult, op1=ALU.subtract)

                p.op("dve", dxm, reads=("ar_c", "xT"), writes=("ar_c",),
                     tag=f"dxm_{s}_{tc}")
                if tc == 0:
                    p.op("dve", lambda q: q.scalar_tensor_tensor(
                        dxv[:, :, 0:1], xTv[:, :, 1:2], 0.5, xTv[:, :, 0:1],
                        op0=ALU.mult, op1=ALU.subtract),
                        reads=("xT",), writes=("ar_c",), tag=f"dx0_{s}")
                if tc == NTC - 1:
                    p.op("dve", lambda q: q.scalar_tensor_tensor(
                        dxv[:, :, TCW - 1:TCW], xTv[:, :, T - 2:T - 1], 0.5,
                        xTv[:, :, T - 1:T], op0=ALU.mult, op1=ALU.subtract),
                        reads=("xT",), writes=("ar_c",), tag=f"dxl_{s}")

                tmxb = mcols["tmx"].ap().unsqueeze(2).broadcast_to([128, NC, TCW])
                p.op("dve", lambda q: q.tensor_tensor(
                    xmv[:, :, :], dxv[:, :, 0:TCW], tmxb, ALU.mult),
                    reads=("ar_c", "tmx_c"), writes=("ar_a",), tag=f"mx1_{s}_{tc}")
                p.op("dve", lambda q, t0=t0: q.tensor_add(
                    xmv[:, :, :], xmv[:, :, :], xTv[:, :, t0:t0 + TCW]),
                    reads=("ar_a", "xT"), writes=("ar_a",), tag=f"mx2_{s}_{tc}")

                def mmx1(q):
                    ins = None
                    for c in range(NC):
                        ins = q.matmul(b.pb[0][:, 0:TCW],
                                       tw1_sb.ap()[:, c * 160: c * 160 + 128],
                                       xmv[:, c, :],
                                       start=(c == 0), stop=(c == NC - 1))
                    return ins

                p.op("mm", mmx1, reads=("tw1_sb", "ar_a"), writes=("pb0",),
                     tag=f"xx1_{s}_{tc}")

                def mmx2(q):
                    ins = None
                    for c in range(NC):
                        ins = q.matmul(b.pb[1][0:32, 0:TCW],
                                       tw1_sb.ap()[:, c * 160 + 128: c * 160 + 160],
                                       xmv[:, c, :],
                                       start=(c == 0), stop=(c == NC - 1))
                    return ins

                p.op("mm", mmx2, reads=("tw1_sb", "ar_a"), writes=("pb1",),
                     tag=f"xx2_{s}_{tc}")
                p.op("act", lambda q, t0=t0: q.activation(
                    xx01.ap()[:, t0:t0 + TCW], b.pb[0][0:64, 0:TCW], AF.Tanh,
                    bias=0.0, scale=1.0),
                    reads=("pb0",), writes=("xx01",), tag=f"xxe1_{s}_{tc}")
                p.op("act", lambda q, t0=t0: q.activation(
                    xx23.ap()[:, t0:t0 + TCW], b.pb[0][64:128, 0:TCW], AF.Tanh,
                    bias=0.0, scale=1.0),
                    reads=("pb0",), writes=("xx23",), tag=f"xxe2_{s}_{tc}")
                p.op("act", lambda q, t0=t0: q.activation(
                    xx4.ap()[:, t0:t0 + TCW], b.pb[1][0:32, 0:TCW], AF.Tanh,
                    bias=0.0, scale=1.0),
                    reads=("pb1",), writes=("xx4",), tag=f"xxe3_{s}_{tc}")

                for f in range(5):
                    xxt, xxk, xoff = [(xx01, "xx01", 0), (xx01, "xx01", 32),
                                      (xx23, "xx23", 0), (xx23, "xx23", 32),
                                      (xx4, "xx4", 0)][f]
                    for off in range(2):
                        p.op("pd", lambda q, off=off, f=f: q.dma_start(
                            out=tw2_sb.ap()[off * 32:(off + 1) * 32, :],
                            in_=tw2_d[f]),
                            writes=("tw2_sb",), sem="dma_tw2sb", no_waw=True,
                            tag=f"ld_tw2_{s}_{tc}_{f}_{off}")
                    if f != 0:
                        wi = [None, WI["wk"], WI["wv"], WI["wr"], WI["wg"]][f]

                        def ldwf(q, wi=wi):
                            return q.dma_start(
                                out=wft.ap().rearrange("p (c d) -> p c d", c=NC),
                                in_=wts_d[wi].rearrange("c p d -> p c d"))

                        p.op("ld", ldwf, reads=("wts_dram",), writes=("wft",),
                             tag=f"ldwf_{s}_{tc}_{f}")

                    for dc in range(NC):
                        bank = b.pb[2 + dc % 2]
                        bkk = f"pb{2 + dc % 2}"

                        def mmf(q, f=f, dc=dc, xxt=xxt, xoff=xoff, bank=bank,
                                t0=t0):
                            return q.matmul(
                                bank[:, 0:TCW],
                                tw2_sb.ap()[xoff:xoff + 32,
                                            dc * 128: dc * 128 + 128],
                                xxt.ap()[xoff:xoff + 32, t0:t0 + TCW],
                                start=True, stop=True)

                        p.op("mm", mmf, reads=("tw2_sb", xxk), writes=(bkk,),
                             tag=f"mf_{s}_{tc}_{f}_{dc}")

                        mcb = fcol[f]
                        mck = fkey[f]

                        def xb1(q, f=f, dc=dc, bank=bank, mcb=mcb):
                            return q.scalar_tensor_tensor(
                                xbv[:, dc, :], bank[:, 0:TCW],
                                mcb[:, dc:dc + 1], dxv[:, dc, 0:TCW],
                                op0=ALU.add, op1=ALU.mult)

                        p.op("dve", xb1, reads=(bkk, mck, "ar_c"),
                             writes=("ar_b",), tag=f"xb1_{s}_{tc}_{f}_{dc}")

                        def xb2(q, dc=dc, t0=t0):
                            return q.tensor_add(xbv[:, dc, :], xbv[:, dc, :],
                                                xTv[:, dc, t0:t0 + TCW])

                        p.op("dve", xb2, reads=("ar_b", "xT"), writes=("ar_b",),
                             tag=f"xb2_{s}_{tc}_{f}_{dc}")

                    if f == 0:
                        def mmtw(q):
                            ins = None
                            for c in range(NC):
                                ins = q.matmul(
                                    b.pb[4][0:64, 0:TCW],
                                    dw1_sb.ap()[:, c * 64:(c + 1) * 64],
                                    xbv[:, c, :],
                                    start=(c == 0), stop=(c == NC - 1))
                            return ins

                        p.op("mm", mmtw, reads=("dw1_sb", "ar_b"), writes=("pb4",),
                             tag=f"tw_{s}_{tc}")
                        p.op("act", lambda q, t0=t0: q.activation(
                            tw_sb.ap()[:, t0:t0 + TCW], b.pb[4][0:64, 0:TCW],
                            AF.Tanh, bias=0.0, scale=1.0),
                            reads=("pb4",), writes=("tw_sb",), tag=f"twe_{s}_{tc}")
                    elif f in (1, 3):
                        spd = ksp_d if f == 1 else rsp_d
                        for ic in range(NC):
                            bank = b.pb[4 + ic % 2]
                            bkk = f"pb{4 + ic % 2}"

                            def mmk(q, ic=ic, bank=bank):
                                ins = None
                                for c in range(NC):
                                    ins = q.matmul(
                                        bank[:, 0:TCW],
                                        wft.ap()[:, c * D + ic * 128:
                                                 c * D + ic * 128 + 128],
                                        xbv[:, c, :],
                                        start=(c == 0), stop=(c == NC - 1))
                                return ins

                            p.op("mm", mmk, reads=("wft", "ar_b"), writes=(bkk,),
                                 tag=f"kp_{s}_{tc}_{f}_{ic}")

                            def evk(q, ic=ic, bank=bank):
                                return q.tensor_copy(
                                    ktile.ap()[:, ic * TCW:(ic + 1) * TCW],
                                    bank[:, 0:TCW])

                            p.op("dve", evk, reads=(bkk,), writes=("ktile",),
                                 tag=f"kpe_{s}_{tc}_{f}_{ic}")

                        def spk(q, spd=spd, t0=t0):
                            return q.dma_start(
                                out=spd.rearrange("p (c t) -> p c t", c=NC)[:, :, t0:t0 + TCW],
                                in_=ktile.ap().rearrange("p (c w) -> p c w", c=NC))

                        p.op("pd", spk, reads=("ktile",),
                             writes=("ksp_dram" if f == 1 else "rsp_dram",),
                             sem="st_ksp" if f == 1 else "st_rsp",
                             tag=f"spk_{s}_{tc}_{f}")
                    else:
                        for tt in range(TCW // 128):
                            for og in range(2):
                                bank = b.pb[4 + og]
                                bkk = f"pb{4 + og}"

                                def mmv(q, tt=tt, og=og, bank=bank):
                                    ins = None
                                    for c in range(NC):
                                        ins = q.matmul(
                                            bank[:, 0:512],
                                            xbv[:, c, tt * 128:(tt + 1) * 128],
                                            wft.ap()[:, c * D + og * 512:
                                                     c * D + og * 512 + 512],
                                            start=(c == 0), stop=(c == NC - 1))
                                    return ins

                                p.op("mm", mmv, reads=("wft", "ar_b"),
                                     writes=(bkk,), tag=f"vg_{s}_{tc}_{f}_{tt}_{og}")
                                if f == 2:
                                    def evv(q, tt=tt, og=og, bank=bank, t0=t0):
                                        tg = (t0 // 128 + tt)
                                        return q.tensor_copy(
                                            vtok.ap()[:, tg * D + og * 512:
                                                      tg * D + og * 512 + 512],
                                            bank[:, 0:512])

                                    p.op("dve", evv, reads=(bkk,),
                                         writes=("vtok",),
                                         tag=f"vge_{s}_{tc}_{tt}_{og}")
                                else:
                                    def evg(q, og=og, bank=bank):
                                        return q.activation(
                                            gq.ap()[:, og * 512:(og + 1) * 512],
                                            bank[:, 0:512], AF.Silu,
                                            bias=0.0, scale=1.0)

                                    p.op("act", evg, reads=(bkk,), writes=("gq",),
                                         tag=f"gge_{s}_{tc}_{tt}_{og}")
                            if f == 4:
                                def spg(q, tt=tt, t0=t0):
                                    tg = t0 + tt * 128
                                    return q.dma_start(
                                        out=gsp_d[tg:tg + 128, :], in_=gq[:])

                                p.op("pd", spg, reads=("gq",), writes=("gsp_dram",),
                                     sem="st_gsp", tag=f"spg_{s}_{tc}_{tt}")

            for hp in range(8):
                def ldkp(q, hp=hp):
                    return q.dma_start(out=v_kp, in_=ksp_d[:, hp * T:(hp + 1) * T])

                p.op("pd", ldkp, reads=("ksp_dram",), writes=("xT",),
                     sem="ld_kp", tag=f"ldkp_{s}_{hp}")

                def ldrp(q, hp=hp):
                    return q.dma_start(out=v_rp, in_=rsp_d[:, hp * T:(hp + 1) * T])

                p.op("pd", ldrp, reads=("rsp_dram",), writes=("xT",),
                     sem="ld_rp", tag=f"ldrp_{s}_{hp}")

                for tcc in range(NTC):
                    def mmw(q, hp=hp, tcc=tcc):
                        return q.matmul(
                            b.pb[0][:, 0:TCW],
                            dw2_sb.ap()[:, hp * 128:(hp + 1) * 128],
                            tw_sb.ap()[:, tcc * TCW:(tcc + 1) * TCW],
                            start=True, stop=True)

                    p.op("mm", mmw, reads=("dw2_sb", "tw_sb"), writes=("pb0",),
                         tag=f"wd_{s}_{hp}_{tcc}")

                    def evw(q, hp=hp, tcc=tcc):
                        return q.activation(
                            S1[:, tcc * TCW:(tcc + 1) * TCW], b.pb[0][:, 0:TCW],
                            AF.Identity, bias=mcols["tdec"][:, hp:hp + 1],
                            scale=1.0)

                    p.op("act", evw, reads=("pb0", "tdec_c"), writes=("ar_a",),
                         tag=f"wde_{s}_{hp}_{tcc}")
                p.op("act", lambda q: q.activation(S1, S1, AF.Exp, bias=0.0,
                                                   scale=1.0),
                     reads=("ar_a",), writes=("ar_a",), tag=f"wne_{s}_{hp}")
                p.op("dve", lambda q: q.tensor_scalar_mul(S1, S1, -1.0),
                     reads=("ar_a",), writes=("ar_a",), tag=f"wnn_{s}_{hp}")
                p.op("dve", lambda q: q.tensor_tensor_scan(
                    S2, S1, S1, 0.0, op0=ALU.add, op1=ALU.bypass),
                    reads=("ar_a",), writes=("ar_b",), tag=f"cs_{s}_{hp}")
                p.op("dve", lambda q: q.tensor_sub(S3, S2, S1),
                     reads=("ar_a", "ar_b"), writes=("ar_b",), tag=f"csb_{s}_{hp}")
                p.op("dve", lambda q: q.tensor_scalar(
                    S1, S2, S2[:, T // 2:T // 2 + 1], 60.0,
                    op0=ALU.subtract, op1=ALU.min),
                    reads=("ar_b",), writes=("ar_a",), tag=f"cf1_{s}_{hp}")
                p.op("dve", lambda q: q.tensor_scalar(S1, S1, -60.0, None,
                                                      op0=ALU.max),
                     reads=("ar_a",), writes=("ar_a",), tag=f"cf2_{s}_{hp}")
                p.op("act", lambda q: q.activation(v_ef, S1, AF.Exp, bias=0.0,
                                                   scale=1.0),
                     reads=("ar_a",), writes=("xT",), tag=f"ef_{s}_{hp}")
                p.op("act", lambda q: q.activation(v_emf, S1, AF.Exp, bias=0.0,
                                                   scale=-1.0),
                     reads=("ar_a",), writes=("xT",), tag=f"emf_{s}_{hp}")
                p.op("dve", lambda q: q.tensor_scalar(
                    S2, S3, S3[:, T // 2:T // 2 + 1], 60.0,
                    op0=ALU.subtract, op1=ALU.min),
                    reads=("ar_b",), writes=("ar_b",), tag=f"cb1_{s}_{hp}")
                p.op("dve", lambda q: q.tensor_scalar(S2, S2, -60.0, None,
                                                      op0=ALU.max),
                     reads=("ar_b",), writes=("ar_b",), tag=f"cb2_{s}_{hp}")
                p.op("act", lambda q: q.activation(v_eb, S2, AF.Exp, bias=0.0,
                                                   scale=1.0),
                     reads=("ar_b",), writes=("xT",), tag=f"eb_{s}_{hp}")
                p.op("act", lambda q: q.activation(v_emb, S2, AF.Exp, bias=0.0,
                                                   scale=-1.0),
                     reads=("ar_b",), writes=("xT",), tag=f"emb_{s}_{hp}")
                p.op("dve", lambda q: q.tensor_mul(v_rxf, v_rp, v_ef),
                     reads=("xT",), writes=("xT",), tag=f"rxf_{s}_{hp}")
                p.op("dve", lambda q: q.tensor_mul(v_kxf, v_kp, v_emf),
                     reads=("xT",), writes=("xT",), tag=f"kxf_{s}_{hp}")
                p.op("dve", lambda q: q.tensor_mul(v_rxb, v_rp, v_emb),
                     reads=("xT",), writes=("ktile",), tag=f"rxb_{s}_{hp}")
                p.op("dve", lambda q: q.tensor_mul(v_kxb, v_kp, v_eb),
                     reads=("xT",), writes=("ktile",), tag=f"kxb_{s}_{hp}")

                for hh in range(2):
                    h = 2 * hp + hh
                    po = hh * 64
                    for qt in range(NT):
                        for kq in range(NQ):
                            bank = b.pb[kq % 2]
                            bkk = f"pb{kq % 2}"
                            kts = [kq * 4 + j for j in range(4)]
                            dj = [j for j, kt in enumerate(kts) if kt == qt]

                            def mma(q, po=po, qt=qt, kts=kts, bank=bank):
                                ins = None
                                for j, kt in enumerate(kts):
                                    if kt <= qt:
                                        kex, rex = v_kxf, v_rxf
                                    else:
                                        kex, rex = v_kxb, v_rxb
                                    ins = q.matmul(
                                        bank[:, j * 128:(j + 1) * 128],
                                        kex[po:po + 64, kt * 128:(kt + 1) * 128],
                                        rex[po:po + 64, qt * 128:(qt + 1) * 128],
                                        start=True, stop=True)
                                    if kt == qt:
                                        ins = q.matmul(
                                            b.pb[5][:, 0:128],
                                            v_kxb[po:po + 64, kt * 128:(kt + 1) * 128],
                                            v_rxb[po:po + 64, qt * 128:(qt + 1) * 128],
                                            start=True, stop=True)
                                return ins

                            rd = ("xT", "ktile")
                            wr = (bkk, "pb5") if dj else (bkk,)
                            p.op("mm", mma, reads=rd, writes=wr,
                                 tag=f"at_{s}_{h}_{qt}_{kq}")

                            def eva(q, kq=kq, bank=bank):
                                return q.tensor_copy(
                                    ar_c.ap()[:, kq % 2 * 512: kq % 2 * 512 + 512],
                                    bank[:, 0:512])

                            p.op("dve", eva, reads=(bkk,), writes=("ar_c",),
                                 tag=f"ate_{s}_{h}_{qt}_{kq}")
                            if dj:
                                j = dj[0]
                                at_sl = ar_c.ap()[:, kq % 2 * 512 + j * 128:
                                                  kq % 2 * 512 + j * 128 + 128]

                                def evab(q):
                                    return q.tensor_copy(
                                        ar_c.ap()[:, 1024:1152], b.pb[5][:, 0:128])

                                p.op("dve", evab, reads=("pb5",), writes=("ar_c",),
                                     tag=f"atb_{s}_{h}_{qt}")

                                def msk1(q, at_sl=at_sl):
                                    return q.affine_select(
                                        ar_c.ap()[:, 1152:1280], at_sl,
                                        pattern=[[1, 128]], compare_op=ALU.is_ge,
                                        fill=0.0, base=0, channel_multiplier=-1)

                                p.op("pool", msk1, reads=("ar_c",), writes=("ar_c",),
                                     tag=f"mk1_{s}_{h}_{qt}")

                                def msk2(q):
                                    return q.affine_select(
                                        ar_c.ap()[:, 1280:1408],
                                        ar_c.ap()[:, 1024:1152],
                                        pattern=[[-1, 128]], compare_op=ALU.is_ge,
                                        fill=0.0, base=-1, channel_multiplier=1)

                                p.op("pool", msk2, reads=("ar_c",), writes=("ar_c",),
                                     tag=f"mk2_{s}_{h}_{qt}")

                                def mrg(q, at_sl=at_sl):
                                    return q.tensor_add(
                                        at_sl, ar_c.ap()[:, 1152:1280],
                                        ar_c.ap()[:, 1280:1408])

                                p.op("dve", mrg, reads=("ar_c",), writes=("ar_c",),
                                     tag=f"mrg_{s}_{h}_{qt}")

                            def pv(q, h=h, kq=kq, kts=kts):
                                ins = None
                                for j, kt in enumerate(kts):
                                    ins = q.matmul(
                                        b.pb[6][:, 0:64],
                                        ar_c.ap()[:, kq % 2 * 512 + j * 128:
                                                  kq % 2 * 512 + j * 128 + 128],
                                        vtok.ap()[:, kt * D + h * 64:
                                                  kt * D + h * 64 + 64],
                                        start=(kt == 0), stop=(kt == NT - 1))
                                return ins

                            p.op("mm", pv, reads=("ar_c", "vtok"), writes=("pb6",),
                                 tag=f"pv_{s}_{h}_{qt}_{kq}")

                        def evy(q, h=h, qt=qt):
                            return q.tensor_copy(
                                ytok.ap()[:, qt * D + h * 64: qt * D + h * 64 + 64],
                                b.pb[6][:, 0:64])

                        p.op("dve", evy, reads=("pb6",), writes=("ytok",),
                             tag=f"evy_{s}_{h}_{qt}")

            def ldwo(q):
                return q.dma_start(
                    out=wft.ap().rearrange("p (c d) -> p c d", c=NC),
                    in_=wts_d[WI["wo"]].rearrange("c p d -> p c d"))

            p.op("ld", ldwo, reads=("wts_dram",), writes=("wft",),
                 tag=f"ldwo_{s}")
            yc = ar_b.ap()[:, 0:1024]
            ysq = ar_b.ap()[:, 1024:2048]
            ygb = ar_c.ap()[:, 0:1024]
            ygT = ar_c.ap()[:, 1024:2048]
            for qt in range(NT):
                yv = ytok.ap()[:, qt * D:(qt + 1) * D].rearrange(
                    "p (g k) -> p g k", g=16)
                ycv = yc.rearrange("p (g k) -> p g k", g=16)
                ysv = ysq.rearrange("p (g k) -> p g k", g=16)

                def gsum(q, yv=yv):
                    return q.tensor_reduce(gns[:, 0:16], yv, axis=AX.X,
                                           op=ALU.add)

                p.op("dve", gsum, reads=("ytok",), writes=("gns",),
                     tag=f"gs_{s}_{qt}")
                p.op("dve", lambda q: q.tensor_scalar_mul(
                    gns[:, 16:32], gns[:, 0:16], 1.0 / 64),
                    reads=("gns",), writes=("gns",), tag=f"gm_{s}_{qt}")

                def gcen(q, yv=yv, ycv=ycv):
                    mb = gns.ap()[:, 16:32].unsqueeze(2).broadcast_to([128, 16, 64])
                    return q.tensor_tensor(ycv, yv, mb, ALU.subtract)

                p.op("dve", gcen, reads=("ytok", "gns"), writes=("ar_b",),
                     tag=f"gc_{s}_{qt}")
                p.op("act", lambda q: q.activation(ysq, yc, AF.Square, bias=0.0,
                                                   scale=1.0),
                     reads=("ar_b",), writes=("ar_b",), tag=f"gq2_{s}_{qt}")

                def gss(q, ysv=ysv):
                    return q.tensor_reduce(gns[:, 32:48], ysv, axis=AX.X,
                                           op=ALU.add)

                p.op("dve", gss, reads=("ar_b",), writes=("gns",),
                     tag=f"gv_{s}_{qt}")
                p.op("dve", lambda q: q.tensor_scalar(
                    gns[:, 48:64], gns[:, 32:48], 1.0 / 64, 64e-5,
                    op0=ALU.mult, op1=ALU.add),
                    reads=("gns",), writes=("gns",), tag=f"gvv_{s}_{qt}")
                p.op("act", lambda q: q.activation(gns[:, 64:80], gns[:, 48:64],
                                                   AF.Sqrt, bias=0.0, scale=1.0),
                     reads=("gns",), writes=("gns",), tag=f"gsd_{s}_{qt}")
                p.op("dve", lambda q: q.reciprocal(gns[:, 48:64], gns[:, 64:80]),
                     reads=("gns",), writes=("gns",), tag=f"gr_{s}_{qt}")

                def gnm(q, ycv=ycv):
                    rb = gns.ap()[:, 48:64].unsqueeze(2).broadcast_to([128, 16, 64])
                    return q.tensor_tensor(ycv, ycv, rb, ALU.mult)

                p.op("dve", gnm, reads=("ar_b", "gns"), writes=("ar_b",),
                     tag=f"gn_{s}_{qt}")
                p.op("dve", lambda q: q.tensor_mul(yc, yc, lnw_bc[:]),
                     reads=("ar_b", "lnw_bc"), writes=("ar_b",), tag=f"ga_{s}_{qt}")
                p.op("dve", lambda q: q.tensor_add(yc, yc, lnb_bc[:]),
                     reads=("ar_b", "lnb_bc"), writes=("ar_b",), tag=f"gb_{s}_{qt}")

                def ldg(q, qt=qt):
                    return q.dma_start(out=gq[:],
                                       in_=gsp_d[qt * 128:(qt + 1) * 128, :])

                p.op("pd", ldg, reads=("gsp_dram",), writes=("gq",),
                     sem="ld_gq", tag=f"ldg_{s}_{qt}")
                p.op("dve", lambda q: q.tensor_tensor(ygb, yc, gq[:], ALU.mult),
                     reads=("ar_b", "gq"), writes=("ar_c",), tag=f"gg_{s}_{qt}")
                for half in range(2):
                    bank = b.pb[4 + half]
                    bkk = f"pb{4 + half}"
                    bview = bank.ap().bitcast(BF16)

                    def trg(q, half=half, bview=bview):
                        ins = None
                        for j in range(4):
                            c = half * 4 + j
                            ins = q.transpose(bview[:, j * 128:(j + 1) * 128],
                                              ygb[:, c * 128:(c + 1) * 128],
                                              b.identb[:])
                        return ins

                    p.op("mm", trg, reads=("ar_c", "identb"), writes=(bkk,),
                         tag=f"tg_{s}_{qt}_{half}")

                    def evg2(q, half=half, bview=bview):
                        return q.tensor_copy(
                            ygT[:, half * 512:(half + 1) * 512], bview[:, 0:512])

                    p.op("dve", evg2, reads=(bkk,), writes=("ar_c",),
                         tag=f"tge_{s}_{qt}_{half}")
                def ldr2(q, qt=qt, xs=xs):
                    return q.dma_start(out=xres[qt % 2][:],
                                       in_=xs[qt * 128:(qt + 1) * 128, :])

                p.op("pd", ldr2, writes=("xres0",), tag=f"rr_{s}_{qt}")
                for og in range(2):
                    bank = b.pb[2 + og]
                    bkk = f"pb{2 + og}"

                    def mmo(q, og=og, bank=bank):
                        ins = None
                        for c in range(NC):
                            ins = q.matmul(
                                bank[:, 0:512],
                                ygT[:, c * 128:(c + 1) * 128],
                                wft.ap()[:, c * D + og * 512: c * D + og * 512 + 512],
                                start=(c == 0), stop=(c == NC - 1))
                        return ins

                    p.op("mm", mmo, reads=("ar_c", "wft"), writes=(bkk,),
                         tag=f"wo_{s}_{qt}_{og}")

                    def evo(q, og=og, qt=qt, bank=bank):
                        return q.scalar_tensor_tensor(
                            s_t[:, og * 512:(og + 1) * 512], bank[:, 0:512],
                            0.0, xres[qt % 2].ap()[:, og * 512:(og + 1) * 512],
                            op0=ALU.bypass, op1=ALU.add)

                    p.op("dve", evo, reads=(bkk, "xres0"), writes=("ar_a",),
                         tag=f"woe_{s}_{qt}_{og}")
                ln_tail(b, ln, s_t, "ar_a", ys[qt * 128:(qt + 1) * 128, :],
                        f"l1a_{s}_{qt}")

        p.emit()
    return nc



def build_l1_ffn(cfg):
    T, B = cfg["T"], cfg["B"]
    TOK = T * B
    TCW = min(512, TOK)
    NTC = TOK // TCW
    NTT = TCW // 128

    nc = bass.Bass("TRN2", target_bir_lowering=False)
    x_d = nc.dram_tensor("x", [TOK, D], F32, kind="ExternalInput").ap()
    w_d = nc.dram_tensor("w", [D, D], F32, kind="ExternalInput").ap()
    sc_d = nc.dram_tensor("sc", [D], F32, kind="ExternalInput").ap()
    bi_d = nc.dram_tensor("bi", [D], F32, kind="ExternalInput").ap()
    y_d = nc.dram_tensor("y", [TOK, D], F32, kind="ExternalOutput").ap()

    b = Blk(nc, cfg)
    p = b.p
    with contextlib.ExitStack() as es:
        b.es = es
        common_prep(b)

        swt = b.sbuf("swt", [128, NC * D], BF16)
        wsl = b.sbuf("wsl", [128, D], F32)
        wsg = b.sbuf("wsg", [128, D], F32)
        for rc in range(NC):
            def ldw(q, rc=rc):
                return q.dma_start(out=wsl[:], in_=w_d[rc * 128:(rc + 1) * 128, :])

            p.op("ld", ldw, writes=("wsl",), tag=f"sw_ld{rc}")
            p.op("dve", lambda q: q.tensor_scalar(wsg[:], wsl[:], 0.0, None,
                                                  op0=ALU.is_ge),
                 reads=("wsl",), writes=("wsg",), tag=f"sw_ge{rc}")
            p.op("dve", lambda q: q.tensor_scalar(wsg[:], wsg[:], 2.0, -1.0,
                                                  op0=ALU.mult, op1=ALU.add),
                 reads=("wsg",), writes=("wsg",), tag=f"sw_sc{rc}")
            for half in range(2):
                bank = b.pb[6 + half]
                bkk = f"pb{6 + half}"

                def trs(q, half=half, bank=bank):
                    ins = None
                    for j in range(4):
                        c = half * 4 + j
                        ins = q.transpose(bank[:, j * 128:(j + 1) * 128],
                                          wsg[:, c * 128:(c + 1) * 128], b.ident[:])
                    return ins

                p.op("mm", trs, reads=("wsg", "ident"), writes=(bkk,),
                     tag=f"sw_tr{rc}_{half}")

                def evw(q, rc=rc, half=half, bank=bank):
                    ins = None
                    for j in range(4):
                        c = half * 4 + j
                        ins = q.tensor_copy(
                            swt.ap()[:, c * D + rc * 128: c * D + rc * 128 + 128],
                            bank[:, j * 128:(j + 1) * 128])
                    return ins

                p.op("dve", evw, reads=(bkk,), writes=("swt",),
                     tag=f"sw_ev{rc}_{half}")

        scr = b.sbuf("scr", [1, D], F32R)
        bir = b.sbuf("bir", [1, D], F32R)
        p.op("ld", lambda q: q.dma_start(out=scr[:], in_=sc_d[None, :].bitcast(F32R)),
             writes=("scr",), tag="ld_sc")
        p.op("ld", lambda q: q.dma_start(out=bir[:], in_=bi_d[None, :].bitcast(F32R)),
             writes=("bir",), tag="ld_bi")
        sc_bc = b.sbuf("sc_bc", [128, D], F32)
        bi_bc = b.sbuf("bi_bc", [128, D], F32)
        for half in range(2):
            for (row, rk, dst, dk) in ((scr, "scr", sc_bc, "sc_bc"),
                                       (bir, "bir", bi_bc, "bi_bc")):
                def mmb(q, row=row, half=half):
                    return q.matmul(b.pb[5][:, 0:512], b.ones_r[:],
                                    row[:, half * 512:(half + 1) * 512],
                                    start=True, stop=True)

                p.op("mm", mmb, reads=(rk, "ones_r"), writes=("pb5",),
                     tag=f"bc_mm_{dk}_{half}")

                def evb(q, dst=dst, half=half):
                    return q.tensor_copy(dst[:, half * 512:(half + 1) * 512],
                                         b.pb[5][:, 0:512])

                p.op("dve", evb, reads=("pb5",), writes=(dk,),
                     tag=f"bc_ev_{dk}_{half}")

        xres = [b.sbuf("xres", [128, 1024], F32) for _ in range(NTT)]
        xT = b.sbuf("xT", [128, NC * TCW], BF16)
        s_t = b.sbuf("s_t", [128, 1024], F32)
        ln = LNScratch(b)

        for tc in range(NTC):
            for j in range(NTT):
                t0g = tc * TCW + j * 128
                tin(b, x_d[t0g:t0g + 128, :], xres[j], f"xres{j}",
                    [(xT, "xT", TCW, j * 128)], t0g, banks=(6, 7))
            for j in range(NTT):
                for og in range(2):
                    bank = b.pb[og]
                    bkk = f"pb{og}"

                    def mmp(q, j=j, og=og, bank=bank):
                        ins = None
                        for c in range(NC):
                            ins = q.matmul(
                                bank[:, 0:512],
                                xT.ap()[:, c * TCW + j * 128: c * TCW + j * 128 + 128],
                                swt.ap()[:, c * D + og * 512: c * D + og * 512 + 512],
                                start=(c == 0), stop=(c == NC - 1))
                        return ins

                    p.op("mm", mmp, reads=("xT", "swt"), writes=(bkk,),
                         tag=f"pr_mm_{tc}_{j}_{og}")

                    def ev1(q, og=og, bank=bank):
                        return q.tensor_mul(s_t[:, og * 512:(og + 1) * 512],
                                            bank[:, 0:512],
                                            sc_bc[:, og * 512:(og + 1) * 512])

                    p.op("dve", ev1, reads=(bkk, "sc_bc"), writes=("s_t",),
                         tag=f"pr_sc_{tc}_{j}_{og}")

                    def ev2(q, og=og):
                        return q.tensor_add(s_t[:, og * 512:(og + 1) * 512],
                                            s_t[:, og * 512:(og + 1) * 512],
                                            bi_bc[:, og * 512:(og + 1) * 512])

                    p.op("dve", ev2, reads=("s_t", "bi_bc"), writes=("s_t",),
                         tag=f"pr_bi_{tc}_{j}_{og}")

                    def ev3(q, og=og):
                        return q.tensor_scalar(s_t[:, og * 512:(og + 1) * 512],
                                               s_t[:, og * 512:(og + 1) * 512],
                                               100.0, -100.0,
                                               op0=ALU.min, op1=ALU.max)

                    p.op("dve", ev3, reads=("s_t",), writes=("s_t",),
                         tag=f"pr_cl_{tc}_{j}_{og}")

                    def ev4(q, j=j, og=og):
                        return q.tensor_add(s_t[:, og * 512:(og + 1) * 512],
                                            s_t[:, og * 512:(og + 1) * 512],
                                            xres[j].ap()[:, og * 512:(og + 1) * 512])

                    p.op("dve", ev4, reads=("s_t", f"xres{j}"), writes=("s_t",),
                         tag=f"pr_rs_{tc}_{j}_{og}")
                t0g = tc * TCW + j * 128
                ln_tail(b, ln, s_t, "s_t", y_d[t0g:t0g + 128, :], f"l1f_{tc}_{j}")

        p.emit()
    return nc


CFG_FULL = dict(T=2048, B=4)


def _f32(a):
    return np.ascontiguousarray(np.asarray(a, dtype=np.float32))


def kernel(**inputs):
    cfg = CFG_FULL
    x = _f32(inputs["L0_attn_x"])
    shards = [np.ascontiguousarray(s) for s in np.split(x, N_CORES, axis=0)]
    cores = list(range(N_CORES))

    nc1 = build_l0_attn(cfg)
    m1 = dict(
        wq=_f32(inputs["L0_attn_Wq"]), wk=_f32(inputs["L0_attn_Wk"]),
        wv=_f32(inputs["L0_attn_Wv"]), wo=_f32(inputs["L0_attn_Wo"]),
        bq=_f32(inputs["L0_attn_bq"]), bk=_f32(inputs["L0_attn_bk"]),
        bv=_f32(inputs["L0_attn_bv"]), bo=_f32(inputs["L0_attn_bo"]),
        alpha=_f32(inputs["L0_attn_alpha"]).reshape(1),
    )
    res = run_bass_kernel_spmd(nc1, [dict(m1, x=s) for s in shards],
                               core_ids=cores)
    y = [r["y"] for r in res.results]

    nc2 = build_l0_ffn(cfg)
    m2 = dict(
        w1=_f32(inputs["L0_ffn_W1"]), b1=_f32(inputs["L0_ffn_b1"]),
        w2=_f32(inputs["L0_ffn_W2"]), b2=_f32(inputs["L0_ffn_b2"]),
        wr=_f32(inputs["L0_ffn_Wr"]), br=_f32(inputs["L0_ffn_br"]),
    )
    res = run_bass_kernel_spmd(
        nc2, [dict(m2, x=yy.reshape(cfg["B"] * cfg["T"], D)) for yy in y],
        core_ids=cores)
    y = [r["y"] for r in res.results]

    nc3 = build_l1_attn(cfg)
    m3 = dict(
        tmx=_f32(inputs["L1_attn_time_maa_x"]).reshape(D),
        tmw=_f32(inputs["L1_attn_time_maa_w"]).reshape(D),
        tmk=_f32(inputs["L1_attn_time_maa_k"]).reshape(D),
        tmv=_f32(inputs["L1_attn_time_maa_v"]).reshape(D),
        tmr=_f32(inputs["L1_attn_time_maa_r"]).reshape(D),
        tmg=_f32(inputs["L1_attn_time_maa_g"]).reshape(D),
        tw1=_f32(inputs["L1_attn_time_maa_w1"]),
        tw2=_f32(inputs["L1_attn_time_maa_w2"]),
        tdec=_f32(inputs["L1_attn_time_decay"]).reshape(D),
        dw1=_f32(inputs["L1_attn_time_decay_w1"]),
        dw2=_f32(inputs["L1_attn_time_decay_w2"]),
        wr=_f32(inputs["L1_attn_W_r"]), wk=_f32(inputs["L1_attn_W_k"]),
        wv=_f32(inputs["L1_attn_W_v"]), wg=_f32(inputs["L1_attn_W_g"]),
        wo=_f32(inputs["L1_attn_W_o"]),
        lnw=_f32(inputs["L1_attn_ln_w"]), lnb=_f32(inputs["L1_attn_ln_b"]),
    )
    res = run_bass_kernel_spmd(
        nc3, [dict(m3, x=yy.reshape(cfg["B"], cfg["T"], D)) for yy in y],
        core_ids=cores)
    y = [r["y"] for r in res.results]

    nc4 = build_l1_ffn(cfg)
    m4 = dict(
        w=_f32(inputs["L1_ffn_weights"]),
        sc=_f32(inputs["L1_ffn_scale"]),
        bi=_f32(inputs["L1_ffn_bias"]),
    )
    res = run_bass_kernel_spmd(
        nc4, [dict(m4, x=yy.reshape(cfg["B"] * cfg["T"], D)) for yy in y],
        core_ids=cores)
    y = [r["y"].reshape(cfg["B"], cfg["T"], D) for r in res.results]
    return np.concatenate(y, axis=0)


# version 06
# speedup vs baseline: 1.3980x; 1.3980x over previous
import contextlib
import numpy as np
import concourse.bass as bass
import concourse.mybir as mybir
from concourse.bass_utils import run_bass_kernel_spmd

AF = mybir.ActivationFunctionType
ALU = mybir.AluOpType
AX = mybir.AxisListType
F32 = mybir.dt.float32
F32R = mybir.dt.float32r
BF16 = mybir.dt.bfloat16

N_CORES = 8
D = 1024
NC = 8
NEG_BIG = -1e30


LANES = {
    "mm": ("tensor", 1),
    "act": ("scalar", 1),
    "dve": ("vector", 1),
    "pool": ("gpsimd", 1),
    "ld": ("sync", 16),
    "sd": ("scalar", 16),
    "pd": ("gpsimd", 16),
}


class Prog:

    def __init__(self, nc):
        self.nc = nc
        self.ops = []
        self.count = {}
        self.lane_count = {k: 0 for k in LANES}
        self.regions = {}
        self.pending_barrier = None
        self.barrier_seen = set()

    def claim(self, dst, *srcs):
        ws = {}
        rs = []
        w = None
        for srfranc in srcs:
            r = self.regions.get(srfranc)
            if not r:
                continue
            if r.get("w"):
                rs.append(r["w"])
            rs.extend(r["r"])
            for sm, c in r.get("ws", {}).items():
                if ws.get(sm, 0) < c:
                    ws[sm] = c
        reg = self.regions.setdefault(dst, {"w": None, "r": [], "no_waw": False})
        reg["r"] = reg["r"] + rs
        old = reg.get("ws", {})
        for sm, c in ws.items():
            if old.get(sm, 0) < c:
                old[sm] = c
        reg["ws"] = old

    def barrier(self):
        self.pending_barrier = dict(self.count)
        self.barrier_seen = set()
        self.regions = {}

    def op(self, lane, fn, reads=(), writes=(), tag="", sem=None, no_waw=False):
        assert lane in LANES
        is_dma = LANES[lane][1] == 16
        if sem is None:
            if is_dma:
                key0 = writes[0] if writes else (reads[0] if reads else lane)
                sem = "dma_" + key0
            else:
                sem = lane
        waits = {}

        def add(s, cnt):
            if waits.get(s, 0) < cnt:
                waits[s] = cnt

        eng = LANES[lane][0]
        if self.pending_barrier is not None and eng not in self.barrier_seen:
            for s, c in self.pending_barrier.items():
                add(s, c)
            self.barrier_seen.add(eng)
        for key in reads:
            reg = self.regions.get(key)
            if reg:
                if reg["w"]:
                    add(*reg["w"])
                for s, c in reg.get("ws", {}).items():
                    add(s, c)
        for key in writes:
            reg = self.regions.get(key)
            if reg:
                if not (no_waw and reg.get("no_waw")):
                    if reg["w"]:
                        add(*reg["w"])
                    for s, c in reg.get("ws", {}).items():
                        add(s, c)
                for r in reg["r"]:
                    add(*r)
        self.count[sem] = self.count.get(sem, 0) + LANES[lane][1]
        me = (sem, self.count[sem])
        for key in reads:
            reg = self.regions.setdefault(key, {"w": None, "r": [], "no_waw": False})
            reg["r"].append(me)
        for key in writes:
            if no_waw:
                reg = self.regions.setdefault(
                    key, {"w": None, "r": [], "no_waw": True, "ws": {}})
                reg["no_waw"] = True
                reg.setdefault("ws", {})[sem] = self.count[sem]
                reg["w"] = None
                reg["r"] = []
            else:
                self.regions[key] = {"w": me, "r": [], "no_waw": False}
        self.lane_count[lane] += 1
        self.ops.append({"lane": lane, "fn": fn, "waits": waits, "sem": sem,
                         "amt": LANES[lane][1], "tag": tag})
        return me

    def emit(self):
        nc = self.nc
        with contextlib.ExitStack() as es:
            sems = {s: es.enter_context(nc.semaphore(f"s_{s}"))
                    for s in self.count}
            block = es.enter_context(nc.Block())
            per_engine = {}
            for o in self.ops:
                per_engine.setdefault(LANES[o["lane"]][0], []).append(o)

            def make_section(ops):
                hwm = {}

                def section(q):
                    for o in ops:
                        for s, cnt in sorted(o["waits"].items()):
                            if hwm.get(s, 0) >= cnt:
                                continue
                            q.wait_ge(sems[s], cnt)
                            hwm[s] = cnt
                        ins = o["fn"](q)
                        assert ins is not None, f"op returned None: {o['tag']}"
                        ins.then_inc(sems[o["sem"]], o["amt"])
                return section

            for eng, ops in per_engine.items():
                getattr(block, eng)(make_section(ops))

            final = dict(self.count)

            def tail(q):
                for s, cnt in sorted(final.items()):
                    q.wait_ge(sems[s], cnt)

            block.sync(tail)


class Blk:
    def __init__(self, nc, cfg):
        self.nc = nc
        self.p = Prog(nc)
        self.cfg = cfg
        self.es = None
        self.uid = 0
        self.pb = None

    def name(self, base):
        self.uid += 1
        return f"{base}_{self.uid}"

    def sbuf(self, base, shape, dtype):
        return self.es.enter_context(
            self.nc.sbuf_tensor(self.name(base), list(shape), dtype))

    def psum(self, base, shape, dtype=F32):
        return self.es.enter_context(
            self.nc.psum_tensor(self.name(base), list(shape), dtype))



def common_prep(b):
    p = b.p
    b.pbS = b.psum("pbS", [128, 2048], F32)
    b.pb = [b.pbS.ap()[:, i * 512:(i + 1) * 512] for i in range(4)] + \
           [b.psum("pb", [128, 512], F32).ap() for i in range(4)]
    b.ident = b.sbuf("ident", [128, 128], F32)
    b.identb = b.sbuf("identb", [128, 128], BF16)
    b.ones_r = b.sbuf("ones_r", [1, 128], F32R)
    b.ones_b = b.sbuf("ones_b", [1, 128], BF16)
    onef = b.onef = b.sbuf("onef", [128, 128], F32)
    oneb = b.oneb = b.sbuf("oneb", [128, 128], BF16)

    p.op("pool", lambda q: q.memset(onef[:], 1.0), writes=("onef",), tag="memset_onef")
    p.op("pool", lambda q: q.memset(oneb[:], 1.0), writes=("oneb",), tag="memset_oneb")
    p.op("pool", lambda q: q.affine_select(
        b.ident[:], onef[:], pattern=[[-1, 128]], compare_op=ALU.is_equal,
        fill=0.0, base=0, channel_multiplier=1),
        reads=("onef",), writes=("ident",), tag="ident")
    p.op("pool", lambda q: q.affine_select(
        b.identb[:], oneb[:], pattern=[[-1, 128]], compare_op=ALU.is_equal,
        fill=0.0, base=0, channel_multiplier=1),
        reads=("oneb",), writes=("identb",), tag="identb")
    p.op("dve", lambda q: q.tensor_copy(b.ones_r[:], onef[0:1, :]),
         reads=("onef",), writes=("ones_r",), tag="ones_r")
    p.op("dve", lambda q: q.tensor_copy(b.ones_b[:], oneb[0:1, :]),
         reads=("oneb",), writes=("ones_b",), tag="ones_b")



def load_cols(b, dst, dst_key, src_vec_ap, n):
    p = b.p
    for c in range(n):
        def ld(q, c=c):
            return q.dma_start(out=dst[:, c], in_=src_vec_ap[c * 128:(c + 1) * 128])
        p.op("ld", ld, writes=(dst_key,), tag=f"ldcol_{dst_key}_{c}")


def row_bcast(b, dst, dst_key, row_ap, row_key, bank_i, width, dtype):
    p = b.p
    bank = b.pb[bank_i]
    bk = f"pb{bank_i}"
    ones = b.ones_r

    def mm(q):
        return q.matmul(bank[:, 0:width], ones[:], row_ap, start=True, stop=True)

    p.op("mm", mm, reads=(row_key, "ones_r"), writes=(bk,), tag=f"bcast_{dst_key}")

    def ev(q):
        return q.tensor_copy(dst[:, 0:width], bank[:, 0:width])

    p.op("dve", ev, reads=(bk,), writes=(dst_key,), tag=f"bcast_ev_{dst_key}")


def tin(b, x_rows_ap, xres, xres_key, dsts, t0, banks=(6, 7)):
    p = b.p

    def ld(q):
        return q.dma_start(out=xres[:], in_=x_rows_ap)

    p.op("pd", ld, writes=(xres_key,), tag=f"tin_ld_{xres_key}_{t0}")
    for half in range(2):
        bank = b.pb[banks[half]]
        bk = f"pb{banks[half]}"

        def tr(q, half=half, bank=bank):
            ins = None
            for j in range(4):
                c = half * 4 + j
                ins = q.transpose(bank[:, j * 128:(j + 1) * 128],
                                  xres[:, c * 128:(c + 1) * 128], b.ident[:])
            return ins

        p.op("mm", tr, reads=(xres_key, "ident"), writes=(bk,),
             tag=f"tin_tr_{t0}_{half}")

        for (tens, key, cstride, colbase) in dsts:
            def ev(q, half=half, bank=bank, tens=tens, cstride=cstride,
                   colbase=colbase):
                ins = None
                for j in range(4):
                    c = half * 4 + j
                    ins = q.tensor_copy(
                        tens.ap()[:, c * cstride + colbase: c * cstride + colbase + 128],
                        bank[:, j * 128:(j + 1) * 128])
                return ins

            p.op("dve", ev, reads=(bk,), writes=(key,),
                 tag=f"tin_ev_{key}_{t0}_{half}")


class LNScratch:
    def __init__(self, b, cen=None, sq=None, cen_key="ln_cen", sq_key="ln_sq"):
        self.cen = cen if cen is not None else b.sbuf("ln_cen", [128, 1024], F32).ap()
        self.sq = sq if sq is not None else b.sbuf("ln_sq", [128, 1024], F32).ap()
        self.st = b.sbuf("ln_st", [128, 8], F32).ap()
        self.cen_key = cen_key
        self.sq_key = sq_key


def ln_tail(b, ln, s_t, s_key, out_ap, tag):
    p = b.p
    cen, sq, st = ln.cen, ln.sq, ln.st
    k_cen, k_sq, k_st = ln.cen_key, ln.sq_key, "ln_st"

    p.op("dve", lambda q: q.reduce_sum(st[:, 0:1], s_t[:], axis=AX.X),
         reads=(s_key,), writes=(k_st,), tag=f"ln_sum_{tag}")
    p.op("dve", lambda q: q.tensor_scalar_mul(st[:, 1:2], st[:, 0:1], 1.0 / 1024),
         reads=(k_st,), writes=(k_st,), tag=f"ln_mean_{tag}")
    p.op("dve", lambda q: q.tensor_scalar(cen, s_t[:], st[:, 1:2], None,
                                          op0=ALU.subtract),
         reads=(s_key, k_st), writes=(k_cen,), tag=f"ln_cen_{tag}")
    p.op("act", lambda q: q.activation(sq, cen, AF.Square, bias=0.0,
                                       scale=1.0, accum_out=st[:, 2:3]),
         reads=(k_cen,), writes=(k_sq, k_st), tag=f"ln_sq_{tag}")
    p.op("dve", lambda q: q.tensor_scalar(st[:, 3:4], st[:, 2:3], 1.0 / 1024, 1e-5,
                                          op0=ALU.mult, op1=ALU.add),
         reads=(k_st,), writes=(k_st,), tag=f"ln_var_{tag}")
    p.op("act", lambda q: q.activation(st[:, 4:5], st[:, 3:4], AF.Sqrt,
                                       bias=0.0, scale=1.0),
         reads=(k_st,), writes=(k_st,), tag=f"ln_sd_{tag}")
    p.op("dve", lambda q: q.reciprocal(st[:, 5:6], st[:, 4:5]),
         reads=(k_st,), writes=(k_st,), tag=f"ln_rstd_{tag}")
    p.op("dve", lambda q: q.tensor_scalar_mul(cen, cen, st[:, 5:6]),
         reads=(k_cen, k_st), writes=(k_cen,), tag=f"ln_y_{tag}")
    p.op("pd", lambda q: q.dma_start(out=out_ap, in_=cen),
         reads=(k_cen,), tag=f"ln_out_{tag}")



def emit_l0_ffn(b, cfg, DR):
    nc = b.nc
    p = b.p
    T, B = cfg["T"], cfg["B"]
    HF, E = 4096, 8
    TOK = T * B
    TCW = min(512, TOK)
    NTC = TOK // TCW
    NTT = TCW // 128
    NHG = HF // 512
    NH2 = HF // 128

    x_d = DR["x_d"]
    w1_d = DR["w1_d"]
    b1_d = DR["b1_d"]
    w2_d = DR["w2_d"]
    b2_d = DR["b2_d"]
    wr_d = DR["wr_d"]
    br_d = DR["br_d"]
    y_d = DR["y_d"]
    w1b_d = DR["w1b_d"]
    w2b_d = DR["w2b_d"]

    cast_in = [b.sbuf("cast_in", [128, 4096], F32) for _ in range(2)]
    cast_out = [b.sbuf("cast_out", [128, 4096], BF16) for _ in range(2)]
    n_slab1 = E * D // 128
    n_slab2 = E * HF // 128
    w1_r = w1_d.rearrange("e (s p) h -> (e s) p h", p=128)
    w1b_r = w1b_d.rearrange("e (s p) h -> (e s) p h", p=128)
    w2_r = w2_d.rearrange("e (s p) d -> (e s) p d", p=128)
    w2b_r = w2b_d.rearrange("e (s p) d -> (e s) p d", p=128)
    for i in range(n_slab1):
        par = i % 2

        def ld(q, i=i, par=par):
            return q.dma_start(out=cast_in[par][:], in_=w1_r[i])

        def cv(q, par=par):
            return q.tensor_copy(cast_out[par][:], cast_in[par][:])

        def sto(q, i=i, par=par):
            return q.dma_start(out=w1b_r[i], in_=cast_out[par][:])

        p.op("ld", ld, writes=(f"cin{par}",), tag=f"w1cast_ld{i}")
        p.op("dve", cv, reads=(f"cin{par}",), writes=(f"cout{par}",),
             tag=f"w1cast_cv{i}")
        p.op("pd", sto, reads=(f"cout{par}",), writes=("w1b_dram",),
             sem=f"st_w1b{par}", no_waw=True, tag=f"w1cast_st{i}")
    w2_r4 = w2_d.rearrange("e (g s p) d -> (e g) p s d", p=128, s=4)
    w2b_r4 = w2b_d.rearrange("e (g s p) d -> (e g) p s d", p=128, s=4)
    for i in range(n_slab2 // 4):
        par = i % 2

        def ld(q, i=i, par=par):
            return q.dma_start(
                out=cast_in[par].ap().rearrange("p (s d) -> p s d", s=4),
                in_=w2_r4[i])

        def cv(q, par=par):
            return q.tensor_copy(cast_out[par][:], cast_in[par][:])

        def sto(q, i=i, par=par):
            return q.dma_start(
                out=w2b_r4[i],
                in_=cast_out[par].ap().rearrange("p (s d) -> p s d", s=4))

        p.op("ld", ld, writes=(f"cin{par}",), tag=f"w2cast_ld{i}")
        p.op("dve", cv, reads=(f"cin{par}",), writes=(f"cout{par}",),
             tag=f"w2cast_cv{i}")
        p.op("pd", sto, reads=(f"cout{par}",), writes=("w2b_dram",),
             sem=f"st_w2b{par}", no_waw=True, tag=f"w2cast_st{i}")

    wr_sb = b.sbuf("wr_sb", [128, NC * E], F32R)
    br_row = b.sbuf("br_row", [1, E], F32R)
    b1c = b.sbuf("b1c", [128, E * (HF // 128)], F32)
    b2_row = b.sbuf("b2_row", [8, D], BF16)
    b2f = b.sbuf("b2f", [8, D], F32)

    p.op("ld", lambda q: q.dma_start(
        out=wr_sb.ap().rearrange("p (c e) -> p c e", c=NC),
        in_=wr_d.rearrange("(c p) e -> p c e", p=128).bitcast(F32R)),
        writes=("wr_sb",), tag="ld_wr")
    p.op("ld", lambda q: q.dma_start(out=br_row[:], in_=br_d[None, :].bitcast(F32R)),
         writes=("br_row",), tag="ld_br")
    for e in range(E):
        for hc in range(HF // 128):
            def ld(q, e=e, hc=hc):
                return q.dma_start(out=b1c[:, e * 32 + hc],
                                   in_=b1_d[e, hc * 128:(hc + 1) * 128])
            p.op("ld", ld, writes=("b1c",), tag=f"ld_b1_{e}_{hc}")
    p.op("ld", lambda q: q.dma_start(out=b2f[:], in_=b2_d), writes=("b2f",),
         tag="ld_b2")
    p.op("dve", lambda q: q.tensor_copy(b2_row[:], b2f[:]),
         reads=("b2f",), writes=("b2_row",), tag="cv_b2")
    oneh = b.sbuf("oneh", [8, E * 128], BF16)
    for e in range(E):
        def mkoh(q, e=e):
            return q.affine_select(oneh[:, e * 128:(e + 1) * 128],
                                   b.oneb[0:8, 0:128], pattern=[[0, 128]],
                                   compare_op=ALU.is_equal, fill=0.0,
                                   base=-e, channel_multiplier=1)
        p.op("pool", mkoh, reads=("oneb",), writes=("oneh",), tag=f"oneh{e}")

    xres = [b.sbuf("xres", [128, 1024], F32) for _ in range(NTT)]
    xT = b.sbuf("xT", [128, NC * TCW], BF16)
    xTr = b.sbuf("xTr", [128, NC * TCW], F32R)
    lg = b.sbuf("lg", [128, E], F32)
    r_m1 = b.sbuf("r_m1", [128, 8], F32)
    r_eq = b.sbuf("r_eq", [128, E], F32)
    r_w = b.sbuf("r_w", [128, E], F32)
    wT_r = b.sbuf("wT_r", [8, TCW], F32R)
    wT_b = b.sbuf("wT_b", [8, TCW], BF16)
    wbc = b.sbuf("wbc", [128, TCW], BF16)
    hg_t = b.sbuf("hg_t", [128, 512], BF16)
    hs = b.sbuf("hs", [128, NH2 * TCW], BF16)
    w1s = [b.sbuf("w1s", [128, NC * 512], BF16) for _ in range(2)]
    w2s = [b.sbuf("w2s", [128, 4 * D], BF16) for _ in range(2)]
    acc = b.sbuf("acc", [128, NC * TCW], F32)
    s_t = b.sbuf("s_t", [128, 1024], F32)
    ln = LNScratch(b)

    for tc in range(NTC):
        for j in range(NTT):
            t0g = tc * TCW + j * 128
            tin(b, x_d[t0g:t0g + 128, :], xres[j], f"xres{j}",
                [(xT, "xT", TCW, j * 128), (xTr, "xTr", TCW, j * 128)],
                t0g, banks=(4, 5))
        for j in range(NTT):
            bank = b.pb[5]

            def mmr(q, j=j):
                ins = None
                for c in range(NC):
                    ins = q.matmul(bank[:, 0:E],
                                   xTr.ap()[:, c * TCW + j * 128: c * TCW + (j) * 128 + 128],
                                   wr_sb.ap()[:, c * E:(c + 1) * E],
                                   start=(c == 0), stop=False)
                ins = q.matmul(bank[:, 0:E], b.ones_r[:], br_row[:],
                               start=False, stop=True)
                return ins

            p.op("mm", mmr, reads=("xTr", "wr_sb", "br_row", "ones_r"),
                 writes=("pb5",), tag=f"route_mm_{tc}_{j}")

            K = ("lg", "r_m1", "r_eq", "r_w")
            p.op("dve", lambda q: q.tensor_copy(lg[:], b.pb[5][:, 0:E]),
                 reads=("pb5",), writes=("lg",), tag=f"rt_cp_{tc}_{j}")
            p.op("dve", lambda q: q.reduce_max(r_m1[:, 0:1], lg[:], axis=AX.X),
                 reads=("lg",), writes=("r_m1",), tag=f"rt_m1_{tc}_{j}")
            p.op("dve", lambda q: q.tensor_scalar(r_eq[:], lg[:], r_m1[:, 0:1],
                                                  None, op0=ALU.is_equal),
                 reads=("lg", "r_m1"), writes=("r_eq",), tag=f"rt_eq_{tc}_{j}")
            p.op("dve", lambda q: q.reduce_sum(r_m1[:, 1:2], r_eq[:], axis=AX.X),
                 reads=("r_eq",), writes=("r_m1",), tag=f"rt_cnt_{tc}_{j}")
            p.op("dve", lambda q: q.scalar_tensor_tensor(
                r_eq[:], r_eq[:], NEG_BIG, lg[:], op0=ALU.mult, op1=ALU.add),
                reads=("r_eq", "lg"), writes=("r_eq",), tag=f"rt_l2_{tc}_{j}")
            p.op("dve", lambda q: q.reduce_max(r_m1[:, 2:3], r_eq[:], axis=AX.X),
                 reads=("r_eq",), writes=("r_m1",), tag=f"rt_m2_{tc}_{j}")
            p.op("dve", lambda q: q.tensor_sub(r_m1[:, 3:4], r_m1[:, 0:1],
                                               r_m1[:, 2:3]),
                 reads=("r_m1",), writes=("r_m1",), tag=f"rt_dif_{tc}_{j}")
            p.op("dve", lambda q: q.tensor_scalar(r_m1[:, 4:5], r_m1[:, 1:2],
                                                  2.0, None, op0=ALU.is_ge),
                 reads=("r_m1",), writes=("r_m1",), tag=f"rt_ge2_{tc}_{j}")
            p.op("dve", lambda q: q.tensor_mul(r_m1[:, 4:5], r_m1[:, 4:5],
                                               r_m1[:, 3:4]),
                 reads=("r_m1",), writes=("r_m1",), tag=f"rt_g2d_{tc}_{j}")
            p.op("dve", lambda q: q.tensor_add(r_m1[:, 5:6], r_m1[:, 2:3],
                                               r_m1[:, 4:5]),
                 reads=("r_m1",), writes=("r_m1",), tag=f"rt_thr_{tc}_{j}")
            p.op("dve", lambda q: q.tensor_scalar(r_w[:], lg[:], r_m1[:, 5:6],
                                                  None, op0=ALU.is_ge),
                 reads=("lg", "r_m1"), writes=("r_w",), tag=f"rt_msk_{tc}_{j}")
            p.op("dve", lambda q: q.tensor_scalar(r_w[:], r_w[:], 1.0, None,
                                                  op0=ALU.subtract),
                 reads=("r_w",), writes=("r_w",), tag=f"rt_mm1_{tc}_{j}")
            p.op("dve", lambda q: q.scalar_tensor_tensor(
                lg[:], r_w[:], -NEG_BIG, lg[:], op0=ALU.mult, op1=ALU.add),
                reads=("r_w", "lg"), writes=("lg",), tag=f"rt_l3_{tc}_{j}")
            p.op("dve", lambda q: q.tensor_scalar_mul(r_m1[:, 6:7],
                                                      r_m1[:, 0:1], -1.0),
                 reads=("r_m1",), writes=("r_m1",), tag=f"rt_nm1_{tc}_{j}")
            p.op("act", lambda q: q.activation(r_w[:], lg[:], AF.Exp,
                                               bias=r_m1[:, 6:7], scale=1.0,
                                               accum_out=r_m1[:, 7:8]),
                 reads=("lg", "r_m1"), writes=("r_w", "r_m1"),
                 tag=f"rt_exp_{tc}_{j}")
            p.op("dve", lambda q: q.reciprocal(r_m1[:, 6:7], r_m1[:, 7:8]),
                 reads=("r_m1",), writes=("r_m1",), tag=f"rt_rz_{tc}_{j}")
            p.op("dve", lambda q: q.tensor_scalar_mul(r_w[:], r_w[:],
                                                      r_m1[:, 6:7]),
                 reads=("r_w", "r_m1"), writes=("r_w",), tag=f"rt_w_{tc}_{j}")

            def trw(q, j=j):
                return q.transpose(b.pb[6][0:E, j * 128:(j + 1) * 128],
                                   r_w[:, 0:E], b.ident[:])

            p.op("mm", trw, reads=("r_w", "ident"), writes=("pb6",),
                 tag=f"rt_tr_{tc}_{j}")
        p.op("dve", lambda q: q.tensor_copy(wT_r[:], b.pb[6][0:E, 0:TCW]),
             reads=("pb6",), writes=("wT_r",), tag=f"rt_evr_{tc}")
        p.op("dve", lambda q: q.tensor_copy(wT_b[:], b.pb[6][0:E, 0:TCW]),
             reads=("pb6",), writes=("wT_b",), tag=f"rt_evb_{tc}")

        for e in range(E):
            def mmb(q, e=e):
                return q.matmul(b.pb[7][:, 0:TCW],
                                oneh[:, e * 128:(e + 1) * 128],
                                wT_b[:], start=True, stop=True)

            p.op("mm", mmb, reads=("wT_b", "oneh"), writes=("pb7",),
                 tag=f"wbc_mm_{tc}_{e}")
            p.op("dve", lambda q: q.tensor_copy(wbc[:], b.pb[7][:, 0:TCW]),
                 reads=("pb7",), writes=("wbc",), tag=f"wbc_ev_{tc}_{e}")

            for hg in range(NHG):
                par = hg % 2
                w1bs = w1b_d[e].rearrange("(c p) h -> p c h", p=128)

                def ldw1(q, e=e, hg=hg, par=par, w1bs=w1bs):
                    return q.dma_start(
                        out=w1s[par].ap().rearrange("p (c h) -> p c h", c=NC),
                        in_=w1bs[:, :, hg * 512:(hg + 1) * 512])

                p.op("ld", ldw1, reads=("w1b_dram",), writes=(f"w1s{par}",), tag=f"w1ld_{tc}_{e}_{hg}")
                for hh in range(4):
                    hc = hg * 4 + hh
                    bank = b.pb[2 + (hc % 2)]
                    bkk = f"pb{2 + (hc % 2)}"

                    def mmw1(q, par=par, hh=hh, bank=bank):
                        ins = None
                        for c in range(NC):
                            ins = q.matmul(
                                bank[:, 0:TCW],
                                w1s[par].ap()[:, c * 512 + hh * 128: c * 512 + hh * 128 + 128],
                                xT.ap()[:, c * TCW:(c + 1) * TCW],
                                start=(c == 0), stop=(c == NC - 1))
                        return ins

                    p.op("mm", mmw1, reads=(f"w1s{par}", "xT"), writes=(bkk,),
                         tag=f"w1mm_{tc}_{e}_{hc}")

                    def evg(q, e=e, hc=hc, bank=bank):
                        return q.activation(hg_t[:, 0:TCW], bank[:, 0:TCW],
                                            AF.Gelu,
                                            bias=b1c[:, e * 32 + hc: e * 32 + hc + 1],
                                            scale=1.0)

                    p.op("act", evg, reads=(bkk, "b1c"), writes=("hg_t",),
                         tag=f"w1gelu_{tc}_{e}_{hc}")

                    def scl(q, hc=hc):
                        return q.tensor_mul(
                            hs.ap()[:, hc * TCW:(hc + 1) * TCW],
                            hg_t[:, 0:TCW], wbc[:, 0:TCW])

                    p.op("dve", scl, reads=("hg_t", "wbc"), writes=("hs",),
                         tag=f"w1scl_{tc}_{e}_{hc}")

            w2bs = w2b_d[e].rearrange("(s p) d -> p s d", p=128)
            for hq in range(NH2 // 4):
                par = hq % 2

                def ldw2(q, hq=hq, par=par, w2bs=w2bs):
                    return q.dma_start(
                        out=w2s[par].ap().rearrange("p (s d) -> p s d", s=4),
                        in_=w2bs[:, 4 * hq:4 * hq + 4, :])

                p.op("ld", ldw2, reads=("w2b_dram",), writes=(f"w2s{par}",), tag=f"w2ld_{tc}_{e}_{hq}")

                def mmw2(q, e=e, hq=hq, par=par):
                    ins = None
                    for s in range(4):
                        hc = 4 * hq + s
                        first = (hc == 0)
                        for dc in range(NC):
                            if e == 0 and first:
                                q.matmul(b.pb[dc][:, 0:TCW],
                                         b2_row.ap()[:, dc * 128:(dc + 1) * 128],
                                         wT_b[:], start=True, stop=False)
                            ins = q.matmul(
                                b.pb[dc][:, 0:TCW],
                                w2s[par].ap()[:, s * D + dc * 128: s * D + (dc + 1) * 128],
                                hs.ap()[:, hc * TCW:(hc + 1) * TCW],
                                start=(first and e != 0), stop=(hc == NH2 - 1))
                    return ins

                p.op("mm", mmw2,
                     reads=(f"w2s{par}", "hs", "b2_row", "wT_b"),
                     writes=tuple(f"pb{dc}" for dc in range(NC)),
                     tag=f"w2mm_{tc}_{e}_{hq}")

            for dc in range(NC):
                if e == 0:
                    def ev0(q, dc=dc):
                        return q.tensor_copy(
                            acc.ap()[:, dc * TCW:(dc + 1) * TCW],
                            b.pb[dc][:, 0:TCW])

                    p.op("dve", ev0, reads=(f"pb{dc}",), writes=("acc",),
                         tag=f"w2ev_{tc}_{e}_{dc}")
                else:
                    def eva(q, dc=dc):
                        return q.scalar_tensor_tensor(
                            acc.ap()[:, dc * TCW:(dc + 1) * TCW],
                            b.pb[dc][:, 0:TCW], 0.0,
                            acc.ap()[:, dc * TCW:(dc + 1) * TCW],
                            op0=ALU.bypass, op1=ALU.add)

                    p.op("dve", eva, reads=(f"pb{dc}", "acc"), writes=("acc",),
                         tag=f"w2ev_{tc}_{e}_{dc}")

        for j in range(NTT):
            for half in range(2):
                bank = b.pb[4 + half]
                bkk = f"pb{4 + half}"

                def trb(q, j=j, half=half, bank=bank):
                    ins = None
                    for cc in range(4):
                        c = half * 4 + cc
                        ins = q.transpose(
                            bank[:, cc * 128:(cc + 1) * 128],
                            acc.ap()[:, c * TCW + j * 128: c * TCW + j * 128 + 128],
                            b.ident[:])
                    return ins

                p.op("mm", trb, reads=("acc", "ident"), writes=(bkk,),
                     tag=f"tb_tr_{tc}_{j}_{half}")

                def evs(q, j=j, half=half, bank=bank):
                    return q.scalar_tensor_tensor(
                        s_t[:, half * 512:(half + 1) * 512],
                        bank[:, 0:512], 0.0,
                        xres[j].ap()[:, half * 512:(half + 1) * 512],
                        op0=ALU.bypass, op1=ALU.add)

                p.op("dve", evs, reads=(bkk, f"xres{j}"), writes=("s_t",),
                     tag=f"tb_res_{tc}_{j}_{half}")
            t0g = tc * TCW + j * 128
            ln_tail(b, ln, s_t, "s_t", y_d[t0g:t0g + 128, :], f"{tc}_{j}")




def build_l0_ffn(cfg):
    T, B = cfg["T"], cfg["B"]
    HF, E = 4096, 8
    TOK = T * B
    TCW = min(512, TOK)
    NTC = TOK // TCW
    NTT = TCW // 128
    NHG = HF // 512
    NH2 = HF // 128

    nc = bass.Bass("TRN2", target_bir_lowering=False)
    x_d = nc.dram_tensor("x", [TOK, D], F32, kind="ExternalInput").ap()
    w1_d = nc.dram_tensor("w1", [E, D, HF], F32, kind="ExternalInput").ap()
    b1_d = nc.dram_tensor("b1", [E, HF], F32, kind="ExternalInput").ap()
    w2_d = nc.dram_tensor("w2", [E, HF, D], F32, kind="ExternalInput").ap()
    b2_d = nc.dram_tensor("b2", [E, D], F32, kind="ExternalInput").ap()
    wr_d = nc.dram_tensor("wr", [D, E], F32, kind="ExternalInput").ap()
    br_d = nc.dram_tensor("br", [E], F32, kind="ExternalInput").ap()
    y_d = nc.dram_tensor("y", [TOK, D], F32, kind="ExternalOutput").ap()
    w1b_d = nc.dram_tensor("w1b", [E, D, HF], BF16, kind="Internal").ap()
    w2b_d = nc.dram_tensor("w2b", [E, HF, D], BF16, kind="Internal").ap()

    DR = dict(x_d=x_d, w1_d=w1_d, b1_d=b1_d, w2_d=w2_d, b2_d=b2_d, wr_d=wr_d, br_d=br_d, y_d=y_d, w1b_d=w1b_d, w2b_d=w2b_d)
    b = Blk(nc, cfg)
    with contextlib.ExitStack() as es:
        b.es = es
        common_prep(b)
        emit_l0_ffn(b, cfg, DR)
        b.p.emit()
    return nc


def prep_wts_dram(b, w_list, wts_d, wtl, wtc):
    p = b.p
    for wi, (nm, wd) in enumerate(w_list):
        for c in range(NC):
            for half in range(2):
                def ldt(q, wd=wd, c=c, half=half):
                    seg = wd[half * 512:(half + 1) * 512, c * 128:(c + 1) * 128]
                    return q.dma_start(
                        out=wtl.ap().rearrange("p (r d) -> p r d", r=4),
                        in_=seg.rearrange("(r p) d -> p r d", p=128))

                p.op("ld", ldt, writes=("wtl",), tag=f"wt_ld_{nm}_{c}_{half}")
                bank = b.pb[6]

                def trt(q, bank=bank):
                    ins = None
                    for j in range(4):
                        ins = q.transpose(bank[:, j * 128:(j + 1) * 128],
                                          wtl[:, j * 128:(j + 1) * 128],
                                          b.ident[:])
                    return ins

                p.op("mm", trt, reads=("wtl", "ident"), writes=("pb6",),
                     tag=f"wt_tr_{nm}_{c}_{half}")

                def evt(q, half=half, bank=bank):
                    return q.tensor_copy(
                        wtc[:, half * 512:(half + 1) * 512], bank[:, 0:512])

                p.op("dve", evt, reads=("pb6",), writes=("wtc",),
                     tag=f"wt_ev_{nm}_{c}_{half}")

            def sw(q, wi=wi, c=c):
                return q.dma_start(out=wts_d[wi, c], in_=wtc[:])

            p.op("pd", sw, reads=("wtc",), writes=("wts_dram",),
                 sem="st_wts", tag=f"wt_st_{nm}_{c}")


def load_wft(b, wft, wts_d, wi, tag):
    p = b.p

    def ldwf(q, wi=wi):
        return q.dma_start(
            out=wft.ap().rearrange("p (c d) -> p c d", c=NC),
            in_=wts_d[wi].rearrange("c p d -> p c d"))

    p.op("ld", ldwf, reads=("wts_dram",), writes=("wft",), tag=tag)



def build_wt(b, w_dram, name, banks=(6, 7)):
    p = b.p
    wt = b.sbuf(name, [128, NC * D], BF16)
    wsl = b.sbuf(name + "_sl", [128, D], F32)
    for rc in range(NC):
        def ldw(q, rc=rc):
            return q.dma_start(out=wsl[:], in_=w_dram[rc * 128:(rc + 1) * 128, :])

        p.op("ld", ldw, writes=(name + "_sl",), tag=f"{name}_ld{rc}")
        for half in range(2):
            bank = b.pb[banks[half]]
            bkk = f"pb{banks[half]}"

            def trs(q, half=half, bank=bank):
                ins = None
                for j in range(4):
                    c = half * 4 + j
                    ins = q.transpose(bank[:, j * 128:(j + 1) * 128],
                                      wsl[:, c * 128:(c + 1) * 128], b.ident[:])
                return ins

            p.op("mm", trs, reads=(name + "_sl", "ident"), writes=(bkk,),
                 tag=f"{name}_tr{rc}_{half}")

            def evw(q, rc=rc, half=half, bank=bank):
                ins = None
                for j in range(4):
                    c = half * 4 + j
                    ins = q.tensor_copy(
                        wt.ap()[:, c * D + rc * 128: c * D + rc * 128 + 128],
                        bank[:, j * 128:(j + 1) * 128])
                return ins

            p.op("dve", evw, reads=(bkk,), writes=(name,),
                 tag=f"{name}_ev{rc}_{half}")
    return wt



def emit_l0_attn(b, cfg, DR):
    nc = b.nc
    p = b.p
    T, B = cfg["T"], cfg["B"]
    H, HD = 16, 64
    NT = T // 128
    KCN = T // 512 if T >= 512 else 1
    KCW = min(512, T)
    SW = 2 * T - 128

    x_d = DR["x_d"]
    wq_d = DR["wq_d"]
    wk_d = DR["wk_d"]
    wv_d = DR["wv_d"]
    wo_d = DR["wo_d"]
    bq_d = DR["bq_d"]
    bk_d = DR["bk_d"]
    bv_d = DR["bv_d"]
    bo_d = DR["bo_d"]
    al_d = DR["al_d"]
    y_d = DR["y_d"]
    wts_d = DR["wts_d"]

    wtl = b.sbuf("wtl", [128, 512], F32)
    wtc = b.sbuf("wtc", [128, D], BF16)
    prep_wts_dram(b, [("wq", wq_d), ("wk", wk_d), ("wv", wv_d),
                      ("wo", wo_d)], wts_d, wtl, wtc)
    wft = b.sbuf("wft", [128, NC * D], BF16)

    bqc = b.sbuf("bqc", [128, NC], F32)
    bkc = b.sbuf("bkc", [128, NC], F32)
    load_cols(b, bqc, "bqc", bq_d, NC)
    load_cols(b, bkc, "bkc", bk_d, NC)
    bvf = b.sbuf("bvf", [1, D], F32)
    bof = b.sbuf("bof", [1, D], F32)
    bvr = b.sbuf("bvr", [1, D], BF16)
    bor = b.sbuf("bor", [1, D], BF16)
    p.op("ld", lambda q: q.dma_start(out=bvf[:], in_=bv_d[None, :]),
         writes=("bvf",), tag="ld_bv")
    p.op("ld", lambda q: q.dma_start(out=bof[:], in_=bo_d[None, :]),
         writes=("bof",), tag="ld_bo")
    p.op("dve", lambda q: q.tensor_copy(bvr[:], bvf[:]), reads=("bvf",),
         writes=("bvr",), tag="cv_bv")
    p.op("dve", lambda q: q.tensor_copy(bor[:], bof[:]), reads=("bof",),
         writes=("bor",), tag="cv_bo")

    a1 = b.sbuf("a1", [1, 2], F32R)
    nal = b.sbuf("nal", [128, 1], F32)
    for cc in range(2):
        p.op("ld", lambda q, cc=cc: q.dma_start(
            out=a1[:, cc:cc + 1], in_=al_d[None, :].bitcast(F32R)),
            writes=("a1",), sem="dma_a1", no_waw=True, tag=f"ld_alpha{cc}")
    p.op("mm", lambda q: q.matmul(b.pb[5][:, 0:2], b.ones_r[:], a1[:],
                                  start=True, stop=True),
         reads=("a1", "ones_r"), writes=("pb5",), tag="al_bc")
    p.op("dve", lambda q: q.tensor_scalar_mul(nal[:], b.pb[5][:, 0:1], -1.0),
         reads=("pb5",), writes=("nal",), tag="al_neg")
    strip = b.sbuf("strip", [128, SW], BF16)
    lnc = b.sbuf("lnc", [128, 1], F32)
    p.op("pool", lambda q: q.memset(lnc[:], float(np.log(1.0 / 8.0))),
         writes=("lnc",), tag="lnc")
    p.op("pool", lambda q: q.iota(strip[:], pattern=[[-1, SW]], base=T - 128,
                                  channel_multiplier=1,
                                  allow_small_or_imprecise_dtypes=True),
         writes=("strip",), tag="strip_iota")
    p.op("act", lambda q: q.activation(strip[:], strip[:], AF.Abs, bias=0.0,
                                       scale=1.0),
         reads=("strip",), writes=("strip",), tag="strip_abs")
    p.op("act", lambda q: q.activation(strip[:], strip[:], AF.Exp,
                                       bias=lnc[:, 0:1],
                                       scale=nal[:, 0:1]),
         reads=("strip", "nal", "lnc"), writes=("strip",), tag="strip_exp")

    xT = b.sbuf("xT", [128, NC * T], BF16)
    qT = b.sbuf("qT", [128, NC * T], BF16)
    kT = b.sbuf("kT", [128, NC * T], BF16)
    vtok = b.sbuf("vtok", [128, NT * D], BF16)
    xres = [b.sbuf("xres", [128, D], F32) for _ in range(2)]
    S_sb = [b.sbuf("S_sb", [128, max(T, 2048)], F32) for _ in range(2)]
    mx4 = [b.sbuf("mx4", [128, 4], F32) for _ in range(2)]
    ast = [b.sbuf("ast", [128, 4], F32) for _ in range(4)]
    P_sb = [b.sbuf("P_sb", [128, T], BF16) for _ in range(2)]
    PT_sb = [xres[0].ap().bitcast(BF16)[:, 0:T], xres[1].ap().bitcast(BF16)[:, 0:T]]
    s_t = b.sbuf("s_t", [128, D], F32)
    ln = LNScratch(b, cen=S_sb[0].ap()[:, 0:1024], sq=S_sb[1].ap()[:, 0:1024],
                   cen_key="S0", sq_key="S1")

    for s in range(B):
        xs = x_d[s]
        ys = y_d[s]
        for tt in range(NT):
            tin(b, xs[tt * 128:(tt + 1) * 128, :], xres[tt % 2],
                f"xres{tt % 2}", [(xT, "xT", T, tt * 128)], tt, banks=(6, 7))
        for (wi, bc, bck, dst, dk) in ((0, bqc, "bqc", qT, "qT"),
                                       (1, bkc, "bkc", kT, "kT")):
            load_wft(b, wft, wts_d, wi, f"ldw_{dk}_{s}")
            for ic in range(NC):
                for tq in range(KCN):
                    bank = b.pb[tq % 2]
                    bkk = f"pb{tq % 2}"

                    def mmq(q, ic=ic, tq=tq, bank=bank):
                        ins = None
                        for c in range(NC):
                            ins = q.matmul(
                                bank[:, 0:KCW],
                                wft.ap()[:, c * D + ic * 128: c * D + ic * 128 + 128],
                                xT.ap()[:, c * T + tq * KCW: c * T + tq * KCW + KCW],
                                start=(c == 0), stop=(c == NC - 1))
                        return ins

                    p.op("mm", mmq, reads=("wft", "xT"), writes=(bkk,),
                         tag=f"pj_{dk}_{s}_{ic}_{tq}")

                    def evq(q, bc=bc, ic=ic, tq=tq, dst=dst, bank=bank):
                        return q.activation(
                            dst.ap()[:, ic * T + tq * KCW: ic * T + tq * KCW + KCW],
                            bank[:, 0:KCW], AF.Identity,
                            bias=bc[:, ic:ic + 1], scale=1.0)

                    p.op("act", evq, reads=(bkk, bck), writes=(dk,),
                         tag=f"pjv_{dk}_{s}_{ic}_{tq}")
        load_wft(b, wft, wts_d, 2, f"ldw_v_{s}")
        for tt in range(NT):
            for og in range(2):
                bank = b.pb[2 + og]
                bkk = f"pb{2 + og}"

                def mmv(q, tt=tt, og=og, bank=bank):
                    q.matmul(bank[:, 0:512], b.ones_b[:],
                             bvr[:, og * 512:(og + 1) * 512],
                             start=True, stop=False)
                    ins = None
                    for c in range(NC):
                        ins = q.matmul(
                            bank[:, 0:512],
                            xT.ap()[:, c * T + tt * 128: c * T + tt * 128 + 128],
                            wft.ap()[:, c * D + og * 512: c * D + og * 512 + 512],
                            start=False, stop=(c == NC - 1))
                    return ins

                p.op("mm", mmv, reads=("xT", "wft", "bvr", "ones_b"),
                     writes=(bkk,), tag=f"pv_{s}_{tt}_{og}")

                def evv(q, tt=tt, og=og, bank=bank):
                    return q.tensor_copy(
                        vtok.ap()[:, tt * D + og * 512: tt * D + og * 512 + 512],
                        bank[:, 0:512])

                p.op("dve", evv, reads=(bkk,), writes=("vtok",),
                     tag=f"pvv_{s}_{tt}_{og}")

        iters = [(h, qt) for h in range(H) for qt in range(NT)]

        def stage_a_mm(i):
            h, qt = iters[i]
            po = (h % 2) * 64
            hc = h // 2

            def mms(q, po=po, hc=hc, qt=qt):
                ins = None
                for kc in range(KCN):
                    ins = q.matmul(
                        b.pb[kc][:, 0:KCW],
                        qT.ap()[po:po + 64, hc * T + qt * 128: hc * T + qt * 128 + 128],
                        kT.ap()[po:po + 64, hc * T + kc * KCW: hc * T + kc * KCW + KCW],
                        start=True, stop=True)
                return ins

            p.op("mm", mms, reads=("qT", "kT"),
                 writes=tuple(f"pb{kc}" for kc in range(KCN)),
                 tag=f"at_s_{s}_{h}_{qt}")

        def stage_a_rest(i):
            h, qt = iters[i]
            pr = i % 2
            pa = i % 4
            Sk, Ak = f"S{pr}", f"ast{pa}"
            Ssb, Asb = S_sb[pr].ap(), ast[pa]
            base_v = (T - 128) - qt * 128

            def ttr(q, Ssb=Ssb, base_v=base_v):
                return q.tensor_tensor(
                    Ssb[:, 0:T], b.pbS.ap()[:, 0:T],
                    strip[:, base_v: base_v + T], ALU.mult)

            p.op("dve", ttr,
                 reads=tuple(f"pb{kc}" for kc in range(KCN)) + ("strip",),
                 writes=(Sk,), tag=f"at_ttr_{s}_{h}_{qt}")

            def mxr(q, Ssb=Ssb, Asb=Asb):
                sv = Ssb[:, 0:T].rearrange("p (a e) -> p a e", e=8)[:, :, 0:1]
                sv = sv.transpose([0, 2, 1])
                return q.reduce_max(Asb[:, 0:1], sv, axis=AX.X)

            p.op("dve", mxr, reads=(Sk,), writes=(Ak,),
                 tag=f"at_m_{s}_{h}_{qt}")

            def ngm(q, Asb=Asb):
                return q.tensor_scalar_mul(Asb[:, 1:2], Asb[:, 0:1], -1.0)

            p.op("dve", ngm, reads=(Ak,), writes=(Ak,),
                 tag=f"at_nm_{s}_{h}_{qt}")

        def stage_b(i):
            h, qt = iters[i]
            pr = i % 2
            pa = i % 4
            Sk, Pk, Ak = f"S{pr}", f"P{pr}", f"ast{pa}"
            Ssb, Psb, Asb = S_sb[pr].ap(), P_sb[pr].ap(), ast[pa]

            def exp(q, Ssb=Ssb, Psb=Psb, Asb=Asb):
                return q.activation(Psb[:], Ssb[:, 0:T], AF.Exp,
                                    bias=Asb[:, 1:2], scale=1.0,
                                    accum_out=Asb[:, 2:3])

            p.op("act", exp, reads=(Sk, Ak), writes=(Pk, Ak),
                 tag=f"at_e_{s}_{h}_{qt}")

        def stage_c(i):
            h, qt = iters[i]
            pr = i % 2
            pa = i % 4
            Pk, Ak = f"P{pr}", f"ast{pa}"
            Psb = P_sb[pr].ap()
            Asb = ast[pa]

            def rcz(q, Asb=Asb):
                return q.reciprocal(Asb[:, 3:4], Asb[:, 2:3])

            p.op("dve", rcz, reads=(Ak,), writes=(Ak,),
                 tag=f"at_rz_{s}_{h}_{qt}")

            def pnm(q, Psb=Psb, Asb=Asb):
                return q.tensor_scalar_mul(Psb[:], Psb[:], Asb[:, 3:4])

            p.op("dve", pnm, reads=(Pk, Ak), writes=(Pk,),
                 tag=f"at_pn_{s}_{h}_{qt}")

        def stage_d(i):
            h, qt = iters[i]
            pr = i % 2
            po = (h % 2) * 64
            hc = h // 2
            Pk, Tk = f"P{pr}", f"xres{pr}"
            Psb, Tsb = P_sb[pr].ap(), PT_sb[pr]
            nhalf = (NT + 7) // 8
            for half in range(nhalf):
                bank = b.pb[4 + half % 2]
                bkk = f"pb{4 + half % 2}"
                bview = bank.bitcast(BF16)
                ntl = min(8, NT - half * 8)

                def trp(q, half=half, bview=bview, ntl=ntl, Psb=Psb):
                    ins = None
                    for j in range(ntl):
                        kt = half * 8 + j
                        ins = q.transpose(
                            bview[:, j * 128:(j + 1) * 128],
                            Psb[:, kt * 128:(kt + 1) * 128], b.identb[:])
                    return ins

                p.op("mm", trp, reads=(Pk, "identb"), writes=(bkk,),
                     tag=f"at_tp_{s}_{h}_{qt}_{half}")

                if half == 0:
                    def evp(q, half=half, bview=bview, ntl=ntl, Tsb=Tsb):
                        return q.tensor_copy(
                            Tsb[:, half * 1024: half * 1024 + ntl * 128],
                            bview[:, 0:ntl * 128])

                    p.op("dve", evp, reads=(bkk,), writes=(Tk,),
                         tag=f"at_ep_{s}_{h}_{qt}_{half}")
                else:
                    def evp(q, half=half, bview=bview, ntl=ntl, Tsb=Tsb):
                        return q.copy(
                            Tsb[:, half * 1024: half * 1024 + ntl * 128],
                            bview[:, 0:ntl * 128])

                    p.op("act", evp, reads=(bkk,), writes=(Tk,),
                         tag=f"at_ep_{s}_{h}_{qt}_{half}")

            def pv(q, h=h, Tsb=Tsb):
                ins = None
                for kt in range(NT):
                    ins = q.matmul(
                        b.pb[6][0:64, 0:128],
                        vtok.ap()[:, kt * D + h * 64: kt * D + h * 64 + 64],
                        Tsb[:, kt * 128:(kt + 1) * 128],
                        start=(kt == 0), stop=(kt == NT - 1))
                return ins

            p.op("mm", pv, reads=(Tk, "vtok"), writes=("pb6",),
                 tag=f"at_pv_{s}_{h}_{qt}")

            def evo(q, po=po, hc=hc, qt=qt):
                return q.copy(
                    xT.ap()[po:po + 64, hc * T + qt * 128: hc * T + qt * 128 + 128],
                    b.pb[6][0:64, 0:128])

            p.op("act", evo, reads=("pb6",), writes=("xT",),
                 tag=f"at_o_{s}_{h}_{qt}")

        NI = len(iters)
        for idx in range(NI + 3):
            if idx < NI:
                stage_a_mm(idx)
            if idx >= 3:
                stage_d(idx - 3)
            if 2 <= idx < NI + 2:
                stage_c(idx - 2)
            if 1 <= idx < NI + 1:
                stage_b(idx - 1)
            if idx < NI:
                stage_a_rest(idx)

        load_wft(b, wft, wts_d, 3, f"ldw_o_{s}")
        for tt in range(NT):
            def ldr(q, tt=tt, xs=xs):
                return q.dma_start(out=xres[tt % 2][:],
                                   in_=xs[tt * 128:(tt + 1) * 128, :])

            p.op("pd", ldr, writes=(f"xres{tt % 2}",), tag=f"or_ld_{s}_{tt}")
            for og in range(2):
                bank = b.pb[2 + og]
                bkk = f"pb{2 + og}"

                def mmo(q, tt=tt, og=og, bank=bank):
                    q.matmul(bank[:, 0:512], b.ones_b[:],
                             bor[:, og * 512:(og + 1) * 512],
                             start=True, stop=False)
                    ins = None
                    for c in range(NC):
                        ins = q.matmul(
                            bank[:, 0:512],
                            xT.ap()[:, c * T + tt * 128: c * T + tt * 128 + 128],
                            wft.ap()[:, c * D + og * 512: c * D + og * 512 + 512],
                            start=False, stop=(c == NC - 1))
                    return ins

                p.op("mm", mmo, reads=("xT", "wft", "bor", "ones_b"),
                     writes=(bkk,), tag=f"or_mm_{s}_{tt}_{og}")

                def evr(q, tt=tt, og=og, bank=bank):
                    return q.scalar_tensor_tensor(
                        s_t[:, og * 512:(og + 1) * 512], bank[:, 0:512],
                        0.0, xres[tt % 2].ap()[:, og * 512:(og + 1) * 512],
                        op0=ALU.bypass, op1=ALU.add)

                p.op("dve", evr, reads=(bkk, f"xres{tt % 2}"), writes=("s_t",),
                     tag=f"or_rs_{s}_{tt}_{og}")
            ln_tail(b, ln, s_t, "s_t", ys[tt * 128:(tt + 1) * 128, :],
                    f"l0a_{s}_{tt}")




def build_l0_attn(cfg):
    T, B = cfg["T"], cfg["B"]
    H, HD = 16, 64
    NT = T // 128
    KCN = T // 512 if T >= 512 else 1
    KCW = min(512, T)
    SW = 2 * T - 128

    nc = bass.Bass("TRN2", target_bir_lowering=False)
    x_d = nc.dram_tensor("x", [B, T, D], F32, kind="ExternalInput").ap()
    wq_d = nc.dram_tensor("wq", [D, D], F32, kind="ExternalInput").ap()
    wk_d = nc.dram_tensor("wk", [D, D], F32, kind="ExternalInput").ap()
    wv_d = nc.dram_tensor("wv", [D, D], F32, kind="ExternalInput").ap()
    wo_d = nc.dram_tensor("wo", [D, D], F32, kind="ExternalInput").ap()
    bq_d = nc.dram_tensor("bq", [D], F32, kind="ExternalInput").ap()
    bk_d = nc.dram_tensor("bk", [D], F32, kind="ExternalInput").ap()
    bv_d = nc.dram_tensor("bv", [D], F32, kind="ExternalInput").ap()
    bo_d = nc.dram_tensor("bo", [D], F32, kind="ExternalInput").ap()
    al_d = nc.dram_tensor("alpha", [1], F32, kind="ExternalInput").ap()
    y_d = nc.dram_tensor("y", [B, T, D], F32, kind="ExternalOutput").ap()
    wts_d = nc.dram_tensor("wts", [4, NC, 128, D], BF16, kind="Internal").ap()

    DR = dict(x_d=x_d, wq_d=wq_d, wk_d=wk_d, wv_d=wv_d, wo_d=wo_d, bq_d=bq_d, bk_d=bk_d, bv_d=bv_d, bo_d=bo_d, al_d=al_d, y_d=y_d, wts_d=wts_d)
    b = Blk(nc, cfg)
    with contextlib.ExitStack() as es:
        b.es = es
        common_prep(b)
        emit_l0_attn(b, cfg, DR)
        b.p.emit()
    return nc



def emit_l1_attn(b, cfg, DR):
    nc = b.nc
    p = b.p
    WI = {"wr": 0, "wk": 1, "wv": 2, "wg": 3, "wo": 4}
    T, B = cfg["T"], cfg["B"]
    H, HD = 16, 64
    NT = T // 128
    NTC = T // 512 if T >= 512 else 1
    TCW = min(512, T)
    NQ = NT // 4

    x_d = DR["x_d"]
    dn = DR["dn"]
    tw1_d = DR["tw1_d"]
    tw2_d = DR["tw2_d"]
    dw1_d = DR["dw1_d"]
    dw2_d = DR["dw2_d"]
    wX_d = DR["wX_d"]
    y_d = DR["y_d"]
    wts_d = DR["wts_d"]
    ksp_d = DR["ksp_d"]
    rsp_d = DR["rsp_d"]
    gsp_d = DR["gsp_d"]

    wtl = b.sbuf("wtl", [128, 512], F32)
    wtc = b.sbuf("wtc", [128, D], BF16)
    prep_wts_dram(b, [(nm, wX_d[nm]) for nm in
                      ("wr", "wk", "wv", "wg", "wo")], wts_d, wtl, wtc)

    mcols = {}
    for nm in ("tmx", "tmw", "tmk", "tmv", "tmr", "tmg", "tdec"):
        t = b.sbuf(nm + "_c", [128, NC], F32)
        load_cols(b, t, nm + "_c", dn[nm], NC)
        mcols[nm] = t
    fcol = [mcols[n] for n in ("tmw", "tmk", "tmv", "tmr", "tmg")]
    fkey = [n + "_c" for n in ("tmw", "tmk", "tmv", "tmr", "tmg")]

    tw1_sb = b.sbuf("tw1_sb", [128, NC * 160], BF16)
    p.op("pd", lambda q: q.dma_start(
        out=tw1_sb.ap().rearrange("p (c m) -> p c m", c=NC),
        in_=tw1_d.rearrange("(c p) m -> p c m", p=128)),
        writes=("tw1_sb",), tag="ld_tw1")
    tw2_sb = b.sbuf("tw2_sb", [64, D], BF16)
    dw1_sb = b.sbuf("dw1_sb", [128, NC * 64], BF16)
    p.op("pd", lambda q: q.dma_start(
        out=dw1_sb.ap().rearrange("p (c m) -> p c m", c=NC),
        in_=dw1_d.rearrange("(c p) m -> p c m", p=128)),
        writes=("dw1_sb",), tag="ld_dw1")
    dw2_sb = b.sbuf("dw2_sb", [64, D], BF16)
    p.op("pd", lambda q: q.dma_start(out=dw2_sb[:], in_=dw2_d),
         writes=("dw2_sb",), tag="ld_dw2")

    lnw_r = b.sbuf("lnw_r", [1, D], F32R)
    lnb_r = b.sbuf("lnb_r", [1, D], F32R)
    p.op("ld", lambda q: q.dma_start(out=lnw_r[:],
                                     in_=dn["lnw"][None, :].bitcast(F32R)),
         writes=("lnw_r",), tag="ld_lnw")
    p.op("ld", lambda q: q.dma_start(out=lnb_r[:],
                                     in_=dn["lnb"][None, :].bitcast(F32R)),
         writes=("lnb_r",), tag="ld_lnb")
    lnw_bc = b.sbuf("lnw_bc", [128, D], F32)
    lnb_bc = b.sbuf("lnb_bc", [128, D], F32)
    for (row, rk, dst, dk) in ((lnw_r, "lnw_r", lnw_bc, "lnw_bc"),
                               (lnb_r, "lnb_r", lnb_bc, "lnb_bc")):
        for half in range(2):
            def mmbc(q, row=row, half=half):
                return q.matmul(b.pb[5][:, 0:512], b.ones_r[:],
                                row[:, half * 512:(half + 1) * 512],
                                start=True, stop=True)

            p.op("mm", mmbc, reads=(rk, "ones_r"), writes=("pb5",),
                 tag=f"bc_{dk}_{half}")

            def evbc(q, dst=dst, half=half):
                return q.tensor_copy(dst[:, half * 512:(half + 1) * 512],
                                     b.pb[5][:, 0:512])

            p.op("dve", evbc, reads=("pb5",), writes=(dk,),
                 tag=f"bcv_{dk}_{half}")

    xT = b.sbuf("xT", [128, NC * T], BF16)
    ar_a = b.sbuf("ar_a", [128, 2048], F32)
    ar_b = b.sbuf("ar_b", [128, 4096], F32)
    ar_c = b.sbuf("ar_c", [128, 4096], BF16)
    wft = b.sbuf("wft", [128, NC * D], BF16)
    ktile = b.sbuf("ktile", [128, NC * TCW], BF16)
    vtok = b.sbuf("vtok", [128, NT * D], BF16)
    ytok = b.sbuf("ytok", [128, NT * D], BF16)
    tw_sb = b.sbuf("tw_sb", [64, T], BF16)
    xx01 = b.sbuf("xx01", [64, T], BF16)
    xx23 = b.sbuf("xx23", [64, T], BF16)
    xx4 = b.sbuf("xx4", [32, T], BF16)
    xres_1 = b.sbuf("xres", [128, D], F32)
    xres = [xres_1, xres_1]
    gq = b.sbuf("gq", [128, D], BF16)
    gns = b.sbuf("gns", [128, 80], F32)
    s_t = ar_a.ap()[:, 0:1024]
    ln = LNScratch(b, cen=ar_b.ap()[:, 2048:3072], sq=ar_b.ap()[:, 3072:4096],
                   cen_key="ar_b", sq_key="ar_b")

    xmxb = ar_a.ap().bitcast(BF16)
    xbtb = ar_b.ap().bitcast(BF16)
    xTa = xT.ap()
    v_ef = xTa[:, 0 * T:1 * T]
    v_emf = xTa[:, 1 * T:2 * T]
    v_eb = xTa[:, 2 * T:3 * T]
    v_emb = xTa[:, 3 * T:4 * T]
    v_kp = xTa[:, 4 * T:5 * T]
    v_rp = xTa[:, 5 * T:6 * T]
    v_rxf = xTa[:, 6 * T:7 * T]
    v_kxf = xTa[:, 7 * T:8 * T]
    ktb = ktile.ap()
    v_rxb = ktb[:, 0:T]
    v_kxb = ktb[:, (NC * TCW - T):NC * TCW]
    S1 = ar_a.ap()[:, 0:T]
    S2 = ar_b.ap()[:, 0:T]
    S3 = ar_b.ap()[:, 2048:2048 + T]

    for s in range(B):
        xs = x_d[s]
        ys = y_d[s]
        for tt in range(NT):
            tin(b, xs[tt * 128:(tt + 1) * 128, :], xres[tt % 2],
                "xres0", [(xT, "xT", T, tt * 128)], tt, banks=(6, 7))

        for tc in range(NTC):
            t0 = tc * TCW
            xTv = xT.ap().rearrange("p (c w) -> p c w", c=NC)
            dxv = ar_c.ap().rearrange("p (c w) -> p c w", c=NC)
            xmv = xmxb[:, 0:NC * TCW].rearrange("p (c w) -> p c w", c=NC)
            xbv = xbtb[:, 0:NC * TCW].rearrange("p (c w) -> p c w", c=NC)

            lo = 1 if tc == 0 else 0
            hi = TCW - 1 if tc == NTC - 1 else TCW

            def dxu(q, t0=t0, lo=lo, hi=hi):
                return q.tensor_add(dxv[:, :, lo:hi],
                                    xTv[:, :, t0 + lo - 1: t0 + hi - 1],
                                    xTv[:, :, t0 + lo + 1: t0 + hi + 1])

            p.op("dve", dxu, reads=("xT",), writes=("ar_c",),
                 tag=f"dxu_{s}_{tc}")

            def dxm(q, t0=t0, lo=lo, hi=hi):
                return q.scalar_tensor_tensor(
                    dxv[:, :, lo:hi], dxv[:, :, lo:hi], 0.5,
                    xTv[:, :, t0 + lo: t0 + hi],
                    op0=ALU.mult, op1=ALU.subtract)

            p.op("dve", dxm, reads=("ar_c", "xT"), writes=("ar_c",),
                 tag=f"dxm_{s}_{tc}")
            if tc == 0:
                p.op("dve", lambda q: q.scalar_tensor_tensor(
                    dxv[:, :, 0:1], xTv[:, :, 1:2], 0.5, xTv[:, :, 0:1],
                    op0=ALU.mult, op1=ALU.subtract),
                    reads=("xT",), writes=("ar_c",), tag=f"dx0_{s}")
            if tc == NTC - 1:
                p.op("dve", lambda q: q.scalar_tensor_tensor(
                    dxv[:, :, TCW - 1:TCW], xTv[:, :, T - 2:T - 1], 0.5,
                    xTv[:, :, T - 1:T], op0=ALU.mult, op1=ALU.subtract),
                    reads=("xT",), writes=("ar_c",), tag=f"dxl_{s}")

            tmxb = mcols["tmx"].ap().unsqueeze(2).broadcast_to([128, NC, TCW])
            p.op("dve", lambda q: q.tensor_tensor(
                xmv[:, :, :], dxv[:, :, 0:TCW], tmxb, ALU.mult),
                reads=("ar_c", "tmx_c"), writes=("ar_a",), tag=f"mx1_{s}_{tc}")
            p.op("dve", lambda q, t0=t0: q.tensor_add(
                xmv[:, :, :], xmv[:, :, :], xTv[:, :, t0:t0 + TCW]),
                reads=("ar_a", "xT"), writes=("ar_a",), tag=f"mx2_{s}_{tc}")

            def mmx1(q):
                ins = None
                for c in range(NC):
                    ins = q.matmul(b.pb[0][:, 0:TCW],
                                   tw1_sb.ap()[:, c * 160: c * 160 + 128],
                                   xmv[:, c, :],
                                   start=(c == 0), stop=(c == NC - 1))
                return ins

            p.op("mm", mmx1, reads=("tw1_sb", "ar_a"), writes=("pb0",),
                 tag=f"xx1_{s}_{tc}")

            def mmx2(q):
                ins = None
                for c in range(NC):
                    ins = q.matmul(b.pb[1][0:32, 0:TCW],
                                   tw1_sb.ap()[:, c * 160 + 128: c * 160 + 160],
                                   xmv[:, c, :],
                                   start=(c == 0), stop=(c == NC - 1))
                return ins

            p.op("mm", mmx2, reads=("tw1_sb", "ar_a"), writes=("pb1",),
                 tag=f"xx2_{s}_{tc}")
            p.op("act", lambda q, t0=t0: q.activation(
                xx01.ap()[:, t0:t0 + TCW], b.pb[0][0:64, 0:TCW], AF.Tanh,
                bias=0.0, scale=1.0),
                reads=("pb0",), writes=("xx01",), tag=f"xxe1_{s}_{tc}")
            p.op("act", lambda q, t0=t0: q.activation(
                xx23.ap()[:, t0:t0 + TCW], b.pb[0][64:128, 0:TCW], AF.Tanh,
                bias=0.0, scale=1.0),
                reads=("pb0",), writes=("xx23",), tag=f"xxe2_{s}_{tc}")
            p.op("act", lambda q, t0=t0: q.activation(
                xx4.ap()[:, t0:t0 + TCW], b.pb[1][0:32, 0:TCW], AF.Tanh,
                bias=0.0, scale=1.0),
                reads=("pb1",), writes=("xx4",), tag=f"xxe3_{s}_{tc}")

            for f in range(5):
                xxt, xxk, xoff = [(xx01, "xx01", 0), (xx01, "xx01", 32),
                                  (xx23, "xx23", 0), (xx23, "xx23", 32),
                                  (xx4, "xx4", 0)][f]
                for off in range(2):
                    p.op("pd", lambda q, off=off, f=f: q.dma_start(
                        out=tw2_sb.ap()[off * 32:(off + 1) * 32, :],
                        in_=tw2_d[f]),
                        writes=("tw2_sb",), sem="dma_tw2sb", no_waw=True,
                        tag=f"ld_tw2_{s}_{tc}_{f}_{off}")
                if f != 0:
                    wi = [None, WI["wk"], WI["wv"], WI["wr"], WI["wg"]][f]

                    def ldwf(q, wi=wi):
                        return q.dma_start(
                            out=wft.ap().rearrange("p (c d) -> p c d", c=NC),
                            in_=wts_d[wi].rearrange("c p d -> p c d"))

                    p.op("ld", ldwf, reads=("wts_dram",), writes=("wft",),
                         tag=f"ldwf_{s}_{tc}_{f}")

                for dc in range(NC):
                    bank = b.pb[2 + dc % 2]
                    bkk = f"pb{2 + dc % 2}"

                    def mmf(q, f=f, dc=dc, xxt=xxt, xoff=xoff, bank=bank,
                            t0=t0):
                        return q.matmul(
                            bank[:, 0:TCW],
                            tw2_sb.ap()[xoff:xoff + 32,
                                        dc * 128: dc * 128 + 128],
                            xxt.ap()[xoff:xoff + 32, t0:t0 + TCW],
                            start=True, stop=True)

                    p.op("mm", mmf, reads=("tw2_sb", xxk), writes=(bkk,),
                         tag=f"mf_{s}_{tc}_{f}_{dc}")

                    mcb = fcol[f]
                    mck = fkey[f]

                    def xb1(q, f=f, dc=dc, bank=bank, mcb=mcb):
                        return q.scalar_tensor_tensor(
                            xbv[:, dc, :], bank[:, 0:TCW],
                            mcb[:, dc:dc + 1], dxv[:, dc, 0:TCW],
                            op0=ALU.add, op1=ALU.mult)

                    p.op("dve", xb1, reads=(bkk, mck, "ar_c"),
                         writes=("ar_b",), tag=f"xb1_{s}_{tc}_{f}_{dc}")

                    def xb2(q, dc=dc, t0=t0):
                        return q.tensor_add(xbv[:, dc, :], xbv[:, dc, :],
                                            xTv[:, dc, t0:t0 + TCW])

                    p.op("dve", xb2, reads=("ar_b", "xT"), writes=("ar_b",),
                         tag=f"xb2_{s}_{tc}_{f}_{dc}")

                if f == 0:
                    def mmtw(q):
                        ins = None
                        for c in range(NC):
                            ins = q.matmul(
                                b.pb[4][0:64, 0:TCW],
                                dw1_sb.ap()[:, c * 64:(c + 1) * 64],
                                xbv[:, c, :],
                                start=(c == 0), stop=(c == NC - 1))
                        return ins

                    p.op("mm", mmtw, reads=("dw1_sb", "ar_b"), writes=("pb4",),
                         tag=f"tw_{s}_{tc}")
                    p.op("act", lambda q, t0=t0: q.activation(
                        tw_sb.ap()[:, t0:t0 + TCW], b.pb[4][0:64, 0:TCW],
                        AF.Tanh, bias=0.0, scale=1.0),
                        reads=("pb4",), writes=("tw_sb",), tag=f"twe_{s}_{tc}")
                elif f in (1, 3):
                    spd = ksp_d if f == 1 else rsp_d
                    for ic in range(NC):
                        bank = b.pb[4 + ic % 2]
                        bkk = f"pb{4 + ic % 2}"

                        def mmk(q, ic=ic, bank=bank):
                            ins = None
                            for c in range(NC):
                                ins = q.matmul(
                                    bank[:, 0:TCW],
                                    wft.ap()[:, c * D + ic * 128:
                                             c * D + ic * 128 + 128],
                                    xbv[:, c, :],
                                    start=(c == 0), stop=(c == NC - 1))
                            return ins

                        p.op("mm", mmk, reads=("wft", "ar_b"), writes=(bkk,),
                             tag=f"kp_{s}_{tc}_{f}_{ic}")

                        def evk(q, ic=ic, bank=bank):
                            return q.tensor_copy(
                                ktile.ap()[:, ic * TCW:(ic + 1) * TCW],
                                bank[:, 0:TCW])

                        p.op("dve", evk, reads=(bkk,), writes=("ktile",),
                             tag=f"kpe_{s}_{tc}_{f}_{ic}")

                    def spk(q, spd=spd, t0=t0):
                        return q.dma_start(
                            out=spd.rearrange("p (c t) -> p c t", c=NC)[:, :, t0:t0 + TCW],
                            in_=ktile.ap().rearrange("p (c w) -> p c w", c=NC))

                    p.op("pd", spk, reads=("ktile",),
                         writes=("ksp_dram" if f == 1 else "rsp_dram",),
                         sem="st_ksp" if f == 1 else "st_rsp",
                         tag=f"spk_{s}_{tc}_{f}")
                else:
                    for tt in range(TCW // 128):
                        for og in range(2):
                            bank = b.pb[4 + og]
                            bkk = f"pb{4 + og}"

                            def mmv(q, tt=tt, og=og, bank=bank):
                                ins = None
                                for c in range(NC):
                                    ins = q.matmul(
                                        bank[:, 0:512],
                                        xbv[:, c, tt * 128:(tt + 1) * 128],
                                        wft.ap()[:, c * D + og * 512:
                                                 c * D + og * 512 + 512],
                                        start=(c == 0), stop=(c == NC - 1))
                                return ins

                            p.op("mm", mmv, reads=("wft", "ar_b"),
                                 writes=(bkk,), tag=f"vg_{s}_{tc}_{f}_{tt}_{og}")
                            if f == 2:
                                def evv(q, tt=tt, og=og, bank=bank, t0=t0):
                                    tg = (t0 // 128 + tt)
                                    return q.tensor_copy(
                                        vtok.ap()[:, tg * D + og * 512:
                                                  tg * D + og * 512 + 512],
                                        bank[:, 0:512])

                                p.op("dve", evv, reads=(bkk,),
                                     writes=("vtok",),
                                     tag=f"vge_{s}_{tc}_{tt}_{og}")
                            else:
                                def evg(q, og=og, bank=bank):
                                    return q.activation(
                                        gq.ap()[:, og * 512:(og + 1) * 512],
                                        bank[:, 0:512], AF.Silu,
                                        bias=0.0, scale=1.0)

                                p.op("act", evg, reads=(bkk,), writes=("gq",),
                                     tag=f"gge_{s}_{tc}_{tt}_{og}")
                        if f == 4:
                            def spg(q, tt=tt, t0=t0):
                                tg = t0 + tt * 128
                                return q.dma_start(
                                    out=gsp_d[tg:tg + 128, :], in_=gq[:])

                            p.op("pd", spg, reads=("gq",), writes=("gsp_dram",),
                                 sem="st_gsp", tag=f"spg_{s}_{tc}_{tt}")

        p.claim("ar_c0", "ar_c")
        p.claim("ar_c1", "ar_c")
        for hp in range(8):
            def ldkp(q, hp=hp):
                return q.dma_start(out=v_kp, in_=ksp_d[:, hp * T:(hp + 1) * T])

            p.op("pd", ldkp, reads=("ksp_dram",), writes=("xT",),
                 sem="ld_kp", tag=f"ldkp_{s}_{hp}")

            def ldrp(q, hp=hp):
                return q.dma_start(out=v_rp, in_=rsp_d[:, hp * T:(hp + 1) * T])

            p.op("pd", ldrp, reads=("rsp_dram",), writes=("xT",),
                 sem="ld_rp", tag=f"ldrp_{s}_{hp}")

            for tcc in range(NTC):
                def mmw(q, hp=hp, tcc=tcc):
                    return q.matmul(
                        b.pb[0][:, 0:TCW],
                        dw2_sb.ap()[:, hp * 128:(hp + 1) * 128],
                        tw_sb.ap()[:, tcc * TCW:(tcc + 1) * TCW],
                        start=True, stop=True)

                p.op("mm", mmw, reads=("dw2_sb", "tw_sb"), writes=("pb0",),
                     tag=f"wd_{s}_{hp}_{tcc}")

                def evw(q, hp=hp, tcc=tcc):
                    return q.activation(
                        S1[:, tcc * TCW:(tcc + 1) * TCW], b.pb[0][:, 0:TCW],
                        AF.Identity, bias=mcols["tdec"][:, hp:hp + 1],
                        scale=1.0)

                p.op("act", evw, reads=("pb0", "tdec_c"), writes=("ar_a",),
                     tag=f"wde_{s}_{hp}_{tcc}")
            p.op("act", lambda q: q.activation(S1, S1, AF.Exp, bias=0.0,
                                               scale=1.0),
                 reads=("ar_a",), writes=("ar_a",), tag=f"wne_{s}_{hp}")
            p.op("dve", lambda q: q.tensor_scalar_mul(S1, S1, -1.0),
                 reads=("ar_a",), writes=("ar_a",), tag=f"wnn_{s}_{hp}")
            p.op("dve", lambda q: q.tensor_tensor_scan(
                S2, S1, S1, 0.0, op0=ALU.add, op1=ALU.bypass),
                reads=("ar_a",), writes=("ar_b",), tag=f"cs_{s}_{hp}")
            p.op("dve", lambda q: q.tensor_sub(S3, S2, S1),
                 reads=("ar_a", "ar_b"), writes=("ar_b",), tag=f"csb_{s}_{hp}")
            p.op("dve", lambda q: q.tensor_scalar(
                S1, S2, S2[:, T // 2:T // 2 + 1], 60.0,
                op0=ALU.subtract, op1=ALU.min),
                reads=("ar_b",), writes=("ar_a",), tag=f"cf1_{s}_{hp}")
            p.op("dve", lambda q: q.tensor_scalar(S1, S1, -60.0, None,
                                                  op0=ALU.max),
                 reads=("ar_a",), writes=("ar_a",), tag=f"cf2_{s}_{hp}")
            p.op("act", lambda q: q.activation(v_ef, S1, AF.Exp, bias=0.0,
                                               scale=1.0),
                 reads=("ar_a",), writes=("xT",), tag=f"ef_{s}_{hp}")
            p.op("act", lambda q: q.activation(v_emf, S1, AF.Exp, bias=0.0,
                                               scale=-1.0),
                 reads=("ar_a",), writes=("xT",), tag=f"emf_{s}_{hp}")
            p.op("dve", lambda q: q.tensor_scalar(
                S2, S3, S3[:, T // 2:T // 2 + 1], 60.0,
                op0=ALU.subtract, op1=ALU.min),
                reads=("ar_b",), writes=("ar_b",), tag=f"cb1_{s}_{hp}")
            p.op("dve", lambda q: q.tensor_scalar(S2, S2, -60.0, None,
                                                  op0=ALU.max),
                 reads=("ar_b",), writes=("ar_b",), tag=f"cb2_{s}_{hp}")
            p.op("act", lambda q: q.activation(v_eb, S2, AF.Exp, bias=0.0,
                                               scale=1.0),
                 reads=("ar_b",), writes=("xT",), tag=f"eb_{s}_{hp}")
            p.op("act", lambda q: q.activation(v_emb, S2, AF.Exp, bias=0.0,
                                               scale=-1.0),
                 reads=("ar_b",), writes=("xT",), tag=f"emb_{s}_{hp}")
            p.op("dve", lambda q: q.tensor_mul(v_rxf, v_rp, v_ef),
                 reads=("xT",), writes=("xT",), tag=f"rxf_{s}_{hp}")
            p.op("dve", lambda q: q.tensor_mul(v_kxf, v_kp, v_emf),
                 reads=("xT",), writes=("xT",), tag=f"kxf_{s}_{hp}")
            p.op("dve", lambda q: q.tensor_mul(v_rxb, v_rp, v_emb),
                 reads=("xT",), writes=("ktile",), tag=f"rxb_{s}_{hp}")
            p.op("dve", lambda q: q.tensor_mul(v_kxb, v_kp, v_eb),
                 reads=("xT",), writes=("ktile",), tag=f"kxb_{s}_{hp}")

            hiters = [(hh, qt) for hh in range(2) for qt in range(NT)]

            def lat_a(i):
                hh, qt = hiters[i]
                po = hh * 64
                for kq in range(NQ):
                    bank = b.pb[kq]
                    bkk = f"pb{kq}"
                    kts = [kq * 4 + j for j in range(4)]
                    dj = [j for j, kt in enumerate(kts) if kt == qt]

                    def mma(q, po=po, qt=qt, kts=kts, bank=bank):
                        ins = None
                        for j, kt in enumerate(kts):
                            if kt <= qt:
                                kex, rex = v_kxf, v_rxf
                            else:
                                kex, rex = v_kxb, v_rxb
                            ins = q.matmul(
                                bank[:, j * 128:(j + 1) * 128],
                                kex[po:po + 64, kt * 128:(kt + 1) * 128],
                                rex[po:po + 64, qt * 128:(qt + 1) * 128],
                                start=True, stop=True)
                            if kt == qt:
                                ins = q.matmul(
                                    b.pb[5][:, 0:128],
                                    v_kxb[po:po + 64, kt * 128:(kt + 1) * 128],
                                    v_rxb[po:po + 64, qt * 128:(qt + 1) * 128],
                                    start=True, stop=True)
                        return ins

                    rd = ("xT", "ktile")
                    wr = (bkk, "pb5") if dj else (bkk,)
                    p.op("mm", mma, reads=rd, writes=wr,
                         tag=f"at_{s}_{hp}_{i}_{kq}")

            def lat_b(i):
                hh, qt = hiters[i]
                par = i % 2
                atk = f"ar_c{par}"
                for kq in range(NQ):
                    bank = b.pb[kq]
                    bkk = f"pb{kq}"
                    dst = ar_c.ap()[:, par * 2048 + kq * 512:
                                    par * 2048 + kq * 512 + 512]

                    if kq < 2:
                        def eva(q, bank=bank, dst=dst):
                            return q.tensor_copy(dst, bank[:, 0:512])

                        p.op("dve", eva, reads=(bkk,), writes=(atk,),
                             tag=f"ate_{s}_{hp}_{i}_{kq}")
                    else:
                        def eva(q, bank=bank, dst=dst):
                            return q.copy(dst, bank[:, 0:512])

                        p.op("act", eva, reads=(bkk,), writes=(atk,),
                             tag=f"ate_{s}_{hp}_{i}_{kq}")
                if qt < NT:
                    kq, j = qt // 4, qt % 4
                    at_sl = ar_c.ap()[:, par * 2048 + kq * 512 + j * 128:
                                      par * 2048 + kq * 512 + j * 128 + 128]

                    def evab(q):
                        return q.tensor_copy(gq.ap()[:, 0:128], b.pb[5][:, 0:128])

                    p.op("dve", evab, reads=("pb5",), writes=("gq",),
                         tag=f"atb_{s}_{hp}_{i}")

                    def msk1(q, at_sl=at_sl):
                        return q.affine_select(
                            gq.ap()[:, 128:256], at_sl,
                            pattern=[[1, 128]], compare_op=ALU.is_ge,
                            fill=0.0, base=0, channel_multiplier=-1)

                    p.op("pool", msk1, reads=(atk,), writes=("gq",),
                         tag=f"mk1_{s}_{hp}_{i}")

                    def msk2(q):
                        return q.affine_select(
                            gq.ap()[:, 256:384], gq.ap()[:, 0:128],
                            pattern=[[-1, 128]], compare_op=ALU.is_ge,
                            fill=0.0, base=-1, channel_multiplier=1)

                    p.op("pool", msk2, reads=("gq",), writes=("gq",),
                         tag=f"mk2_{s}_{hp}_{i}")

                    def mrg(q, at_sl=at_sl):
                        return q.tensor_add(at_sl, gq.ap()[:, 128:256],
                                            gq.ap()[:, 256:384])

                    p.op("dve", mrg, reads=("gq",), writes=(atk,),
                         tag=f"mrg_{s}_{hp}_{i}")

            def lat_c(i):
                hh, qt = hiters[i]
                par = i % 2
                h = 2 * hp + hh
                atk = f"ar_c{par}"

                def pv(q, h=h, par=par):
                    ins = None
                    for kt in range(NT):
                        ins = q.matmul(
                            b.pb[6][:, 0:64],
                            ar_c.ap()[:, par * 2048 + kt * 128:
                                      par * 2048 + kt * 128 + 128],
                            vtok.ap()[:, kt * D + h * 64: kt * D + h * 64 + 64],
                            start=(kt == 0), stop=(kt == NT - 1))
                    return ins

                p.op("mm", pv, reads=(atk, "vtok"), writes=("pb6",),
                     tag=f"pv_{s}_{hp}_{i}")

                def evy(q, h=h, qt=qt):
                    return q.copy(
                        ytok.ap()[:, qt * D + h * 64: qt * D + h * 64 + 64],
                        b.pb[6][:, 0:64])

                p.op("act", evy, reads=("pb6",), writes=("ytok",),
                     tag=f"evy_{s}_{hp}_{i}")

            NHI = len(hiters)
            for idx in range(NHI + 2):
                if 1 <= idx < NHI + 1:
                    lat_b(idx - 1)
                if idx >= 2:
                    lat_c(idx - 2)
                if idx < NHI:
                    lat_a(idx)

        p.claim("ar_c", "ar_c0", "ar_c1")
        def ldwo(q):
            return q.dma_start(
                out=wft.ap().rearrange("p (c d) -> p c d", c=NC),
                in_=wts_d[WI["wo"]].rearrange("c p d -> p c d"))

        p.op("ld", ldwo, reads=("wts_dram",), writes=("wft",),
             tag=f"ldwo_{s}")
        yc = ar_b.ap()[:, 0:1024]
        ysq = ar_b.ap()[:, 1024:2048]
        ygb = ar_c.ap()[:, 0:1024]
        ygT = ar_c.ap()[:, 1024:2048]
        for qt in range(NT):
            yv = ytok.ap()[:, qt * D:(qt + 1) * D].rearrange(
                "p (g k) -> p g k", g=16)
            ycv = yc.rearrange("p (g k) -> p g k", g=16)
            ysv = ysq.rearrange("p (g k) -> p g k", g=16)

            def gsum(q, yv=yv):
                return q.tensor_reduce(gns[:, 0:16], yv, axis=AX.X,
                                       op=ALU.add)

            p.op("dve", gsum, reads=("ytok",), writes=("gns",),
                 tag=f"gs_{s}_{qt}")
            p.op("dve", lambda q: q.tensor_scalar_mul(
                gns[:, 16:32], gns[:, 0:16], 1.0 / 64),
                reads=("gns",), writes=("gns",), tag=f"gm_{s}_{qt}")

            def gcen(q, yv=yv, ycv=ycv):
                mb = gns.ap()[:, 16:32].unsqueeze(2).broadcast_to([128, 16, 64])
                return q.tensor_tensor(ycv, yv, mb, ALU.subtract)

            p.op("dve", gcen, reads=("ytok", "gns"), writes=("ar_b",),
                 tag=f"gc_{s}_{qt}")
            p.op("act", lambda q: q.activation(ysq, yc, AF.Square, bias=0.0,
                                               scale=1.0),
                 reads=("ar_b",), writes=("ar_b",), tag=f"gq2_{s}_{qt}")

            def gss(q, ysv=ysv):
                return q.tensor_reduce(gns[:, 32:48], ysv, axis=AX.X,
                                       op=ALU.add)

            p.op("dve", gss, reads=("ar_b",), writes=("gns",),
                 tag=f"gv_{s}_{qt}")
            p.op("dve", lambda q: q.tensor_scalar(
                gns[:, 48:64], gns[:, 32:48], 1.0 / 64, 64e-5,
                op0=ALU.mult, op1=ALU.add),
                reads=("gns",), writes=("gns",), tag=f"gvv_{s}_{qt}")
            p.op("act", lambda q: q.activation(gns[:, 64:80], gns[:, 48:64],
                                               AF.Sqrt, bias=0.0, scale=1.0),
                 reads=("gns",), writes=("gns",), tag=f"gsd_{s}_{qt}")
            p.op("dve", lambda q: q.reciprocal(gns[:, 48:64], gns[:, 64:80]),
                 reads=("gns",), writes=("gns",), tag=f"gr_{s}_{qt}")

            def gnm(q, ycv=ycv):
                rb = gns.ap()[:, 48:64].unsqueeze(2).broadcast_to([128, 16, 64])
                return q.tensor_tensor(ycv, ycv, rb, ALU.mult)

            p.op("dve", gnm, reads=("ar_b", "gns"), writes=("ar_b",),
                 tag=f"gn_{s}_{qt}")
            p.op("dve", lambda q: q.tensor_mul(yc, yc, lnw_bc[:]),
                 reads=("ar_b", "lnw_bc"), writes=("ar_b",), tag=f"ga_{s}_{qt}")
            p.op("dve", lambda q: q.tensor_add(yc, yc, lnb_bc[:]),
                 reads=("ar_b", "lnb_bc"), writes=("ar_b",), tag=f"gb_{s}_{qt}")

            def ldg(q, qt=qt):
                return q.dma_start(out=gq[:],
                                   in_=gsp_d[qt * 128:(qt + 1) * 128, :])

            p.op("pd", ldg, reads=("gsp_dram",), writes=("gq",),
                 sem="ld_gq", tag=f"ldg_{s}_{qt}")
            p.op("dve", lambda q: q.tensor_tensor(ygb, yc, gq[:], ALU.mult),
                 reads=("ar_b", "gq"), writes=("ar_c",), tag=f"gg_{s}_{qt}")
            for half in range(2):
                bank = b.pb[4 + half]
                bkk = f"pb{4 + half}"
                bview = bank.bitcast(BF16)

                def trg(q, half=half, bview=bview):
                    ins = None
                    for j in range(4):
                        c = half * 4 + j
                        ins = q.transpose(bview[:, j * 128:(j + 1) * 128],
                                          ygb[:, c * 128:(c + 1) * 128],
                                          b.identb[:])
                    return ins

                p.op("mm", trg, reads=("ar_c", "identb"), writes=(bkk,),
                     tag=f"tg_{s}_{qt}_{half}")

                def evg2(q, half=half, bview=bview):
                    return q.tensor_copy(
                        ygT[:, half * 512:(half + 1) * 512], bview[:, 0:512])

                p.op("dve", evg2, reads=(bkk,), writes=("ar_c",),
                     tag=f"tge_{s}_{qt}_{half}")
            def ldr2(q, qt=qt, xs=xs):
                return q.dma_start(out=xres[qt % 2][:],
                                   in_=xs[qt * 128:(qt + 1) * 128, :])

            p.op("pd", ldr2, writes=("xres0",), tag=f"rr_{s}_{qt}")
            for og in range(2):
                bank = b.pb[2 + og]
                bkk = f"pb{2 + og}"

                def mmo(q, og=og, bank=bank):
                    ins = None
                    for c in range(NC):
                        ins = q.matmul(
                            bank[:, 0:512],
                            ygT[:, c * 128:(c + 1) * 128],
                            wft.ap()[:, c * D + og * 512: c * D + og * 512 + 512],
                            start=(c == 0), stop=(c == NC - 1))
                    return ins

                p.op("mm", mmo, reads=("ar_c", "wft"), writes=(bkk,),
                     tag=f"wo_{s}_{qt}_{og}")

                def evo(q, og=og, qt=qt, bank=bank):
                    return q.scalar_tensor_tensor(
                        s_t[:, og * 512:(og + 1) * 512], bank[:, 0:512],
                        0.0, xres[qt % 2].ap()[:, og * 512:(og + 1) * 512],
                        op0=ALU.bypass, op1=ALU.add)

                p.op("dve", evo, reads=(bkk, "xres0"), writes=("ar_a",),
                     tag=f"woe_{s}_{qt}_{og}")
            ln_tail(b, ln, s_t, "ar_a", ys[qt * 128:(qt + 1) * 128, :],
                    f"l1a_{s}_{qt}")




def build_l1_attn(cfg):
    T, B = cfg["T"], cfg["B"]
    H, HD = 16, 64
    NT = T // 128
    NTC = T // 512 if T >= 512 else 1
    TCW = min(512, T)
    NQ = NT // 4

    nc = bass.Bass("TRN2", target_bir_lowering=False)
    x_d = nc.dram_tensor("x", [B, T, D], F32, kind="ExternalInput").ap()
    dn = {}
    for nm in ("tmx", "tmw", "tmk", "tmv", "tmr", "tmg", "tdec", "lnw", "lnb"):
        dn[nm] = nc.dram_tensor(nm, [D], F32, kind="ExternalInput").ap()
    tw1_d = nc.dram_tensor("tw1", [D, 160], F32, kind="ExternalInput").ap()
    tw2_d = nc.dram_tensor("tw2", [5, 32, D], F32, kind="ExternalInput").ap()
    dw1_d = nc.dram_tensor("dw1", [D, 64], F32, kind="ExternalInput").ap()
    dw2_d = nc.dram_tensor("dw2", [64, D], F32, kind="ExternalInput").ap()
    wX_d = {nm: nc.dram_tensor(nm, [D, D], F32, kind="ExternalInput").ap()
            for nm in ("wr", "wk", "wv", "wg", "wo")}
    y_d = nc.dram_tensor("y", [B, T, D], F32, kind="ExternalOutput").ap()
    wts_d = nc.dram_tensor("wts", [5, NC, 128, D], BF16, kind="Internal").ap()
    ksp_d = nc.dram_tensor("ksp", [128, NC * T], BF16, kind="Internal").ap()
    rsp_d = nc.dram_tensor("rsp", [128, NC * T], BF16, kind="Internal").ap()
    gsp_d = nc.dram_tensor("gsp", [T, D], BF16, kind="Internal").ap()

    WI = {"wr": 0, "wk": 1, "wv": 2, "wg": 3, "wo": 4}
    DR = dict(x_d=x_d, dn=dn, tw1_d=tw1_d, tw2_d=tw2_d, dw1_d=dw1_d, dw2_d=dw2_d, wX_d=wX_d, y_d=y_d, wts_d=wts_d, ksp_d=ksp_d, rsp_d=rsp_d, gsp_d=gsp_d)
    b = Blk(nc, cfg)
    with contextlib.ExitStack() as es:
        b.es = es
        common_prep(b)
        emit_l1_attn(b, cfg, DR)
        b.p.emit()
    return nc



def emit_l1_ffn(b, cfg, DR):
    nc = b.nc
    p = b.p
    T, B = cfg["T"], cfg["B"]
    TOK = T * B
    TCW = min(512, TOK)
    NTC = TOK // TCW
    NTT = TCW // 128

    x_d = DR["x_d"]
    w_d = DR["w_d"]
    sc_d = DR["sc_d"]
    bi_d = DR["bi_d"]
    y_d = DR["y_d"]

    swt = b.sbuf("swt", [128, NC * D], BF16)
    wsl = b.sbuf("wsl", [128, D], F32)
    wsg = b.sbuf("wsg", [128, D], F32)
    for rc in range(NC):
        def ldw(q, rc=rc):
            return q.dma_start(out=wsl[:], in_=w_d[rc * 128:(rc + 1) * 128, :])

        p.op("ld", ldw, writes=("wsl",), tag=f"sw_ld{rc}")
        p.op("dve", lambda q: q.tensor_scalar(wsg[:], wsl[:], 0.0, None,
                                              op0=ALU.is_ge),
             reads=("wsl",), writes=("wsg",), tag=f"sw_ge{rc}")
        p.op("dve", lambda q: q.tensor_scalar(wsg[:], wsg[:], 2.0, -1.0,
                                              op0=ALU.mult, op1=ALU.add),
             reads=("wsg",), writes=("wsg",), tag=f"sw_sc{rc}")
        for half in range(2):
            bank = b.pb[6 + half]
            bkk = f"pb{6 + half}"

            def trs(q, half=half, bank=bank):
                ins = None
                for j in range(4):
                    c = half * 4 + j
                    ins = q.transpose(bank[:, j * 128:(j + 1) * 128],
                                      wsg[:, c * 128:(c + 1) * 128], b.ident[:])
                return ins

            p.op("mm", trs, reads=("wsg", "ident"), writes=(bkk,),
                 tag=f"sw_tr{rc}_{half}")

            def evw(q, rc=rc, half=half, bank=bank):
                ins = None
                for j in range(4):
                    c = half * 4 + j
                    ins = q.tensor_copy(
                        swt.ap()[:, c * D + rc * 128: c * D + rc * 128 + 128],
                        bank[:, j * 128:(j + 1) * 128])
                return ins

            p.op("dve", evw, reads=(bkk,), writes=("swt",),
                 tag=f"sw_ev{rc}_{half}")

    scr = b.sbuf("scr", [1, D], F32R)
    bir = b.sbuf("bir", [1, D], F32R)
    p.op("ld", lambda q: q.dma_start(out=scr[:], in_=sc_d[None, :].bitcast(F32R)),
         writes=("scr",), tag="ld_sc")
    p.op("ld", lambda q: q.dma_start(out=bir[:], in_=bi_d[None, :].bitcast(F32R)),
         writes=("bir",), tag="ld_bi")
    sc_bc = b.sbuf("sc_bc", [128, D], F32)
    bi_bc = b.sbuf("bi_bc", [128, D], F32)
    for half in range(2):
        for (row, rk, dst, dk) in ((scr, "scr", sc_bc, "sc_bc"),
                                   (bir, "bir", bi_bc, "bi_bc")):
            def mmb(q, row=row, half=half):
                return q.matmul(b.pb[5][:, 0:512], b.ones_r[:],
                                row[:, half * 512:(half + 1) * 512],
                                start=True, stop=True)

            p.op("mm", mmb, reads=(rk, "ones_r"), writes=("pb5",),
                 tag=f"bc_mm_{dk}_{half}")

            def evb(q, dst=dst, half=half):
                return q.tensor_copy(dst[:, half * 512:(half + 1) * 512],
                                     b.pb[5][:, 0:512])

            p.op("dve", evb, reads=("pb5",), writes=(dk,),
                 tag=f"bc_ev_{dk}_{half}")

    xres = [b.sbuf("xres", [128, 1024], F32) for _ in range(NTT)]
    xT = b.sbuf("xT", [128, NC * TCW], BF16)
    s_t = b.sbuf("s_t", [128, 1024], F32)
    ln = LNScratch(b)

    for tc in range(NTC):
        for j in range(NTT):
            t0g = tc * TCW + j * 128
            tin(b, x_d[t0g:t0g + 128, :], xres[j], f"xres{j}",
                [(xT, "xT", TCW, j * 128)], t0g, banks=(6, 7))
        for j in range(NTT):
            for og in range(2):
                bank = b.pb[og]
                bkk = f"pb{og}"

                def mmp(q, j=j, og=og, bank=bank):
                    ins = None
                    for c in range(NC):
                        ins = q.matmul(
                            bank[:, 0:512],
                            xT.ap()[:, c * TCW + j * 128: c * TCW + j * 128 + 128],
                            swt.ap()[:, c * D + og * 512: c * D + og * 512 + 512],
                            start=(c == 0), stop=(c == NC - 1))
                    return ins

                p.op("mm", mmp, reads=("xT", "swt"), writes=(bkk,),
                     tag=f"pr_mm_{tc}_{j}_{og}")

                def ev1(q, og=og, bank=bank):
                    return q.tensor_mul(s_t[:, og * 512:(og + 1) * 512],
                                        bank[:, 0:512],
                                        sc_bc[:, og * 512:(og + 1) * 512])

                p.op("dve", ev1, reads=(bkk, "sc_bc"), writes=("s_t",),
                     tag=f"pr_sc_{tc}_{j}_{og}")

                def ev2(q, og=og):
                    return q.tensor_add(s_t[:, og * 512:(og + 1) * 512],
                                        s_t[:, og * 512:(og + 1) * 512],
                                        bi_bc[:, og * 512:(og + 1) * 512])

                p.op("dve", ev2, reads=("s_t", "bi_bc"), writes=("s_t",),
                     tag=f"pr_bi_{tc}_{j}_{og}")

                def ev3(q, og=og):
                    return q.tensor_scalar(s_t[:, og * 512:(og + 1) * 512],
                                           s_t[:, og * 512:(og + 1) * 512],
                                           100.0, -100.0,
                                           op0=ALU.min, op1=ALU.max)

                p.op("dve", ev3, reads=("s_t",), writes=("s_t",),
                     tag=f"pr_cl_{tc}_{j}_{og}")

                def ev4(q, j=j, og=og):
                    return q.tensor_add(s_t[:, og * 512:(og + 1) * 512],
                                        s_t[:, og * 512:(og + 1) * 512],
                                        xres[j].ap()[:, og * 512:(og + 1) * 512])

                p.op("dve", ev4, reads=("s_t", f"xres{j}"), writes=("s_t",),
                     tag=f"pr_rs_{tc}_{j}_{og}")
            t0g = tc * TCW + j * 128
            ln_tail(b, ln, s_t, "s_t", y_d[t0g:t0g + 128, :], f"l1f_{tc}_{j}")




def build_l1_ffn(cfg):
    T, B = cfg["T"], cfg["B"]
    TOK = T * B
    TCW = min(512, TOK)
    NTC = TOK // TCW
    NTT = TCW // 128

    nc = bass.Bass("TRN2", target_bir_lowering=False)
    x_d = nc.dram_tensor("x", [TOK, D], F32, kind="ExternalInput").ap()
    w_d = nc.dram_tensor("w", [D, D], F32, kind="ExternalInput").ap()
    sc_d = nc.dram_tensor("sc", [D], F32, kind="ExternalInput").ap()
    bi_d = nc.dram_tensor("bi", [D], F32, kind="ExternalInput").ap()
    y_d = nc.dram_tensor("y", [TOK, D], F32, kind="ExternalOutput").ap()

    DR = dict(x_d=x_d, w_d=w_d, sc_d=sc_d, bi_d=bi_d, y_d=y_d)
    b = Blk(nc, cfg)
    with contextlib.ExitStack() as es:
        b.es = es
        common_prep(b)
        emit_l1_ffn(b, cfg, DR)
        b.p.emit()
    return nc


CFG_FULL = dict(T=2048, B=4)


def build_fused(cfg):
    T, B = cfg["T"], cfg["B"]
    TOK = T * B
    HF, E = 4096, 8
    nc = bass.Bass("TRN2", target_bir_lowering=False)
    dt = nc.dram_tensor
    x0 = dt("x", [B, T, D], F32, kind="ExternalInput").ap()
    y3 = dt("y", [B, T, D], F32, kind="ExternalOutput").ap()
    s1 = dt("s1", [B, T, D], F32, kind="Internal").ap()
    s2 = dt("s2", [B, T, D], F32, kind="Internal").ap()
    s3 = dt("s3", [B, T, D], F32, kind="Internal").ap()
    flat = lambda s: s.rearrange("b t d -> (b t) d")

    DR1 = dict(
        x_d=x0, y_d=s1,
        wq_d=dt("a0_wq", [D, D], F32, kind="ExternalInput").ap(),
        wk_d=dt("a0_wk", [D, D], F32, kind="ExternalInput").ap(),
        wv_d=dt("a0_wv", [D, D], F32, kind="ExternalInput").ap(),
        wo_d=dt("a0_wo", [D, D], F32, kind="ExternalInput").ap(),
        bq_d=dt("a0_bq", [D], F32, kind="ExternalInput").ap(),
        bk_d=dt("a0_bk", [D], F32, kind="ExternalInput").ap(),
        bv_d=dt("a0_bv", [D], F32, kind="ExternalInput").ap(),
        bo_d=dt("a0_bo", [D], F32, kind="ExternalInput").ap(),
        al_d=dt("a0_alpha", [1], F32, kind="ExternalInput").ap(),
        wts_d=dt("a0_wts", [4, NC, 128, D], BF16, kind="Internal").ap(),
    )
    DR2 = dict(
        x_d=flat(s1), y_d=flat(s2),
        w1_d=dt("f0_w1", [E, D, HF], F32, kind="ExternalInput").ap(),
        b1_d=dt("f0_b1", [E, HF], F32, kind="ExternalInput").ap(),
        w2_d=dt("f0_w2", [E, HF, D], F32, kind="ExternalInput").ap(),
        b2_d=dt("f0_b2", [E, D], F32, kind="ExternalInput").ap(),
        wr_d=dt("f0_wr", [D, E], F32, kind="ExternalInput").ap(),
        br_d=dt("f0_br", [E], F32, kind="ExternalInput").ap(),
        w1b_d=dt("f0_w1b", [E, D, HF], BF16, kind="Internal").ap(),
        w2b_d=dt("f0_w2b", [E, HF, D], BF16, kind="Internal").ap(),
    )
    DR3 = dict(
        x_d=s2, y_d=s3,
        dn={nm: dt("a1_" + nm, [D], F32, kind="ExternalInput").ap()
            for nm in ("tmx", "tmw", "tmk", "tmv", "tmr", "tmg", "tdec",
                       "lnw", "lnb")},
        tw1_d=dt("a1_tw1", [D, 160], F32, kind="ExternalInput").ap(),
        tw2_d=dt("a1_tw2", [5, 32, D], F32, kind="ExternalInput").ap(),
        dw1_d=dt("a1_dw1", [D, 64], F32, kind="ExternalInput").ap(),
        dw2_d=dt("a1_dw2", [64, D], F32, kind="ExternalInput").ap(),
        wX_d={nm: dt("a1_" + nm, [D, D], F32, kind="ExternalInput").ap()
              for nm in ("wr", "wk", "wv", "wg", "wo")},
        wts_d=dt("a1_wts", [5, NC, 128, D], BF16, kind="Internal").ap(),
        ksp_d=dt("a1_ksp", [128, NC * T], BF16, kind="Internal").ap(),
        rsp_d=dt("a1_rsp", [128, NC * T], BF16, kind="Internal").ap(),
        gsp_d=dt("a1_gsp", [T, D], BF16, kind="Internal").ap(),
    )
    DR4 = dict(
        x_d=flat(s3), y_d=flat(y3),
        w_d=dt("f1_w", [D, D], F32, kind="ExternalInput").ap(),
        sc_d=dt("f1_sc", [D], F32, kind="ExternalInput").ap(),
        bi_d=dt("f1_bi", [D], F32, kind="ExternalInput").ap(),
    )

    b = Blk(nc, cfg)
    p = b.p
    with contextlib.ExitStack() as es0:
        b.es = es0
        common_prep(b)
        for emit_fn, DRX in ((emit_l0_attn, DR1), (emit_l0_ffn, DR2),
                             (emit_l1_attn, DR3), (emit_l1_ffn, DR4)):
            with contextlib.ExitStack() as es:
                b.es = es
                emit_fn(b, cfg, DRX)
            p.barrier()
        b.es = es0
        p.emit()
    return nc


def _f32(a):
    return np.ascontiguousarray(np.asarray(a, dtype=np.float32))


def kernel(**inputs):
    cfg = CFG_FULL
    x = _f32(inputs["L0_attn_x"])
    shards = [np.ascontiguousarray(s) for s in np.split(x, N_CORES, axis=0)]
    nc = build_fused(cfg)
    m = dict(
        a0_wq=_f32(inputs["L0_attn_Wq"]), a0_wk=_f32(inputs["L0_attn_Wk"]),
        a0_wv=_f32(inputs["L0_attn_Wv"]), a0_wo=_f32(inputs["L0_attn_Wo"]),
        a0_bq=_f32(inputs["L0_attn_bq"]), a0_bk=_f32(inputs["L0_attn_bk"]),
        a0_bv=_f32(inputs["L0_attn_bv"]), a0_bo=_f32(inputs["L0_attn_bo"]),
        a0_alpha=_f32(inputs["L0_attn_alpha"]).reshape(1),
        f0_w1=_f32(inputs["L0_ffn_W1"]), f0_b1=_f32(inputs["L0_ffn_b1"]),
        f0_w2=_f32(inputs["L0_ffn_W2"]), f0_b2=_f32(inputs["L0_ffn_b2"]),
        f0_wr=_f32(inputs["L0_ffn_Wr"]), f0_br=_f32(inputs["L0_ffn_br"]),
        a1_tmx=_f32(inputs["L1_attn_time_maa_x"]).reshape(D),
        a1_tmw=_f32(inputs["L1_attn_time_maa_w"]).reshape(D),
        a1_tmk=_f32(inputs["L1_attn_time_maa_k"]).reshape(D),
        a1_tmv=_f32(inputs["L1_attn_time_maa_v"]).reshape(D),
        a1_tmr=_f32(inputs["L1_attn_time_maa_r"]).reshape(D),
        a1_tmg=_f32(inputs["L1_attn_time_maa_g"]).reshape(D),
        a1_tw1=_f32(inputs["L1_attn_time_maa_w1"]),
        a1_tw2=_f32(inputs["L1_attn_time_maa_w2"]),
        a1_tdec=_f32(inputs["L1_attn_time_decay"]).reshape(D),
        a1_dw1=_f32(inputs["L1_attn_time_decay_w1"]),
        a1_dw2=_f32(inputs["L1_attn_time_decay_w2"]),
        a1_wr=_f32(inputs["L1_attn_W_r"]), a1_wk=_f32(inputs["L1_attn_W_k"]),
        a1_wv=_f32(inputs["L1_attn_W_v"]), a1_wg=_f32(inputs["L1_attn_W_g"]),
        a1_wo=_f32(inputs["L1_attn_W_o"]),
        a1_lnw=_f32(inputs["L1_attn_ln_w"]), a1_lnb=_f32(inputs["L1_attn_ln_b"]),
        f1_w=_f32(inputs["L1_ffn_weights"]),
        f1_sc=_f32(inputs["L1_ffn_scale"]),
        f1_bi=_f32(inputs["L1_ffn_bias"]),
    )
    res = run_bass_kernel_spmd(nc, [dict(m, x=s) for s in shards],
                               core_ids=list(range(N_CORES)))
    return np.concatenate([r["y"] for r in res.results], axis=0)
